# version 07
# baseline (speedup 1.0000x reference)
import numpy as np
import ml_dtypes
from contextlib import ExitStack

import concourse.bass as bass
import concourse.mybir as mybir
from concourse.bass_utils import run_bass_kernel_spmd

F32 = mybir.dt.float32
BF16 = mybir.dt.bfloat16
I32 = mybir.dt.int32
AF = mybir.ActivationFunctionType
ALU = mybir.AluOpType
AX = mybir.AxisListType

ENGS = ("sync", "scalar", "vector", "tensor", "gpsimd")

D = 1024
KC = 8
TOK = 4096
TT = 512
NTT = TOK // TT
NCORES = 8
BIG = 1.0e30
_dbg = {}



class _OpRec:
    __slots__ = ("incs",)

    def __init__(self):
        self.incs = []

    def then_inc(self, sem_handle, n):
        self.incs.append((sem_handle, n))
        return self


_AUTOSYNC_OPS = ("tensor_", "reduce_", "reciprocal", "memset", "memzero",
                 "activation", "select", "copy_predicated", "bn_", "iota",
                 "affine_select", "scalar_tensor_tensor", "square", "sqrt",
                 "sign", "transpose")
_ENG_SEM = {"vector": "V", "scalar": "S", "gpsimd": "P"}


def _op_is_small(a):
    if not a:
        return True
    out = a[0]
    try:
        shape = out.shape
    except Exception:
        return True
    if len(shape) < 2:
        return True
    parts = shape[0]
    free = 1
    for s in shape[1:]:
        free *= s
    return parts < 128 or free < 512


class _EngProxy:
    def __init__(self, prog, eng):
        self._p = prog
        self._e = eng

    def __getattr__(self, name):
        p, e = self._p, self._e

        def call(*a, **k):
            rec = _OpRec()
            p.q[e].append((name, a, k, rec))
            if e in _ENG_SEM and name.startswith(_AUTOSYNC_OPS)                     and _op_is_small(a):
                sname = _ENG_SEM[e]
                p.cnt[sname] += 1
                rec.incs.append((p.sem[sname], 1))
                wrec = _OpRec()
                p.q[e].append(("wait_ge", (p.sem[sname], p.cnt[sname]), {},
                               wrec))
            return rec

        return call


class Prog:

    def __init__(self, nc, ctx):
        self.nc = nc
        self.ctx = ctx
        self.q = {e: [] for e in ENGS}
        self.sem = {}
        self.cnt = {}
        for name in ("L", "O", "G", "Q", "V", "T", "S", "P", "B"):
            self.sem[name] = ctx.enter_context(nc.semaphore("sem" + name))
            self.cnt[name] = 0
        self.sy = _EngProxy(self, "sync")
        self.sc = _EngProxy(self, "scalar")
        self.v = _EngProxy(self, "vector")
        self.t = _EngProxy(self, "tensor")
        self.g = _EngProxy(self, "gpsimd")
        self.proxy = {"sync": self.sy, "scalar": self.sc, "vector": self.v,
                      "tensor": self.t, "gpsimd": self.g}
        self.name_prefix = ""
        self.dma_sems = {"sync": ["L", "O"], "gpsimd": ["G"], "scalar": ["Q"]}

    def inc(self, rec, name, n=None):
        if n is None:
            n = 16 if name in ("L", "O", "G", "Q") else 1
        self.cnt[name] += n
        rec.then_inc(self.sem[name], n)
        return self.cnt[name]

    def val(self, name):
        return self.cnt[name]

    def wait(self, proxy, name, value=None):
        if value is None:
            value = self.cnt[name]
        if value > 0:
            proxy.wait_ge(self.sem[name], value)

    def barrier(self):
        for e in ENGS:
            px = self.proxy[e]
            for s in self.dma_sems.get(e, []):
                self.wait(px, s)
        self.cnt["B"] += len(ENGS)
        target = self.cnt["B"]
        for e in ENGS:
            self.proxy[e].sem_inc(self.sem["B"], 1)
        for e in ENGS:
            self.proxy[e].wait_ge(self.sem["B"], target)

    def scope(self):
        from contextlib import contextmanager

        @contextmanager
        def _scope():
            old = self.ctx
            with ExitStack() as es:
                self.ctx = es
                try:
                    yield
                finally:
                    self.ctx = old
        return _scope()

    def sb(self, name, shape, dtype=F32):
        return self.ctx.enter_context(
            self.nc.sbuf_tensor(self.name_prefix + name, shape, dtype))

    def ps(self, name, shape, dtype=F32):
        return self.ctx.enter_context(
            self.nc.psum_tensor(self.name_prefix + name, shape, dtype))

    def play(self):
        with self.nc.Block() as block:
            for ename in ENGS:
                def emit(eng, _q=self.q[ename]):
                    for (name, a, k, rec) in _q:
                        inst = getattr(eng, name)(*a, **k)
                        merged = {}
                        for (h, n) in rec.incs:
                            merged[h] = merged.get(h, 0) + n
                        for (h, n) in merged.items():
                            inst.then_inc(h, n)
                getattr(block, ename)(emit)


class PsumPool:

    def __init__(self, p, name, n, shape, dtype=F32):
        self.p = p
        self.tiles = [p.ps(f"{name}{i}", shape, dtype) for i in range(n)]
        self.n = n
        self.i = 0
        self.guard = [None] * n

    def acquire(self):
        slot = self.i % self.n
        self.i += 1
        if self.guard[slot] is not None:
            nm, v = self.guard[slot]
            self.p.wait(self.p.t, nm, v)
        return self.tiles[slot], slot

    def release(self, slot, sem_name, value):
        self.guard[slot] = (sem_name, value)


CONST_VALUES = [1e-5, 1e-6, 1e-12, 1e-9, 1.0 / 127.0, 0.5, -0.5, 2.0,
                1.0 / D, 1.0 / 16.0, 0.125, 1.0, -1.0, 0.1, 0.3, 0.7, 0.2,
                1.0 / 2048.0, -0.0625]


class TransPool:

    def __init__(self, p, name, dtype=BF16):
        self.p = p
        self.tiles = [p.ps(f"{name}{i}", [128, 128], dtype) for i in range(2)]
        self.i = 0
        self.guard = [0, 0]

    def step(self, in_ap, ident, out_ap, out_rows=128, out_cols=128):
        p = self.p
        sl = self.i % 2
        self.i += 1
        if self.guard[sl]:
            p.wait(p.t, "V", self.guard[sl])
        ps = self.tiles[sl]
        tv = p.inc(p.t.transpose(ps[0:out_rows, 0:out_cols], in_ap, ident),
                   "T")
        p.wait(p.v, "T", tv)
        vv = p.inc(p.v.tensor_copy(out_ap, ps[0:out_rows, 0:out_cols]), "V")
        self.guard[sl] = vv
        return vv


class Const:

    def __init__(self, p):
        nc = p.nc
        nvals = len(CONST_VALUES)
        self.cpool = p.sb("c_pool", [128, nvals], F32)
        self.ones_row_f = p.sb("c_ones_row_f", [1, 128], F32)
        self.ones_col_f = p.sb("c_ones_col_f", [128, 1], F32)
        self.ones_row_b = p.sb("c_ones_row_b", [1, 128], BF16)
        self.ones_col_b = p.sb("c_ones_col_b", [128, 1], BF16)
        self.ident_b = p.sb("c_ident_b", [128, 128], BF16)
        self.ident_f = p.sb("c_ident_f", [128, 128], F32)
        self.iota_pf = p.sb("c_iota_pf", [128, TT], F32)
        srcs = [
            (self.ones_row_f, np.ones((1, 128), np.float32)),
            (self.ones_col_f, np.ones((128, 1), np.float32)),
            (self.ones_row_b, np.ones((1, 128), ml_dtypes.bfloat16)),
            (self.ones_col_b, np.ones((128, 1), ml_dtypes.bfloat16)),
            (self.ident_b, np.eye(128, dtype=ml_dtypes.bfloat16)),
            (self.ident_f, np.eye(128, dtype=np.float32)),
            (self.iota_pf, (np.arange(128)[:, None] -
                            np.arange(TT)[None, :]).astype(np.float32)),
            (self.cpool, np.tile(np.asarray(CONST_VALUES, np.float32)[None, :],
                                 (128, 1))),
        ]
        for i, (dst, arr) in enumerate(srcs):
            h = p.nc.inline_tensor(arr, f"c_inl{i}")
            p.inc(p.sy.dma_start(dst[:], h.ap()), "L")

    def col(self, v):
        i = CONST_VALUES.index(v)
        return self.cpool[:, i:i + 1]

    def one(self, v):
        i = CONST_VALUES.index(v)
        return self.cpool[0:1, i:i + 1]



def chunks_view(t, nch=KC, tw=TOK):
    def view(c, t0=0, w=None):
        if w is None:
            w = tw
        return t[:, c * tw + t0: c * tw + t0 + w]
    return view


def feat_dram_view(ap2d, c, t0, w, p0=0, pn=128):
    return ap2d[c * 128 + p0: c * 128 + p0 + pn, t0: t0 + w]


def emit_weight_lhsT_load(p, hbm_f32, w_sb, din, dout, cast_engine="gpsimd"):
    nk = din // 128
    for kc in range(nk):
        dst = w_sb[:, kc * dout:(kc + 1) * dout]
        src = hbm_f32[kc * 128:(kc + 1) * 128, :]
        p.inc(p.g.dma_start(dst, src), "G")
    return p.val("G")


def w_view(w_sb, dout):
    def view(kc, fc, fw=128):
        return w_sb[:, kc * dout + fc * 128: kc * dout + fc * 128 + fw]
    return view




class LNState:

    def __init__(self, p, name, pool):
        self.p = p
        self.xs = p.sb(name + "_xs", [128, KC * TT], F32)
        self.sq = p.sb(name + "_sq", [128, 2 * TT], F32)
        self.rows = p.sb(name + "_rows", [1, 4 * TT], F32)
        self.st1 = p.ps(name + "_st1", [1, TT], F32)
        self.st2 = p.ps(name + "_st2", [1, TT], F32)
        self.bc = pool
        self.last_v = 0

    def row(self, i, t0=0, w=TT):
        return self.rows[:, i * TT + t0: i * TT + t0 + w]


def emit_ln_tile(p, c, ln, xin_ap, tt, ln_out_view, eps=1e-5, keep_x=False):
    t0 = tt * TT
    xsv = chunks_view(ln.xs, KC, TT)
    if ln.last_v:
        p.wait(p.sy, "V", ln.last_v)
    src = xin_ap.rearrange("(c p) t -> p c t", p=128)[:, :, t0:t0 + TT]
    dst = ln.xs[:].rearrange("p (c t) -> p c t", c=KC)
    lval = p.inc(p.sy.dma_start(dst, src), "L")
    p.wait(p.t, "L", lval)
    if ln.last_v:
        p.wait(p.t, "V", ln.last_v)
    rec = None
    for kc in range(KC):
        rec = p.t.matmul(ln.st1[:], c.ones_col_f[:], xsv(kc), start=(kc == 0),
                         stop=(kc == KC - 1))
    p.inc(rec, "T")
    p.wait(p.sc, "L", lval)
    tq = {}
    for kc in range(KC):
        sqv = ln.sq[:, (kc % 2) * TT:(kc % 2) * TT + TT]
        if kc >= 2:
            p.wait(p.sc, "T", tq[kc - 2])
        sk = p.inc(p.sc.activation(sqv, xsv(kc), AF.Square), "S")
        p.wait(p.t, "S", sk)
        tq[kc] = p.inc(p.t.matmul(ln.st2[:], c.ones_col_f[:], sqv,
                                  start=(kc == 0), stop=(kc == KC - 1)), "T")
    tsum = p.val("T")
    p.wait(p.v, "T", tsum)
    p.v.tensor_scalar_mul(ln.row(0), ln.st1[:], 1.0 / D)
    p.v.tensor_tensor(ln.row(3), ln.row(0), ln.row(0), ALU.mult)
    vv = p.inc(p.v.scalar_tensor_tensor(ln.row(1), ln.st2[:], 1.0 / D,
                                        ln.row(3), ALU.mult, ALU.subtract),
               "V")
    p.wait(p.sc, "V", vv)
    sv = p.inc(p.sc.activation(ln.row(3), ln.row(1), AF.Sqrt, bias=c.one(eps)), "S")
    p.wait(p.v, "S", sv)
    vv = p.inc(p.v.reciprocal(ln.row(2), ln.row(3)), "V")
    p.wait(p.t, "V", vv)
    bm, sm = ln.bc.acquire()
    p.t.matmul(bm[:], c.ones_row_f[:], ln.row(0), start=True, stop=True)
    br, sr = ln.bc.acquire()
    tb = p.inc(p.t.matmul(br[:], c.ones_row_f[:], ln.row(2), start=True,
                          stop=True), "T")
    p.wait(p.v, "T", tb)
    for kc in range(KC):
        tmp = ln.sq[:, (kc % 2) * TT:(kc % 2) * TT + TT]
        p.v.tensor_tensor(tmp, xsv(kc), bm[:], ALU.subtract)
        vv = p.inc(p.v.tensor_tensor(ln_out_view(kc), tmp, br[:], ALU.mult),
                   "V")
    ln.bc.release(sm, "V", vv)
    ln.bc.release(sr, "V", vv)
    ln.last_v = vv
    return vv


class ResidualState:
    def __init__(self, p, name, n=2):
        self.buf = [p.sb(f"{name}_res{i}", [128, 3 * TT], F32)
                    for i in range(n)]
        self.n = n
        self.i = 0
        self.guard = [None] * n


def emit_residual_store(p, c, res, xin_ap, xout_ap, fc, tt, psum_ap, alpha,
                        tval, bias_col=None, scale_col=None):
    if not isinstance(psum_ap, bass.AP):
        psum_ap = psum_ap[:]
    slot = res.i % res.n
    res.i += 1
    buf = res.buf[slot]
    if res.guard[slot] is not None:
        p.wait(p.sy, "O", res.guard[slot])
    lv = p.inc(p.sy.dma_start(buf[:, 0:TT],
                              feat_dram_view(xin_ap, fc, tt * TT, TT)), "L")
    p.wait(p.v, "L", lv)
    p.wait(p.v, "T", tval)
    if bias_col is None and scale_col is None:
        vv = p.inc(p.v.scalar_tensor_tensor(buf[:, TT:2 * TT], psum_ap, alpha,
                                            buf[:, 0:TT], ALU.mult, ALU.add),
                   "V")
    else:
        tmp = buf[:, 2 * TT:3 * TT]
        if bias_col is not None:
            p.v.tensor_scalar(tmp, psum_ap, bias_col, alpha, ALU.add, ALU.mult)
        else:
            p.v.tensor_scalar(tmp, psum_ap, scale_col, alpha, ALU.mult,
                              ALU.mult)
        vv = p.inc(p.v.tensor_tensor(buf[:, TT:2 * TT], tmp, buf[:, 0:TT],
                                     ALU.add), "V")
    p.wait(p.sy, "V", vv)
    ov = p.inc(p.sy.dma_start(feat_dram_view(xout_ap, fc, tt * TT, TT),
                              buf[:, TT:2 * TT]), "O")
    res.guard[slot] = ov
    return vv



def emit_proj(p, wv, nk, in_view, nf, epilogue, pool, tts=None):
    if tts is None:
        tts = range(NTT)
    for fc in range(nf // 128):
        for tt in tts:
            ps_ap, slot = pool.acquire()
            rec = None
            for kc in range(nk):
                rec = p.t.matmul(ps_ap[:], wv(kc, fc), in_view(kc, tt),
                                 start=(kc == 0), stop=(kc == nk - 1))
            tval = p.inc(rec, "T")
            epilogue(fc, tt, ps_ap, slot, tval)



class ConvWork:

    def __init__(self, p):
        self.stage = [p.sb(f"cv_st{i}", [128, 4096], BF16) for i in range(2)]
        self.tbuf = [p.sb(f"cv_tb{i}", [128, 4096], BF16) for i in range(2)]
        self.pst = [p.ps(f"cv_ps{i}", [128, 128], BF16) for i in range(2)]
        self.stage_guard = [0, 0]
        self.stage_cast_guard = [0, 0]
        self.tbuf_guard = [0, 0]
        self.pst_guard = [0, 0]


def emit_convert_lhsT(p, c, cw, src_ap, dst_ap, R, C):
    nr = R // 128
    ncc = C // 128
    for rc in range(nr):
        sl = rc % 2
        if cw.stage_guard[sl]:
            p.wait(p.g, "T", cw.stage_guard[sl])
        gv = p.inc(p.g.dma_start(cw.stage[sl][:, 0:C],
                                 src_ap[rc * 128:(rc + 1) * 128, :]), "G")
        p.wait(p.t, "G", gv)
        tbl = rc % 2
        if cw.tbuf_guard[tbl]:
            p.wait(p.v, "O", cw.tbuf_guard[tbl])
        vv = 0
        for cc in range(ncc):
            psl = cc % 2
            if cw.pst_guard[psl]:
                p.wait(p.t, "V", cw.pst_guard[psl])
            tv = p.inc(p.t.transpose(cw.pst[psl][:],
                                     cw.stage[sl][:, cc * 128:(cc + 1) * 128],
                                     c.ident_b[:]), "T")
            cw.stage_guard[sl] = tv
            p.wait(p.v, "T", tv)
            vv = p.inc(p.v.tensor_copy(
                cw.tbuf[tbl][:, cc * 128:(cc + 1) * 128], cw.pst[psl][:]),
                "V")
            cw.pst_guard[psl] = vv
        p.wait(p.sy, "V", vv)
        dstv = dst_ap[rc].rearrange("k p f -> p k f")
        srcv = cw.tbuf[tbl][:, 0:ncc * 128].rearrange("p (k f) -> p k f",
                                                      f=128)
        ov = p.inc(p.sy.dma_start(dstv, srcv), "O")
        cw.tbuf_guard[tbl] = ov
    return p.val("O")


def emit_load_wconv(p, cw_ap4, w_sb, nk, fcs, queue="Q", w_sb_off=0):
    px = p.sc if queue == "Q" else (p.sy if queue in ("L", "O") else p.g)
    for i, fc in enumerate(fcs):
        src = cw_ap4[fc].rearrange("k p f -> p k f")
        dst = w_sb[:, w_sb_off + i * nk * 128: w_sb_off + (i + 1) * nk * 128]
        dst = dst.rearrange("p (k f) -> p k f", f=128)
        p.inc(px.dma_start(dst, src), queue)
    return p.val(queue)


def wconv_view(w_sb, nk, off=0):
    def view(kc, i, fw=128):
        base = off + (i * nk + kc) * 128
        return w_sb[:, base: base + fw]
    return view


def emit_load_w(p, hbm_f32, w_sb, din, dout):
    nk = din // 128
    for kc in range(nk):
        dst = w_sb[:, kc * dout:(kc + 1) * dout]
        src = hbm_f32[kc * 128:(kc + 1) * 128, :]
        p.inc(p.g.dma_start(dst, src), "G")
    return p.val("G")



def emit_l1_ffn(p, c, xin_ap, xout_ap, cwg, cwu, cwd):
    H = 3072
    NH = H // 128
    with p.scope():
        wg = p.sb("l1f_wg", [128, NH * KC * 128], BF16)
        wu = p.sb("l1f_wu", [128, NH * KC * 128], BF16)
        wdb = [p.sb(f"l1f_wd{i}", [128, NH * 128], BF16) for i in range(2)]
        emit_load_wconv(p, cwg, wg, KC, range(NH), "Q")
        qv = emit_load_wconv(p, cwu, wu, KC, range(NH), "Q")
        wgv, wuv = wconv_view(wg, KC), wconv_view(wu, KC)
        wdv = [wconv_view(wdb[0], 1), wconv_view(wdb[1], 1)]
        pool = PsumPool(p, "l1f_ps", 4, [128, TT], F32)
        ln = LNState(p, "l1f_ln", pool)
        lnb = p.sb("l1f_lnb", [128, KC * TT], BF16)
        hbuf = p.sb("l1f_h", [128, NH * TT], BF16)
        silu_t = p.sb("l1f_silu", [128, TT], F32)
        res = ResidualState(p, "l1f")
        p.wait(p.t, "Q", qv)
        lnv = chunks_view(lnb, KC, TT)
        hv = chunks_view(hbuf, NH, TT)
        last_hv = None
        wd_guard = [0, 0]

        def load_wd(fc):
            sl = fc % 2
            if wd_guard[sl]:
                p.wait(p.sc, "T", wd_guard[sl])
            return emit_load_wconv(p, cwd, wdb[sl], NH, [fc], "Q")

        for tt in range(NTT):
            lnt = emit_ln_tile(p, c, ln, xin_ap, tt, lambda kc: lnv(kc))
            p.wait(p.t, "V", lnt)
            for fc in range(NH):
                psg, sg = pool.acquire()
                rec = None
                for kc in range(KC):
                    rec = p.t.matmul(psg[:], wgv(kc, fc), lnv(kc),
                                     start=(kc == 0), stop=(kc == KC - 1))
                tg = p.inc(rec, "T")
                psu, su = pool.acquire()
                rec = None
                for kc in range(KC):
                    rec = p.t.matmul(psu[:], wuv(kc, fc), lnv(kc),
                                     start=(kc == 0), stop=(kc == KC - 1))
                tu = p.inc(rec, "T")
                p.wait(p.sc, "T", tg)
                if last_hv is not None:
                    p.wait(p.sc, "V", last_hv)
                sv = p.inc(p.sc.activation(silu_t[:], psg[:], AF.Silu,
                                           scale=c.col(1.0 / 127.0)), "S")
                p.wait(p.v, "S", sv)
                p.wait(p.v, "T", tu)
                last_hv = p.inc(
                    p.v.scalar_tensor_tensor(hv(fc), psu[:], 1.0 / 127.0,
                                             silu_t[:], ALU.mult, ALU.mult),
                    "V")
                pool.release(sg, "S", sv)
                pool.release(su, "V", last_hv)
            p.wait(p.t, "V", last_hv)
            qfc = {0: load_wd(0)}
            for fc in range(KC):
                if fc + 1 < KC:
                    qfc[fc + 1] = load_wd(fc + 1)
                ps_ap, slot = pool.acquire()
                p.wait(p.t, "Q", qfc[fc])
                rec = None
                for kc in range(NH):
                    rec = p.t.matmul(ps_ap[:], wdv[fc % 2](kc, 0), hv(kc),
                                     start=(kc == 0), stop=(kc == NH - 1))
                tval = p.inc(rec, "T")
                wd_guard[fc % 2] = tval
                vv = emit_residual_store(p, c, res, xin_ap, xout_ap, fc, tt,
                                         ps_ap[:], 0.5 / 127.0, tval)
                pool.release(slot, "V", vv)
            p.wait(p.v, "T", p.val("T"))


def emit_l3_ffn(p, c, xin_ap, xout_ap, w):
    FF = 4096
    NF = FF // 128
    R = 256
    NR = R // 128
    with p.scope():
        u1 = p.sb("l3f_u1", [128, KC * R], BF16)
        v1 = p.sb("l3f_v1", [128, NR * FF], BF16)
        u2 = p.sb("l3f_u2", [128, NF * R], BF16)
        v2 = p.sb("l3f_v2", [128, NR * D], BF16)
        emit_load_w(p, w["U1"], u1, D, R)
        emit_load_w(p, w["V1"], v1, R, FF)
        emit_load_w(p, w["U2"], u2, FF, R)
        gv = emit_load_w(p, w["V2"], v2, R, D)
        b1sb = p.sb("l3f_b1", [128, NF], F32)
        b2sb = p.sb("l3f_b2", [128, KC], F32)
        lv = p.inc(p.sy.dma_start(
            b1sb[:], w["b1"].rearrange("(c p) -> p c", p=128),
            allow_slow_non_contiguous=True), "L")
        lv = p.inc(p.sy.dma_start(
            b2sb[:], w["b2"].rearrange("(c p) -> p c", p=128),
            allow_slow_non_contiguous=True), "L")
        u1v, v1v = w_view(u1, R), w_view(v1, FF)
        u2v, v2v = w_view(u2, R), w_view(v2, D)
        pool = PsumPool(p, "l3f_ps", 4, [128, TT], F32)
        ln = LNState(p, "l3f_ln", pool)
        lnb = p.sb("l3f_lnb", [128, KC * TT], BF16)
        pbuf = p.sb("l3f_p", [128, NR * TT], BF16)
        hbuf = p.sb("l3f_h", [128, NF * TT], BF16)
        qbuf = p.sb("l3f_q", [128, NR * TT], BF16)
        res = ResidualState(p, "l3f")
        p.wait(p.t, "G", gv)
        p.wait(p.t, "L", lv)
        lnv = chunks_view(lnb, KC, TT)
        pv = chunks_view(pbuf, NR, TT)
        hv = chunks_view(hbuf, NF, TT)
        qv = chunks_view(qbuf, NR, TT)

        for tt in range(NTT):
            lnt = emit_ln_tile(p, c, ln, xin_ap, tt, lambda kc: lnv(kc))
            p.wait(p.t, "V", lnt)

            def epi_p(fc, _tt, ps_ap, slot, tval):
                p.wait(p.sc, "T", tval)
                sv = p.inc(p.sc.activation(pv(fc), ps_ap[:], AF.Copy), "S")
                pool.release(slot, "S", sv)
            emit_proj(p, u1v, KC, lambda kc, _tt: lnv(kc), R, epi_p, pool,
                      tts=[tt])
            p.wait(p.t, "S", p.val("S"))

            def epi_h(fc, _tt, ps_ap, slot, tval):
                p.wait(p.sc, "T", tval)
                sv = p.inc(p.sc.activation(hv(fc), ps_ap[:], AF.Gelu_apprx_tanh,
                                           bias=b1sb[:, fc:fc + 1]), "S")
                pool.release(slot, "S", sv)
            emit_proj(p, v1v, NR, lambda kc, _tt: pv(kc), FF, epi_h, pool,
                      tts=[tt])
            p.wait(p.t, "S", p.val("S"))

            def epi_q(fc, _tt, ps_ap, slot, tval):
                p.wait(p.sc, "T", tval)
                sv = p.inc(p.sc.activation(qv(fc), ps_ap[:], AF.Copy), "S")
                pool.release(slot, "S", sv)
            emit_proj(p, u2v, NF, lambda kc, _tt: hv(kc), R, epi_q, pool,
                      tts=[tt])
            p.wait(p.t, "S", p.val("S"))

            def epi_o(fc, _tt, ps_ap, slot, tval):
                vv = emit_residual_store(p, c, res, xin_ap, xout_ap, fc, _tt,
                                         ps_ap, 0.5, tval,
                                         bias_col=b2sb[:, fc:fc + 1])
                pool.release(slot, "V", vv)
            emit_proj(p, v2v, NR, lambda kc, _tt: qv(kc), D, epi_o, pool,
                      tts=[tt])
            p.wait(p.sc, "T", p.val("T"))
            p.wait(p.v, "T", p.val("T"))



INPUT_SPECS = {
    "L0_attn_basis": ([1024, 256], F32), "L0_attn_q_coeffs": ([256, 256], F32),
    "L0_attn_k_coeffs": ([256, 256], F32), "L0_attn_v_coeffs": ([256, 256], F32),
    "L0_attn_o_coeffs": ([256, 256], F32), "L0_attn_decay_logit": ([1], F32),
    "L0_attn_out_scale": ([1], F32),
    "L0_ffn_cn_w1": ([1024, 1024], F32), "L0_ffn_cn_b1": ([1024], F32),
    "L0_ffn_cn_w2": ([1024, 512], F32), "L0_ffn_cn_b2": ([512], F32),
    "L0_ffn_cn_w3": ([512, 256], F32), "L0_ffn_cn_b3": ([256], F32),
    "L0_ffn_cn_w4": ([256, 1], F32), "L0_ffn_cn_b4": ([1], F32),
    "L0_ffn_gc_w1": ([1024, 512], F32), "L0_ffn_gc_b1": ([512], F32),
    "L0_ffn_gc_w2": ([512, 256], F32), "L0_ffn_gc_b2": ([256], F32),
    "L0_ffn_gc_w3": ([256, 1], F32), "L0_ffn_gc_b3": ([1], F32),
    "L0_ffn_adaptive_range": ([1], F32),
    "L1_attn_qkv_w": ([3072, 1024], F32), "L1_attn_temperature": ([16], F32),
    "L1_attn_proj_w": ([1024, 1024], F32), "L1_attn_proj_b": ([1024], F32),
    "L1_ffn_w_gate": ([3072, 1024], I32), "L1_ffn_w_up": ([3072, 1024], I32),
    "L1_ffn_w_down": ([1024, 3072], I32),
    "L2_attn_W_lambda": ([16, 1024], F32), "L2_attn_b_lambda": ([16], F32),
    "L2_attn_W_B": ([16, 1024], F32), "L2_attn_W_C": ([1024, 16], F32),
    "L2_attn_W_gate": ([1024, 1024], F32), "L2_attn_rms_weight": ([1024], F32),
    "L2_ffn_gate_w": ([8, 1024], F32), "L2_ffn_w1": ([8, 1024, 4096], F32),
    "L2_ffn_b1": ([8, 4096], F32), "L2_ffn_w2": ([8, 4096, 1024], F32),
    "L2_ffn_b2": ([8, 1024], F32),
    "L3_attn_Wq": ([1024, 1024], F32), "L3_attn_bq": ([1024], F32),
    "L3_attn_Wk": ([1024, 1024], F32), "L3_attn_bk": ([1024], F32),
    "L3_attn_Wv": ([1024, 1024], F32), "L3_attn_bv": ([1024], F32),
    "L3_attn_Wo": ([1024, 1024], F32), "L3_attn_bo": ([1024], F32),
    "L3_attn_rk_table": ([33, 64], F32), "L3_attn_rv_table": ([33, 64], F32),
    "L3_ffn_U1": ([1024, 256], F32), "L3_ffn_V1": ([256, 4096], F32),
    "L3_ffn_b1": ([4096], F32), "L3_ffn_U2": ([4096, 256], F32),
    "L3_ffn_V2": ([256, 1024], F32), "L3_ffn_b2": ([1024], F32),
}

def emit_convert_cast(p, cw, src2d, dst2d, R, C):
    for rc in range(R // 128):
        sl = rc % 2
        if cw.stage_cast_guard[sl]:
            p.wait(p.g, "O", cw.stage_cast_guard[sl])
        gv = p.inc(p.g.dma_start(cw.stage[sl][:, 0:C],
                                 src2d[rc * 128:(rc + 1) * 128, :]), "G")
        p.wait(p.sy, "G", gv)
        cw.stage_cast_guard[sl] = p.inc(
            p.sy.dma_start(dst2d[rc * 128:(rc + 1) * 128, :],
                           cw.stage[sl][:, 0:C]), "O")
    return p.val("O")


CONVERT_W = {
    "L1_ffn_w_gate": (3072, 1024), "L1_ffn_w_up": (3072, 1024),
    "L1_ffn_w_down": (1024, 3072),
    "L1_attn_qkv_w": (3072, 1024), "L1_attn_proj_w": (1024, 1024),
    "L2_attn_W_gate": (1024, 1024),
    "L3_attn_Wq": (1024, 1024), "L3_attn_Wk": (1024, 1024),
    "L3_attn_Wv": (1024, 1024), "L3_attn_Wo": (1024, 1024),
}

KIND_CONVERTS = {
    "l1_ffn": ["L1_ffn_w_gate", "L1_ffn_w_up", "L1_ffn_w_down"],
    "l1_attn": ["L1_attn_qkv_w", "L1_attn_proj_w"],
    "l2_attn": ["L2_attn_W_gate"],
    "l3_attn": ["L3_attn_Wq", "L3_attn_Wk", "L3_attn_Wv", "L3_attn_Wo"],
}

CAST_W = {"L2_ffn_w1": (8192, 4096), "L2_ffn_w2": (32768, 1024)}
KIND_CASTS = {"l2_ffn": ["L2_ffn_w1", "L2_ffn_w2"]}


def build_chain(specs, a2a_before=None):
    nc = bass.Bass("TRN2", target_bir_lowering=False)
    xin = nc.dram_tensor("xin", [D, TOK], F32, kind="ExternalInput").ap()
    xout = nc.dram_tensor("xout", [D, TOK], F32, kind="ExternalOutput").ap()
    used = set()
    for kind, pref in specs:
        for name in INPUT_SPECS:
            if name.startswith(pref):
                used.add(name)
    ins = {}
    for name in sorted(used):
        shape, dt = INPUT_SPECS[name]
        ins[name] = nc.dram_tensor(name, shape, dt, kind="ExternalInput").ap()

    with ExitStack() as ctx:
        p = Prog(nc, ctx)
        c = Const(p)
        p.barrier()
        conv = {}
        need_conv = []
        need_cast = []
        for kind, pref in specs:
            for nm in KIND_CONVERTS.get(kind, []):
                if nm in used and nm not in conv:
                    conv[nm] = None
                    need_conv.append(nm)
            for nm in KIND_CASTS.get(kind, []):
                if nm in used and nm not in conv:
                    conv[nm] = None
                    need_cast.append(nm)
        if need_conv or need_cast:
            with p.scope():
                cw = ConvWork(p)
                for nm in need_conv:
                    R, C = CONVERT_W[nm]
                    dst = nc.dram_tensor("cw_" + nm,
                                         [R // 128, C // 128, 128, 128], BF16,
                                         kind="Internal").ap()
                    emit_convert_lhsT(p, c, cw, ins[nm], dst, R, C)
                    conv[nm] = dst
                for nm in need_cast:
                    R, C = CAST_W[nm]
                    dst = nc.dram_tensor("cw_" + nm, [R, C], BF16,
                                         kind="Internal").ap()
                    src_flat = ins[nm]
                    if len(INPUT_SPECS[nm][0]) == 3:
                        src_flat = src_flat.rearrange("n a b -> (n a) b")
                    emit_convert_cast(p, cw, src_flat, dst, R, C)
                    conv[nm] = dst
        p.barrier()
        n = len(specs)
        mids = [nc.dram_tensor(f"xmid{i}", [D, TOK], F32, kind="Internal").ap()
                for i in range(n - 1)]
        xs = [xin] + mids + [xout]
        if a2a_before is not None:
            a2a_in = nc.dram_tensor("a2a_in", [NCORES * D, TT], F32,
                                    kind="Internal").ap()
            a2a_out = nc.dram_tensor("a2a_out", [NCORES * D, TT], F32,
                                     kind="Internal").ap()
        for i, (kind, pref) in enumerate(specs):
            if a2a_before is not None and i == a2a_before:
                p.name_prefix = f"a2a_"
                emit_a2a_reshard(p, xs[i], a2a_in, a2a_out)
                p.barrier()
            p.name_prefix = f"s{i}_"
            emit_sublayer(p, c, kind, pref, xs[i], xs[i + 1], ins, conv)
            p.barrier()
        p.name_prefix = ""
        from concourse.aot_env import allow_ici_all_to_all
        with allow_ici_all_to_all():
            p.play()
    return nc, sorted(used)


def emit_a2a_reshard(p, x_ap, a2a_in, a2a_out):
    for j in range(NCORES):
        for b in range(2):
            p.inc(p.sy.dma_start(
                a2a_in[j * D:(j + 1) * D, b * 256:(b + 1) * 256],
                x_ap[:, b * 2048 + j * 256:b * 2048 + (j + 1) * 256]), "O")
    p.wait(p.g, "O", p.val("O"))
    rec = p.g.collective_compute(
        "AllToAll", ALU.bypass, replica_groups=[list(range(NCORES))],
        ins=[a2a_in.opt()], outs=[a2a_out.opt()])
    p.inc(rec, "P", 1)
    p.wait(p.sy, "P", p.val("P"))
    for i in range(NCORES):
        for b in range(2):
            p.inc(p.sy.dma_start(
                x_ap[:, (2 * i + b) * 256:(2 * i + b + 1) * 256],
                a2a_out[i * D:(i + 1) * D, b * 256:(b + 1) * 256]), "O")
    p.wait(p.sy, "O", p.val("O"))


def emit_sublayer(p, card, kind, pref, xi, xo, ins, conv):
    if kind == "l1_ffn":
        emit_l1_ffn(p, card, xi, xo, conv[pref + "w_gate"],
                    conv[pref + "w_up"], conv[pref + "w_down"])
    elif kind == "l3_ffn":
        w = {k: ins[pref + k] for k in ("U1", "V1", "b1", "U2", "V2", "b2")}
        emit_l3_ffn(p, card, xi, xo, w)
    elif kind == "l3_attn":
        w = {k: ins[pref + k] for k in ("bq", "bk", "bv", "bo", "rk_table",
                                        "rv_table")}
        emit_l3_attn(p, card, xi, xo, w, conv[pref + "Wq"], conv[pref + "Wk"],
                     conv[pref + "Wv"], conv[pref + "Wo"])
    elif kind == "l1_attn":
        w = {k: ins[pref + k] for k in ("temperature", "proj_b")}
        emit_l1_attn(p, card, xi, xo, w, conv[pref + "qkv_w"],
                     conv[pref + "proj_w"])
    elif kind == "l0_attn":
        w = {k: ins[pref + k] for k in ("basis", "q_coeffs", "k_coeffs",
                                        "v_coeffs", "o_coeffs", "decay_logit",
                                        "out_scale")}
        emit_l0_attn(p, card, xi, xo, w)
    elif kind == "l2_ffn":
        w = {k: ins[pref + k] for k in ("gate_w", "b1", "b2")}
        emit_l2_ffn(p, card, xi, xo, w, conv[pref + "w1"], conv[pref + "w2"])
    elif kind == "l2_attn":
        w = {k: ins[pref + k] for k in ("W_lambda", "b_lambda", "W_B", "W_C",
                                        "rms_weight")}
        emit_l2_attn(p, card, xi, xo, w, conv[pref + "W_gate"])
    elif kind == "l0_ffn":
        keys = ["cn_w%d" % i for i in (1, 2, 3, 4)]
        keys += ["cn_b%d" % i for i in (1, 2, 3, 4)]
        keys += ["gc_w%d" % i for i in (1, 2, 3)]
        keys += ["gc_b%d" % i for i in (1, 2, 3)]
        keys += ["adaptive_range"]
        w = {k: ins[pref + k] for k in keys}
        emit_l0_ffn(p, card, xi, xo, w)
    else:
        raise NotImplementedError(kind)


def emit_l0_ffn(p, c, xin_ap, xout_ap, w):
    with p.scope():
        pool = PsumPool(p, "l0f_ps", 4, [128, TT], F32)
        prow = p.ps("l0f_prow", [1, TT], F32)
        ln = LNState(p, "l0f_ln", pool)
        w1 = p.sb("l0f_w1", [128, KC * 1024], BF16)
        w2 = p.sb("l0f_w2", [128, KC * 512], BF16)
        w3 = p.sb("l0f_w3", [128, 4 * 256], BF16)
        w4 = p.sb("l0f_w4", [128, 2 * 1], BF16)
        g1 = p.sb("l0f_g1", [128, KC * 512], BF16)
        g2 = p.sb("l0f_g2", [128, 4 * 256], BF16)
        g3 = p.sb("l0f_g3", [128, 2 * 1], BF16)
        emit_load_w(p, w["cn_w1"], w1, 1024, 1024)
        emit_load_w(p, w["cn_w2"], w2, 1024, 512)
        emit_load_w(p, w["cn_w3"], w3, 512, 256)
        emit_load_w(p, w["cn_w4"], w4, 256, 1)
        emit_load_w(p, w["gc_w1"], g1, 1024, 512)
        emit_load_w(p, w["gc_w2"], g2, 512, 256)
        gv = emit_load_w(p, w["gc_w3"], g3, 256, 1)
        b1c = p.sb("l0f_b1", [128, 8], F32)
        b2c = p.sb("l0f_b2", [128, 4], F32)
        b3c = p.sb("l0f_b3", [128, 2], F32)
        gb1c = p.sb("l0f_gb1", [128, 4], F32)
        gb2c = p.sb("l0f_gb2", [128, 2], F32)
        smalls = p.sb("l0f_small", [1, 8], F32)
        p.inc(p.sy.dma_start(b1c[:], w["cn_b1"].rearrange("(c p) -> p c", p=128),
                             allow_slow_non_contiguous=True), "L")
        p.inc(p.sy.dma_start(b2c[:], w["cn_b2"].rearrange("(c p) -> p c", p=128),
                             allow_slow_non_contiguous=True), "L")
        p.inc(p.sy.dma_start(b3c[:], w["cn_b3"].rearrange("(c p) -> p c", p=128),
                             allow_slow_non_contiguous=True), "L")
        p.inc(p.sy.dma_start(gb1c[:], w["gc_b1"].rearrange("(c p) -> p c", p=128),
                             allow_slow_non_contiguous=True), "L")
        p.inc(p.sy.dma_start(gb2c[:], w["gc_b2"].rearrange("(c p) -> p c", p=128),
                             allow_slow_non_contiguous=True), "L")
        p.inc(p.sy.dma_start(smalls[:, 0:1], w["cn_b4"][0:1]), "L")
        p.inc(p.sy.dma_start(smalls[:, 1:2], w["gc_b3"][0:1]), "L")
        lv = p.inc(p.sy.dma_start(smalls[:, 2:3], w["adaptive_range"][0:1]), "L")

        lnf = p.sb("l0f_lnf", [128, KC * TT], F32)
        out1 = p.sb("l0f_out1", [128, KC * TT], F32)
        outb = p.sb("l0f_outb", [128, KC * TT], BF16)
        h1 = p.sb("l0f_h1", [128, KC * TT], BF16)
        h2 = p.sb("l0f_h2", [128, 4 * TT], BF16)
        h3 = p.sb("l0f_h3", [128, 2 * TT], BF16)
        rows = p.sb("l0f_rows", [1, 4 * TT], F32)
        gacc = p.sb("l0f_gacc", [128, 2 * KC], F32)
        gred = p.sb("l0f_gred", [128, KC], F32)
        gcb = p.sb("l0f_gcb", [128, 2 * KC], BF16)
        gh1 = p.sb("l0f_gh1", [128, 4 * 2], BF16)
        gh2 = p.sb("l0f_gh2", [128, 2 * 2], BF16)
        res = ResidualState(p, "l0f")
        twtmp = p.sb("l0f_twtmp", [128, TT], F32)
        lnfv = chunks_view(lnf, KC, TT)
        out1v = chunks_view(out1, KC, TT)
        outbv = chunks_view(outb, KC, TT)
        h1v = chunks_view(h1, KC, TT)
        h2v = chunks_view(h2, 4, TT)
        h3v = chunks_view(h3, 2, TT)
        w1v, w2v = w_view(w1, 1024), w_view(w2, 512)
        w3v, w4v = w_view(w3, 256), w_view(w4, 1)
        g1v, g2v, g3v = w_view(g1, 512), w_view(g2, 256), w_view(g3, 1)
        p.wait(p.t, "G", gv)
        p.wait(p.t, "L", lv)

        def row(i, w_=TT):
            return rows[:, i * TT:i * TT + w_]

        for tt in range(NTT):
            b = tt // (NTT // 2)
            lnt = emit_ln_tile(p, c, ln, xin_ap, tt, lambda kc: lnfv(kc))
            for kc in range(KC):
                p.v.reduce_sum(gred[:, kc:kc + 1], lnfv(kc), axis=AX.X)
            if tt % 4 == 0:
                vv = p.inc(p.v.tensor_copy(gacc[:, b * KC:(b + 1) * KC],
                                           gred[:]), "V")
            else:
                vv = p.inc(p.v.tensor_tensor(gacc[:, b * KC:(b + 1) * KC],
                                             gacc[:, b * KC:(b + 1) * KC],
                                             gred[:], ALU.add), "V")
            ln.last_v = vv
        src_r = bass.AP(gacc, 0, [[2 * KC, 128], [1, KC], [KC, 2]])
        dst_r = gcb[:].rearrange("p (k b) -> p k b", b=2)
        vv = p.inc(p.v.tensor_scalar_mul(dst_r, src_r, 1.0 / 2048.0), "V")
        p.wait(p.t, "V", vv)

        def gc_rhs(kc):
            return gcb[:, kc * 2:kc * 2 + 2]
        for fc in range(4):
            ps_ap, slot = pool.acquire()
            rec = None
            for kc in range(KC):
                rec = p.t.matmul(ps_ap[:, 0:2], g1v(kc, fc), gc_rhs(kc),
                                 start=(kc == 0), stop=(kc == KC - 1))
            tvv = p.inc(rec, "T")
            p.wait(p.sc, "T", tvv)
            sv = p.inc(p.sc.activation(gh1[:, fc * 2:fc * 2 + 2], ps_ap[:, 0:2],
                                       AF.Gelu, bias=gb1c[:, fc:fc + 1]), "S")
            pool.release(slot, "S", sv)

        def gh1_rhs(kc):
            return bass.AP(gh1, kc * 2, [[8, 128], [1, 2]])
        p.wait(p.t, "S", p.val("S"))
        for fc in range(2):
            ps_ap, slot = pool.acquire()
            rec = None
            for kc in range(4):
                rec = p.t.matmul(ps_ap[:, 0:2], g2v(kc, fc), gh1_rhs(kc),
                                 start=(kc == 0), stop=(kc == 3))
            tvv = p.inc(rec, "T")
            p.wait(p.sc, "T", tvv)
            sv = p.inc(p.sc.activation(gh2[:, fc * 2:fc * 2 + 2], ps_ap[:, 0:2],
                                       AF.Gelu, bias=gb2c[:, fc:fc + 1]), "S")
            pool.release(slot, "S", sv)

        def gh2_rhs(kc):
            return bass.AP(gh2, kc * 2, [[4, 128], [1, 2]])
        p.wait(p.t, "S", p.val("S"))
        rec = None
        for kc in range(2):
            rec = p.t.matmul(prow[:, 0:2], g3v(kc, 0, 1), gh2_rhs(kc),
                             start=(kc == 0), stop=(kc == 1))
        tvv = p.inc(rec, "T")
        p.wait(p.sc, "T", tvv)
        sv = p.inc(p.sc.activation(smalls[:, 4:6], prow[:, 0:2], AF.Sigmoid,
                                   bias=smalls[:, 1:2]), "S")
        p.wait(p.v, "S", sv)
        p.v.tensor_scalar_mul(smalls[:, 4:6], smalls[:, 4:6], 0.7)
        vv = p.inc(p.v.tensor_scalar_mul(smalls[:, 3:4], smalls[:, 2:3], 2.0),
                   "V")

        TWIST = [(4, 1.0), (5, 1.0), (6, -1.0), (7, -1.0),
                 (0, -1.0), (1, -1.0), (2, 1.0), (3, 1.0)]
        p.wait(p.sc, "V", vv)
        for tt in range(NTT):
            b = tt // (NTT // 2)
            lnt = emit_ln_tile(p, c, ln, xin_ap, tt, lambda kc: lnfv(kc))
            cur, nxt = lnf, out1
            for cyc in range(3):
                curv = chunks_view(cur, KC, TT)
                nxtv = chunks_view(nxt, KC, TT)
                for kc in range(KC):
                    vv = p.inc(p.v.tensor_copy(outbv(kc), curv(kc)), "V")
                p.wait(p.t, "V", vv)

                def epi1(fc, _tt, ps_ap, slot, tval):
                    p.wait(p.sc, "T", tval)
                    sv = p.inc(p.sc.activation(h1v(fc), ps_ap[:], AF.Gelu,
                                               bias=b1c[:, fc:fc + 1]), "S")
                    pool.release(slot, "S", sv)
                emit_proj(p, w1v, KC, lambda kc, _: outbv(kc), 1024, epi1,
                          pool, tts=[tt])
                p.wait(p.t, "S", p.val("S"))

                def epi2(fc, _tt, ps_ap, slot, tval):
                    p.wait(p.sc, "T", tval)
                    sv = p.inc(p.sc.activation(h2v(fc), ps_ap[:], AF.Gelu,
                                               bias=b2c[:, fc:fc + 1]), "S")
                    pool.release(slot, "S", sv)
                emit_proj(p, w2v, KC, lambda kc, _: h1v(kc), 512, epi2,
                          pool, tts=[tt])
                p.wait(p.t, "S", p.val("S"))

                def epi3(fc, _tt, ps_ap, slot, tval):
                    p.wait(p.sc, "T", tval)
                    sv = p.inc(p.sc.activation(h3v(fc), ps_ap[:], AF.Gelu,
                                               bias=b3c[:, fc:fc + 1]), "S")
                    pool.release(slot, "S", sv)
                emit_proj(p, w3v, 4, lambda kc, _: h2v(kc), 256, epi3,
                          pool, tts=[tt])
                p.wait(p.t, "S", p.val("S"))

                rec = None
                for kc in range(2):
                    rec = p.t.matmul(prow[:], w4v(kc, 0, 1), h3v(kc),
                                     start=(kc == 0), stop=(kc == 1))
                tvv = p.inc(rec, "T")
                p.wait(p.sc, "T", tvv)
                sv = p.inc(p.sc.activation(row(0), prow[:], AF.Sigmoid,
                                           bias=smalls[:, 0:1]), "S")
                p.wait(p.v, "S", sv)
                p.v.tensor_scalar(row(2), row(0), 0.3, smalls[:, 4 + b:5 + b],
                                  ALU.mult, ALU.add)
                p.v.tensor_scalar(row(2), row(2), -0.5, None, ALU.add)
                vv = p.inc(p.v.tensor_scalar(row(1), row(2),
                                             smalls[:, 3:4], 0.1,
                                             ALU.mult, ALU.add), "V")
                p.wait(p.t, "V", vv)
                bcp, bslot = pool.acquire()
                tvv = p.inc(p.t.matmul(bcp[:], c.ones_row_f[:], row(1),
                                       start=True, stop=True), "T")
                p.wait(p.v, "T", tvv)
                for kc in range(KC):
                    srckc, sgn = TWIST[kc]
                    op = ALU.add if sgn > 0 else ALU.subtract
                    p.v.tensor_tensor(twtmp[:], bcp[:], curv(srckc), ALU.mult)
                    vv = p.inc(p.v.tensor_tensor(nxtv(kc), curv(kc), twtmp[:],
                                                 op), "V")
                pool.release(bslot, "V", vv)
                cur, nxt = nxt, (lnf if cur is out1 else out1)
            curv = chunks_view(cur, KC, TT)
            for kc in range(KC):
                slot = res.i % res.n
                res.i += 1
                buf = res.buf[slot]
                if res.guard[slot] is not None:
                    p.wait(p.sy, "O", res.guard[slot])
                lv = p.inc(p.sy.dma_start(buf[:, 0:TT],
                                          feat_dram_view(xin_ap, kc, tt * TT,
                                                         TT)), "L")
                p.wait(p.v, "L", lv)
                vv = p.inc(p.v.scalar_tensor_tensor(buf[:, TT:2 * TT],
                                                    curv(kc), 0.5,
                                                    buf[:, 0:TT], ALU.mult,
                                                    ALU.add), "V")
                p.wait(p.sy, "V", vv)
                ov = p.inc(p.sy.dma_start(feat_dram_view(xout_ap, kc, tt * TT,
                                                         TT),
                                          buf[:, TT:2 * TT]), "O")
                res.guard[slot] = ov
            ln.last_v = p.val("V")
            p.wait(p.v, "T", p.val("T"))



def emit_l2_attn(p, c, xin_ap, xout_ap, w, cw_gate):
    N = 16
    SEQ = 2048
    with p.scope():
        pool = PsumPool(p, "l2a_ps", 3, [128, TT], F32)
        prow = p.ps("l2a_prow", [1, TT], F32)
        tp = TransPool(p, "l2a_tp")
        ln = LNState(p, "l2a_ln", pool)
        wgate = p.sb("l2a_wg", [128, KC * KC * 128], BF16)
        emit_load_wconv(p, cw_gate, wgate, KC, range(KC), "Q")
        wgv = wconv_view(wgate, KC)
        wl_f = p.sb("l2a_wlf", [16, 1024], F32)
        wl_b = p.sb("l2a_wlb", [16, 1024], BF16)
        wlamT = p.sb("l2a_wlT", [128, KC * N], BF16)
        wbT = p.sb("l2a_wbT", [128, KC * N], BF16)
        wc_f = p.sb("l2a_wcf", [128, KC * N], F32)
        wc_b = p.sb("l2a_wcb", [128, KC * N], BF16)
        wcT = p.sb("l2a_wcT", [16, KC * 128], BF16)
        bcol = p.sb("l2a_bcol", [16, 1], F32)
        rmsw = p.sb("l2a_rmsw", [128, KC], F32)
        lv = p.inc(p.sy.dma_start(bcol[:],
                                  w["b_lambda"].rearrange("(p o) -> p o", o=1)),
                   "L")
        lv = p.inc(p.sy.dma_start(rmsw[:],
                                  w["rms_weight"].rearrange("(c p) -> p c",
                                                            p=128),
                                  allow_slow_non_contiguous=True), "L")
        for (wf, wname) in ((wl_f, "W_lambda"),):
            pass
        lv = p.inc(p.sy.dma_start(wl_f[:], w["W_lambda"]), "L")
        lv = p.inc(p.sy.dma_start(
            wc_f[:].rearrange("p (c n) -> p c n", n=N),
            w["W_C"].rearrange("(c p) n -> p c n", p=128)), "L")
        p.wait(p.v, "L", lv)
        vv = p.inc(p.v.tensor_copy(wl_b[:], wl_f[:]), "V")
        vv = p.inc(p.v.tensor_copy(wc_b[:], wc_f[:]), "V")
        p.wait(p.t, "V", vv)
        tprev = []
        for kc in range(KC):
            tp.step(wl_b[:, kc * 128:(kc + 1) * 128], c.ident_b[0:16, 0:16],
                    wlamT[:, kc * N:(kc + 1) * N], out_rows=128, out_cols=16)
        p.wait(p.sy, "V", p.val("V"))
        lv = p.inc(p.sy.dma_start(wl_f[:], w["W_B"]), "L")
        p.wait(p.v, "L", lv)
        p.wait(p.v, "T", p.val("T"))
        vv = p.inc(p.v.tensor_copy(wl_b[:], wl_f[:]), "V")
        p.wait(p.t, "V", vv)
        for kc in range(KC):
            tp.step(wl_b[:, kc * 128:(kc + 1) * 128], c.ident_b[0:16, 0:16],
                    wbT[:, kc * N:(kc + 1) * N], out_rows=128, out_cols=16)
        for kc in range(KC):
            tp.step(wc_b[:, kc * N:(kc + 1) * N], c.ident_b[:],
                    wcT[:, kc * 128:(kc + 1) * 128], out_rows=16,
                    out_cols=128)
        p.wait(p.t, "V", p.val("V"))

        lnfull = p.sb("l2a_lnfull", [128, KC * TOK], BF16)
        lam = p.sb("l2a_lam", [16, TOK], F32)
        bx = p.sb("l2a_bx", [16, TOK], F32)
        hsb = p.sb("l2a_hsb", [16, TOK], BF16)
        gateb = p.sb("l2a_gate", [128, KC * TT], BF16)
        ygb = p.sb("l2a_yg", [128, KC * TT], F32)
        res = ResidualState(p, "l2a")
        lnv = chunks_view(lnfull, KC, TOK)
        gv_ = chunks_view(gateb, KC, TT)
        ygv = chunks_view(ygb, KC, TT)

        for tt in range(NTT):
            lnt = emit_ln_tile(p, c, ln, xin_ap, tt,
                               lambda kc, _t=tt: lnv(kc, _t * TT, TT))
            p.wait(p.t, "V", lnt)
            for (wT, dst, act) in ((wlamT, lam, "sig"), (wbT, bx, "copy")):
                ps_ap, slot = pool.acquire()
                rec = None
                for kc in range(KC):
                    rec = p.t.matmul(ps_ap[0:16, :], wT[:, kc * N:(kc + 1) * N],
                                     lnv(kc, tt * TT, TT), start=(kc == 0),
                                     stop=(kc == KC - 1))
                tvv = p.inc(rec, "T")
                p.wait(p.sc, "T", tvv)
                if act == "sig":
                    sv = p.inc(p.sc.activation(dst[:, tt * TT:(tt + 1) * TT],
                                               ps_ap[0:16, :], AF.Sigmoid,
                                               bias=bcol[:]), "S")
                else:
                    sv = p.inc(p.sc.activation(dst[:, tt * TT:(tt + 1) * TT],
                                               ps_ap[0:16, :], AF.Copy), "S")
                pool.release(slot, "S", sv)
        p.wait(p.v, "S", p.val("S"))
        for b in range(2):
            sl = slice(b * SEQ, (b + 1) * SEQ)
            vv = p.inc(p.v.tensor_tensor_scan(hsb[:, sl], lam[:, sl],
                                              bx[:, sl], 0.0, ALU.mult,
                                              ALU.add), "V")
        hs_done = vv
        p.wait(p.t, "V", hs_done)
        prow_guard = [0]
        for tt in range(NTT):
            def epi_g(fc, _tt, ps_ap, slot, tval):
                p.wait(p.sc, "T", tval)
                sv = p.inc(p.sc.activation(gv_(fc), ps_ap[:], AF.Sigmoid), "S")
                pool.release(slot, "S", sv)
            emit_proj(p, wgv, KC, lambda kc, _t=tt: lnv(kc, _t * TT, TT), D,
                      epi_g, pool, tts=[tt])
            p.wait(p.t, "S", p.val("S"))
            for fc in range(KC):
                ps_ap, slot = pool.acquire()
                tvv = p.inc(p.t.matmul(ps_ap[:],
                                       wcT[:, fc * 128:(fc + 1) * 128],
                                       hsb[:, tt * TT:(tt + 1) * TT],
                                       start=True, stop=True), "T")
                p.wait(p.v, "T", tvv)
                vv = p.inc(p.v.tensor_tensor(ygv(fc), ps_ap[:], gv_(fc),
                                             ALU.mult), "V")
                pool.release(slot, "V", vv)
            p.wait(p.sc, "V", p.val("V"))
            tq = {}
            for kc in range(KC):
                sqv = ln.sq[:, (kc % 2) * TT:(kc % 2) * TT + TT]
                if kc >= 2:
                    p.wait(p.sc, "T", tq[kc - 2])
                sk = p.inc(p.sc.activation(sqv, ygv(kc), AF.Square), "S")
                p.wait(p.t, "S", sk)
                if kc == 0 and prow_guard[0]:
                    p.wait(p.t, "V", prow_guard[0])
                tq[kc] = p.inc(p.t.matmul(prow[:], c.ones_col_f[:], sqv,
                                          start=(kc == 0), stop=(kc == KC - 1)),
                               "T")
            tsum = p.val("T")
            p.wait(p.v, "T", tsum)
            p.v.tensor_scalar_mul(ln.row(0), prow[:], 1.0 / D)
            vv = p.inc(p.v.tensor_copy(ln.row(1), ln.row(0)), "V")
            prow_guard[0] = vv
            p.wait(p.sc, "V", vv)
            sv = p.inc(p.sc.activation(ln.row(2), ln.row(1), AF.Sqrt,
                                       bias=c.one(1e-6)), "S")
            p.wait(p.v, "S", sv)
            vv = p.inc(p.v.reciprocal(ln.row(3), ln.row(2)), "V")
            p.wait(p.t, "V", vv)
            bcp, bslot = pool.acquire()
            tvv = p.inc(p.t.matmul(bcp[:], c.ones_row_f[:], ln.row(3),
                                   start=True, stop=True), "T")
            p.wait(p.v, "T", tvv)
            for kc in range(KC):
                slot = res.i % res.n
                res.i += 1
                buf = res.buf[slot]
                if res.guard[slot] is not None:
                    p.wait(p.sy, "O", res.guard[slot])
                lv = p.inc(p.sy.dma_start(buf[:, 0:TT],
                                          feat_dram_view(xin_ap, kc, tt * TT,
                                                         TT)), "L")
                p.wait(p.v, "L", lv)
                p.v.tensor_tensor(buf[:, 2 * TT:3 * TT], ygv(kc), bcp[:],
                                  ALU.mult)
                vv = p.inc(p.v.scalar_tensor_tensor(buf[:, TT:2 * TT],
                                                    buf[:, 2 * TT:3 * TT],
                                                    rmsw[:, kc:kc + 1],
                                                    buf[:, 0:TT], ALU.mult,
                                                    ALU.add), "V")
                p.wait(p.sy, "V", vv)
                ov = p.inc(p.sy.dma_start(feat_dram_view(xout_ap, kc, tt * TT,
                                                         TT),
                                          buf[:, TT:2 * TT]), "O")
                res.guard[slot] = ov
            pool.release(bslot, "V", p.val("V"))
            p.wait(p.sc, "T", p.val("T"))
            p.wait(p.v, "T", p.val("T"))


def emit_l2_ffn(p, c, xin_ap, xout_ap, w, cw1, cw2):
    NE, FF = 8, 4096
    NFF = FF // 128
    nc = p.nc
    lnhbm = nc.dram_tensor(p.name_prefix + "lnhbm", [D, TOK], BF16,
                           kind="Internal").ap()
    acc = nc.dram_tensor(p.name_prefix + "acc", [D, TOK], F32,
                         kind="Internal").ap()
    _dbg.update({"acc": acc, "lnhbm": lnhbm})
    with p.scope():
        wrow = p.sb("wrow", [NE, TOK], F32)
        _dbg["wrow"] = wrow
        b2sb = p.sb("b2sb", [NE, D], F32)
        b1sb = p.sb("b1sb", [128, NE * NFF], F32)
        lv = p.inc(p.sy.dma_start(b2sb[:], w["b2"]), "L")
        lv = p.inc(p.sy.dma_start(
            b1sb[:].rearrange("p (n c) -> p n c", n=NE),
            w["b1"].rearrange("n (c p) -> p n c", p=128),
            allow_slow_non_contiguous=True), "L")

        with p.scope():
            pool = PsumPool(p, "ps1", 4, [128, TT], F32)
            ln = LNState(p, "ln", pool)
            lnb = p.sb("lnb", [128, KC * TT], BF16)
            gw_f = p.sb("gwf", [NE, D], F32)
            gw_b = p.sb("gwb", [NE, D], BF16)
            gwT = p.sb("gwT", [128, KC * NE], BF16)
            tpg = TransPool(p, "tpg")
            grow = p.sb("grow", [128, 4 * NE], F32)
            gcol = p.sb("gcol", [128, 4], F32)
            lv2 = p.inc(p.sy.dma_start(gw_f[:], w["gate_w"]), "L")
            p.wait(p.v, "L", lv2)
            vv = p.inc(p.v.tensor_copy(gw_b[:], gw_f[:]), "V")
            p.wait(p.t, "V", vv)
            for kc in range(KC):
                tpg.step(gw_b[:, kc * 128:(kc + 1) * 128],
                         c.ident_b[0:NE, 0:NE],
                         gwT[:, kc * NE:(kc + 1) * NE], out_rows=128,
                         out_cols=NE)
            p.wait(p.t, "V", p.val("V"))
            lnv = chunks_view(lnb, KC, TT)
            P_, GE = grow[:, 0:NE], grow[:, NE:2 * NE]
            MK, WW = grow[:, 2 * NE:3 * NE], grow[:, 3 * NE:4 * NE]
            for tt in range(NTT):
                lnt = emit_ln_tile(p, c, ln, xin_ap, tt, lambda kc: lnv(kc))
                p.wait(p.sy, "V", lnt)
                dst = lnhbm.rearrange("(c p) t -> p c t",
                                      p=128)[:, :, tt * TT:(tt + 1) * TT]
                p.inc(p.sy.dma_start(dst,
                                     lnb[:].rearrange("p (c t) -> p c t",
                                                      c=KC)), "O")
                p.wait(p.t, "V", lnt)
                for tc in range(TT // 128):
                    gtc = tt * 4 + tc
                    ps_ap, slot = pool.acquire()
                    rec = None
                    for kc in range(KC):
                        rec = p.t.matmul(ps_ap[:, 0:NE], lnv(kc, tc * 128, 128),
                                         gwT[:, kc * NE:(kc + 1) * NE],
                                         start=(kc == 0), stop=(kc == KC - 1))
                    tvv = p.inc(rec, "T")
                    p.wait(p.v, "T", tvv)
                    vv = p.inc(p.v.reduce_max(gcol[:, 0:1], ps_ap[:, 0:NE],
                                              axis=AX.X, negate=True), "V")
                    p.wait(p.sc, "V", vv)
                    sv = p.inc(p.sc.activation(P_, ps_ap[:, 0:NE], AF.Exp,
                                               bias=gcol[:, 0:1]), "S")
                    pool.release(slot, "S", sv)
                    p.wait(p.v, "S", sv)
                    p.v.reduce_sum(gcol[:, 1:2], P_, axis=AX.X)
                    p.v.reciprocal(gcol[:, 2:3], gcol[:, 1:2])
                    p.v.tensor_scalar_mul(P_, P_, gcol[:, 2:3])
                    p.v.reduce_max(gcol[:, 0:1], P_, axis=AX.X)
                    p.v.tensor_scalar(GE, P_, gcol[:, 0:1], None, ALU.is_equal)
                    p.v.scalar_tensor_tensor(MK, GE, -2.0, P_, ALU.mult,
                                             ALU.add)
                    p.v.reduce_max(gcol[:, 0:1], MK, axis=AX.X)
                    p.v.tensor_scalar(GE, P_, gcol[:, 0:1], None, ALU.is_ge)
                    p.v.tensor_tensor(WW, P_, GE, ALU.mult)
                    p.v.reduce_sum(gcol[:, 1:2], WW, axis=AX.X)
                    p.v.tensor_scalar(gcol[:, 1:2], gcol[:, 1:2], 1e-9, None,
                                      ALU.add)
                    p.v.reciprocal(gcol[:, 2:3], gcol[:, 1:2])
                    vv = p.inc(p.v.tensor_scalar_mul(WW, WW, gcol[:, 2:3]),
                               "V")
                    p.wait(p.t, "V", vv)
                    ps2, slot2 = pool.acquire()
                    tvv = p.inc(p.t.transpose(ps2[0:NE, 0:128], WW,
                                              c.ident_f[:]), "T")
                    p.wait(p.v, "T", tvv)
                    vv = p.inc(p.v.tensor_copy(
                        wrow[:, gtc * 128:(gtc + 1) * 128],
                        ps2[0:NE, 0:128]), "V")
                    pool.release(slot2, "V", vv)
                    p.wait(p.t, "V", vv)
                p.wait(p.v, "T", p.val("T"))

        with p.scope():
            pool = PsumPool(p, "ps2", 7, [128, TT], F32)
            ebc = p.sb("ebc", [NE, NE * 128], F32)
            ebc_np = np.zeros((NE, NE * 128), np.float32)
            for n_ in range(NE):
                ebc_np[n_, n_ * 128:(n_ + 1) * 128] = 1.0
            h_ = p.nc.inline_tensor(ebc_np, p.name_prefix + "ebc_c")
            p.wait(p.sy, "V", p.val("V"))
            ebv = p.inc(p.sy.dma_start(ebc[:], h_.ap()), "L")
            p.wait(p.t, "L", ebv)
            w1sb = p.sb("w1sb", [128, KC * FF], BF16)
            w2sb = p.sb("w2sb", [128, NFF * D], BF16)
            lnt2 = [p.sb("lnt0", [128, KC * TT], BF16)]
            hbuf = p.sb("hbuf", [128, NFF * TT], BF16)
            wnb = p.sb("wnb", [128, TT], F32)
            outsb = [p.sb(f"outsb{i}", [128, TT], F32) for i in range(4)]
            w1v, w2v = w_view(w1sb, FF), w_view(w2sb, D)
            hv = chunks_view(hbuf, NFF, TT)
            ln_guard = [0, 0]
            out_guard = [0, 0, 0, 0]
            acc_guard = {}
            w1_guard = 0
            w2_guard = 0
            for n in range(NE):
                if w1_guard:
                    p.wait(p.sc, "T", w1_guard)
                qv1 = 0
                for kc in range(KC):
                    qv1 = p.inc(p.sc.dma_start(
                        w1sb[:, kc * FF:(kc + 1) * FF],
                        cw1[n * 1024 + kc * 128:n * 1024 + (kc + 1) * 128, :]),
                        "Q")
                if w2_guard:
                    p.wait(p.sc, "T", w2_guard)
                qv = 0
                for kc in range(NFF):
                    qv = p.inc(p.sc.dma_start(
                        w2sb[:, kc * D:(kc + 1) * D],
                        cw2[n * FF + kc * 128:n * FF + (kc + 1) * 128, :]),
                        "Q")
                p.wait(p.t, "Q", qv1)
                first_tt = [True]
                for tt in range(NTT):
                    par = 0
                    if ln_guard[par]:
                        p.wait(p.sy, "T", ln_guard[par])
                    src = lnhbm.rearrange("(c p) t -> p c t",
                                          p=128)[:, :, tt * TT:(tt + 1) * TT]
                    lv = p.inc(p.sy.dma_start(
                        lnt2[par][:].rearrange("p (c t) -> p c t", c=KC), src),
                        "L")
                    lv2 = chunks_view(lnt2[par], KC, TT)
                    ps_b, sl_b = pool.acquire()
                    tvv = p.inc(p.t.matmul(ps_b[:],
                                           ebc[:, n * 128:(n + 1) * 128],
                                           wrow[:, tt * TT:(tt + 1) * TT],
                                           start=True, stop=True), "T")
                    p.wait(p.v, "T", tvv)
                    vv = p.inc(p.v.tensor_copy(wnb[:], ps_b[:]), "V")
                    pool.release(sl_b, "V", vv)
                    p.wait(p.t, "L", lv)
                    for fc in range(NFF):
                        psh, slh = pool.acquire()
                        rec = None
                        for kc in range(KC):
                            rec = p.t.matmul(psh[:], w1v(kc, fc), lv2(kc),
                                             start=(kc == 0),
                                             stop=(kc == KC - 1))
                        tvv = p.inc(rec, "T")
                        p.wait(p.sc, "T", tvv)
                        sv = p.inc(p.sc.activation(
                            hv(fc), psh[:], AF.Gelu,
                            bias=b1sb[:, n * NFF + fc:n * NFF + fc + 1]), "S")
                        pool.release(slh, "S", sv)
                    ln_guard[par] = p.val("T")
                    w1_guard = p.val("T")
                    p.wait(p.t, "S", p.val("S"))
                    if first_tt[0]:
                        p.wait(p.t, "Q", qv)
                        first_tt[0] = False
                    for fc in range(KC):
                        pso, slo = pool.acquire()
                        rec = None
                        for kc in range(NFF):
                            rec = p.t.matmul(pso[:], w2v(kc, fc), hv(kc),
                                             start=(kc == 0),
                                             stop=(kc == NFF - 1))
                        tvv = p.inc(rec, "T")
                        osl = fc % 4
                        if out_guard[osl]:
                            p.wait(p.v, "G", out_guard[osl])
                        p.wait(p.v, "T", tvv)
                        vv = p.inc(p.v.tensor_tensor(outsb[osl][:], pso[:],
                                                     wnb[:], ALU.mult), "V")
                        pool.release(slo, "V", vv)
                        p.wait(p.g, "V", vv)
                        key = (fc, tt)
                        if key in acc_guard:
                            p.wait(p.g, "G", acc_guard[key])
                        gv = p.inc(p.g.dma_start(
                            feat_dram_view(acc, fc, tt * TT, TT),
                            outsb[osl][:],
                            accum_op=(ALU.add if n > 0 else ALU.bypass)), "G")
                        out_guard[osl] = gv
                        acc_guard[key] = gv
                    w2_guard = p.val("T")
            final_g = p.val("G")

        with p.scope():
            pool3 = PsumPool(p, "ps3", 2, [128, TT], F32)
            res = ResidualState(p, "res")
            accb = [p.sb(f"accb{i}", [128, TT], F32) for i in range(2)]
            acc_g = [0, 0]
            p.wait(p.sy, "G", final_g)
            p.wait(p.t, "L", p.val("L"))
            for tt in range(NTT):
                for fc in range(KC):
                    par = (tt * KC + fc) % 2
                    if acc_g[par]:
                        p.wait(p.sy, "V", acc_g[par])
                    lv = p.inc(p.sy.dma_start(accb[par][:],
                                              feat_dram_view(acc, fc, tt * TT,
                                                             TT)), "L")
                    ps_ap, slot = pool3.acquire()
                    tvv = p.inc(p.t.matmul(
                        ps_ap[:], b2sb[:, fc * 128:(fc + 1) * 128],
                        wrow[:, tt * TT:(tt + 1) * TT], start=True, stop=True),
                        "T")
                    slot_r = res.i % res.n
                    res.i += 1
                    buf = res.buf[slot_r]
                    if res.guard[slot_r] is not None:
                        p.wait(p.sy, "O", res.guard[slot_r])
                    lv = p.inc(p.sy.dma_start(
                        buf[:, 0:TT], feat_dram_view(xin_ap, fc, tt * TT, TT)),
                        "L")
                    p.wait(p.v, "L", lv)
                    p.wait(p.v, "T", tvv)
                    p.v.tensor_tensor(buf[:, 2 * TT:3 * TT], accb[par][:],
                                      ps_ap[:], ALU.add)
                    vv = p.inc(p.v.scalar_tensor_tensor(
                        buf[:, TT:2 * TT], buf[:, 2 * TT:3 * TT], 0.5,
                        buf[:, 0:TT], ALU.mult, ALU.add), "V")
                    pool3.release(slot, "V", vv)
                    acc_g[par] = vv
                    p.wait(p.sy, "V", vv)
                    ov = p.inc(p.sy.dma_start(
                        feat_dram_view(xout_ap, fc, tt * TT, TT),
                        buf[:, TT:2 * TT]), "O")
                    res.guard[slot_r] = ov


def emit_l0_attn(p, c, xin_ap, xout_ap, w, prep_only=False):
    CH = 256
    NC2 = 2
    SEQ = 2048
    NKT = SEQ // 128
    NQT = SEQ // TT
    with p.scope():
        pool = PsumPool(p, "l0a_ps", 3, [128, TT], F32)
        ln = LNState(p, "l0a_ln", pool)
        qw = p.sb("l0a_qw", [128, KC * CH], BF16)
        kw = p.sb("l0a_kw", [128, KC * CH], BF16)
        vw = p.sb("l0a_vw", [128, KC * CH], BF16)
        owT = p.sb("l0a_owT", [128, NC2 * D], BF16)
        smalls = p.sb("l0a_small", [1, 4], F32)
        lcol = p.sb("l0a_lcol", [128, 1], F32)
        oscol = p.sb("l0a_oscol", [128, 1], F32)
        prep = p.scope() if not prep_only else None
        if prep is not None:
            prep.__enter__()
        tp = TransPool(p, "l0a_tp")
        bsb = p.sb("l0a_bsb", [128, KC * CH], F32)
        bsb_b = p.sb("l0a_bsbb", [128, KC * CH], BF16)
        basisT = p.sb("l0a_bT", [128, NC2 * D], BF16)
        coefT = p.sb("l0a_cT", [128, 4 * NC2 * CH], BF16)
        cf = p.sb("l0a_cf", [128, NC2 * CH], F32)
        cfb = p.sb("l0a_cfb", [128, NC2 * CH], BF16)
        lv = p.inc(p.sy.dma_start(
            bsb[:].rearrange("p (c n) -> p c n", n=CH),
            w["basis"].rearrange("(c p) n -> p c n", p=128)), "L")
        p.wait(p.v, "L", lv)
        vv = p.inc(p.v.tensor_copy(bsb_b[:], bsb[:]), "V")
        p.wait(p.t, "V", vv)
        for kc in range(KC):
            for cc in range(NC2):
                tp.step(bsb_b[:, kc * CH + cc * 128:kc * CH + (cc + 1) * 128],
                        c.ident_b[:],
                        basisT[:, cc * D + kc * 128:cc * D + (kc + 1) * 128])
        for i, nm in enumerate(("q_coeffs", "k_coeffs", "v_coeffs", "o_coeffs")):
            p.wait(p.sy, "V", p.val("V"))
            lv = p.inc(p.sy.dma_start(
                cf[:].rearrange("p (c n) -> p c n", n=CH),
                w[nm].rearrange("(c p) n -> p c n", p=128)), "L")
            p.wait(p.v, "L", lv)
            vv = p.inc(p.v.tensor_copy(cfb[:], cf[:]), "V")
            p.wait(p.t, "V", vv)
            for kc in range(NC2):
                for cc in range(NC2):
                    tp.step(cfb[:, kc * CH + cc * 128:kc * CH + (cc + 1) * 128],
                            c.ident_b[:],
                            coefT[:, i * NC2 * CH + cc * CH + kc * 128:
                                  i * NC2 * CH + cc * CH + (kc + 1) * 128])
        p.wait(p.t, "V", p.val("V"))
        for i, dstw in enumerate((qw, kw, vw)):
            for vc in range(KC):
                ps_ap, slot = pool.acquire()
                rec = None
                for bc in range(NC2):
                    rec = p.t.matmul(
                        ps_ap[:, 0:CH],
                        basisT[:, bc * D + vc * 128:bc * D + (vc + 1) * 128],
                        coefT[:, i * NC2 * CH + bc * CH:(i * NC2 + bc + 1) * CH],
                        start=(bc == 0), stop=(bc == NC2 - 1))
                tvv = p.inc(rec, "T")
                p.wait(p.v, "T", tvv)
                vv = p.inc(p.v.tensor_copy(dstw[:, vc * CH:(vc + 1) * CH],
                                           ps_ap[:, 0:CH]), "V")
                pool.release(slot, "V", vv)
        for ccO in range(NC2):
            for vs in range(2):
                ps_ap, slot = pool.acquire()
                rec = None
                for bc in range(NC2):
                    rec = p.t.matmul(
                        ps_ap[:],
                        coefT[:, 3 * NC2 * CH + bc * CH + ccO * 128:
                              3 * NC2 * CH + bc * CH + (ccO + 1) * 128],
                        basisT[:, bc * D + vs * TT:bc * D + (vs + 1) * TT],
                        start=(bc == 0), stop=(bc == NC2 - 1))
                tvv = p.inc(rec, "T")
                p.wait(p.v, "T", tvv)
                vv = p.inc(p.v.tensor_copy(
                    owT[:, ccO * D + vs * TT:ccO * D + (vs + 1) * TT],
                    ps_ap[:]), "V")
                pool.release(slot, "V", vv)
        p.wait(p.sy, "V", p.val("V"))
        p.wait(p.sy, "S", p.val("S"))
        lv = p.inc(p.sy.dma_start(smalls[:, 0:1], w["decay_logit"][0:1]), "L")
        lv = p.inc(p.sy.dma_start(smalls[:, 1:2], w["out_scale"][0:1]), "L")
        p.wait(p.sc, "L", lv)
        sv = p.inc(p.sc.activation(smalls[:, 2:3], smalls[:, 0:1], AF.Sigmoid),
                   "S")
        p.wait(p.v, "S", sv)
        vv = p.inc(p.v.tensor_scalar_max(smalls[:, 2:3], smalls[:, 2:3],
                                         1e-38), "V")
        p.wait(p.sc, "V", vv)
        sv = p.inc(p.sc.activation(smalls[:, 2:3], smalls[:, 2:3], AF.Ln), "S")
        p.wait(p.t, "S", sv)
        ps_ap, slot = pool.acquire()
        p.t.matmul(ps_ap[:, 0:1], c.ones_row_f[:], smalls[:, 2:3], start=True,
                   stop=True)
        tvv = p.inc(p.t.matmul(ps_ap[:, 1:2], c.ones_row_f[:], smalls[:, 1:2],
                               start=False, stop=True,
                               skip_group_check=True), "T")
        p.wait(p.v, "T", tvv)
        p.v.tensor_copy(lcol[:], ps_ap[:, 0:1])
        vv = p.inc(p.v.tensor_copy(oscol[:], ps_ap[:, 1:2]), "V")
        pool.release(slot, "V", vv)
        if prep_only:
            _dbg.update({"coefT": coefT, "basisT": basisT, "owT": owT})
            return
        prep.__exit__(None, None, None)

        lnfull = p.sb("l0a_lnfull", [128, KC * TOK], BF16)
        lnv = chunks_view(lnfull, KC, TOK)
        for tt in range(NTT):
            emit_ln_tile(p, c, ln, xin_ap, tt,
                         lambda kc, _t=tt: lnv(kc, _t * TT, TT))
        qb = p.sb("l0a_qb", [128, NC2 * TOK], BF16)
        kb = p.sb("l0a_kb", [128, NC2 * TOK], BF16)
        vb = p.sb("l0a_vb", [128, NC2 * TOK], BF16)
        vtok = p.sb("l0a_vtok", [128, (TOK // 128) * CH], BF16)
        _dbg.update({"qw": qw, "kw": kw, "vw": vw, "owT": owT, "qb": qb,
                     "kb": kb, "vtok": vtok, "lcol": lcol, "oscol": oscol})
        p.wait(p.t, "V", p.val("V"))
        for (wsb, dst) in ((qw, qb), (kw, kb), (vw, vb)):
            wv_ = w_view(wsb, CH)

            def epi(fc, tt_, ps_ap2, slot2, tval, _dst=dst):
                p.wait(p.sc, "T", tval)
                sv2 = p.inc(p.sc.activation(
                    _dst[:, fc * TOK + tt_ * TT:fc * TOK + (tt_ + 1) * TT],
                    ps_ap2[:], AF.Copy), "S")
                pool.release(slot2, "S", sv2)
            emit_proj(p, wv_, KC, lambda kc, _t: lnv(kc, _t * TT, TT), CH,
                      epi, pool)
        p.wait(p.t, "S", p.val("S"))
        with p.scope():
            tp2 = TransPool(p, "l0a_tp2")
            for t128 in range(TOK // 128):
                for cc in range(NC2):
                    tp2.step(
                        vb[:, cc * TOK + t128 * 128:cc * TOK + (t128 + 1) * 128],
                        c.ident_b[:],
                        vtok[:, t128 * CH + cc * 128:t128 * CH + (cc + 1) * 128])
            p.wait(p.t, "V", p.val("V"))
        spool = PsumPool(p, "l0a_ss", 2, [128, TT], F32)

        wdt = p.sb("l0a_wdt", [128, TT], F32)
        msk = p.sb("l0a_msk", [128, TT], F32)
        wex = p.sb("l0a_wex", [128, TT], F32)
        pbuf = [p.sb(f"l0a_pb{i}", [128, TT], BF16) for i in range(2)]
        rb = p.sb("l0a_rb", [128, NC2 * TT], BF16)
        res = ResidualState(p, "l0a")
        owv = w_view(owT, D)
        wexm = p.sb("l0a_wexm", [128, TT], F32)
        pb_guard = [0, 0]
        for b in range(2):
            for qt in range(NQT):
                qt0 = qt * TT
                glob_q = b * SEQ + qt0
                pso = [None, None]
                slo = [None, None]
                for cc in range(NC2):
                    pso[cc], slo[cc] = pool.acquire()
                kt_start = qt0 // 128
                for kt in range(kt_start, NKT):
                    kt0 = kt * 128
                    pss, sls = spool.acquire()
                    rec = None
                    for ccq in range(NC2):
                        rec = p.t.matmul(
                            pss[:], kb[:, ccq * TOK + b * SEQ + kt0:
                                        ccq * TOK + b * SEQ + kt0 + 128],
                            qb[:, ccq * TOK + glob_q:ccq * TOK + glob_q + TT],
                            start=(ccq == 0), stop=(ccq == NC2 - 1))
                    tvv = p.inc(rec, "T")
                    c0 = float(kt0 - qt0 - 1)
                    p.v.tensor_scalar(wdt[:], c.iota_pf[:], c0, 0.0, ALU.add,
                                      ALU.max)
                    vv = p.inc(p.v.tensor_single_scalar(msk[:], c.iota_pf[:],
                                                        -c0, ALU.is_ge), "V")
                    p.wait(p.sc, "V", vv)
                    sv = p.inc(p.sc.activation(wex[:], wdt[:], AF.Exp,
                                               scale=lcol[:]), "S")
                    par = kt % 2
                    if pb_guard[par]:
                        p.wait(p.v, "T", pb_guard[par])
                    p.wait(p.v, "S", sv)
                    p.wait(p.v, "T", tvv)
                    p.v.tensor_tensor(wexm[:], wex[:], msk[:], ALU.mult)
                    vv = p.inc(p.v.tensor_tensor(pbuf[par][:], pss[:], wexm[:],
                                                 ALU.mult), "V")
                    spool.release(sls, "V", vv)
                    p.wait(p.t, "V", vv)
                    for cc in range(NC2):
                        rec = p.t.matmul(
                            pso[cc][:],
                            vtok[:, (b * NKT + kt) * CH + cc * 128:
                                 (b * NKT + kt) * CH + (cc + 1) * 128],
                            pbuf[par][:], start=(kt == kt_start),
                            stop=(kt == NKT - 1))
                    pb_guard[par] = p.inc(rec, "T")
                for cc in range(NC2):
                    p.wait(p.sc, "T", p.val("T"))
                    sv = p.inc(p.sc.activation(rb[:, cc * TT:(cc + 1) * TT],
                                               pso[cc][:], AF.Copy), "S")
                    pool.release(slo[cc], "S", sv)
                p.wait(p.t, "S", p.val("S"))

                def epi_o(fc, _tt, ps_ap2, slot2, tval):
                    vv2 = emit_residual_store(p, c, res, xin_ap, xout_ap, fc,
                                              _tt, ps_ap2[:], 1.0, tval,
                                              scale_col=oscol[:])
                    pool.release(slot2, "V", vv2)
                emit_proj(p, owv, NC2, lambda kc, _t: rb[:, kc * TT:(kc + 1) * TT],
                          D, epi_o, pool, tts=[b * NQT + qt])
                p.wait(p.v, "T", p.val("T"))


def emit_l1_attn(p, c, xin_ap, xout_ap, w, cw_qkv, cw_proj):
    H, DH = 16, 64
    SEQ = 2048
    NKT = SEQ // 128
    NQT4 = SEQ // TT
    nc = p.nc
    mrow_hbm = nc.dram_tensor(p.name_prefix + "mrow", [16, 16 * 128], BF16,
                              kind="Internal").ap()
    drow_hbm = nc.dram_tensor(p.name_prefix + "drow", [16, 16 * 128], F32,
                              kind="Internal").ap()
    with p.scope():
        pool = PsumPool(p, "l1a_ps", 3, [128, TT], F32)
        wpb = [p.sb(f"l1a_wpb{i}", [128, KC * 128], BF16) for i in range(2)]
        wp_guard = [0, 0]
        tcol = p.sb("l1a_tcol", [16, 1], F32)
        hfcol = p.sb("l1a_hf", [128, KC], F32)
        projb = p.sb("l1a_projb", [128, KC], F32)
        with p.scope():
            hsel_np = np.zeros((16, 1024), np.float32)
            for h_ in range(16):
                hsel_np[h_, h_ * 64:(h_ + 1) * 64] = 1.0
            hsel_t = nc.inline_tensor(hsel_np, p.name_prefix + "hsel")
            hsel = p.sb("l1a_hsel", [16, 1024], F32)
            lv = p.inc(p.sy.dma_start(hsel[:], hsel_t.ap()), "L")
            lv = p.inc(p.sy.dma_start(tcol[:],
                                      w["temperature"].rearrange(
                                          "(p o) -> p o", o=1)), "L")
            lv = p.inc(p.sy.dma_start(projb[:],
                                      w["proj_b"].rearrange("(c p) -> p c",
                                                            p=128),
                                      allow_slow_non_contiguous=True), "L")
            p.wait(p.v, "L", lv)
            p.v.tensor_scalar_mul(tcol[:], tcol[:], 64.0)
            vv = p.inc(p.v.reciprocal(tcol[:], tcol[:]), "V")
            p.wait(p.t, "V", vv)
            for kc in range(KC):
                ps_ap, slot = pool.acquire()
                tvv = p.inc(p.t.matmul(ps_ap[:, 0:1],
                                       hsel[:, kc * 128:(kc + 1) * 128],
                                       tcol[:], start=True, stop=True), "T")
                p.wait(p.v, "T", tvv)
                vv = p.inc(p.v.tensor_copy(hfcol[:, kc:kc + 1],
                                           ps_ap[:, 0:1]), "V")
                pool.release(slot, "V", vv)
            p.wait(p.sy, "V", p.val("V"))

        qh = p.sb("l1a_qh", [128, KC * SEQ], BF16)
        kh = p.sb("l1a_kh", [128, KC * SEQ], BF16)
        vtok = p.sb("l1a_vtok", [128, NKT * D], BF16)
        dualrow = p.sb("l1a_dual", [1, 2 * TT], F32)
        rn2 = p.sb("l1a_rn2", [1, 2 * TT], F32)
        res = ResidualState(p, "l1a")

        for b in range(2):
            with p.scope():
                lnA = LNState(p, "l1a_ln%d" % b, pool)
                lnb = p.sb("l1a_lnb%d" % b, [128, KC * TT], BF16)
                qraw = p.sb("l1a_qraw%d" % b, [128, KC * TT], BF16)
                kraw = p.sb("l1a_kraw%d" % b, [128, KC * TT], BF16)
                vtmp = p.sb("l1a_vtmp%d" % b, [128, TT], BF16)
                wqb = [p.sb(f"l1a_wq{b}{i}", [128, KC * 128], BF16)
                       for i in range(2)]
                stq, stk = lnA.st1, lnA.st2
                tpv = TransPool(p, "l1a_tpv%d" % b)
                lnv = chunks_view(lnb, KC, TT)
                qrv = chunks_view(qraw, KC, TT)
                krv = chunks_view(kraw, KC, TT)
                stq_guard = [0]
                wq_guard = [0, 0]
                qfc = {}

                def load_wq(fc):
                    sl = fc % 2
                    if wq_guard[sl]:
                        p.wait(p.sc, "T", wq_guard[sl])
                    return emit_load_wconv(p, cw_qkv, wqb[sl], KC, [fc], "Q")
                for ttl in range(NQT4):
                    gt = b * NQT4 + ttl
                    lnt = emit_ln_tile(p, c, lnA, xin_ap, gt,
                                       lambda kc: lnv(kc))
                    p.wait(p.t, "V", lnt)
                    if stq_guard[0]:
                        p.wait(p.t, "V", stq_guard[0])
                    sq_t = {}
                    qfc[0] = load_wq(0)
                    for fc in range(24):
                        if fc + 1 < 24:
                            qfc[fc + 1] = load_wq(fc + 1)
                        wqv_ = wconv_view(wqb[fc % 2], KC)
                        ps_ap, slot = pool.acquire()
                        p.wait(p.t, "Q", qfc[fc])
                        rec = None
                        for kc in range(KC):
                            rec = p.t.matmul(ps_ap[:], wqv_(kc, 0),
                                             lnv(kc), start=(kc == 0),
                                             stop=(kc == KC - 1))
                        tvv = p.inc(rec, "T")
                        wq_guard[fc % 2] = tvv
                        p.wait(p.sc, "T", tvv)
                        if fc < 8:
                            sv = p.inc(p.sc.activation(qrv(fc), ps_ap[:],
                                                       AF.Copy), "S")
                            sl2 = fc % 2
                            if fc >= 2:
                                p.wait(p.sc, "T", sq_t[fc - 2])
                            sv = p.inc(p.sc.activation(
                                lnA.sq[:, sl2 * TT:(sl2 + 1) * TT], ps_ap[:],
                                AF.Square), "S")
                            p.wait(p.t, "S", sv)
                            sq_t[fc] = p.inc(p.t.matmul(
                                stq[:], c.ones_col_f[:],
                                lnA.sq[:, sl2 * TT:(sl2 + 1) * TT],
                                start=(fc == 0), stop=(fc == 7)), "T")
                        elif fc < 16:
                            sv = p.inc(p.sc.activation(krv(fc - 8), ps_ap[:],
                                                       AF.Copy), "S")
                            sl2 = fc % 2
                            p.wait(p.sc, "T", sq_t[fc - 2])
                            sv = p.inc(p.sc.activation(
                                lnA.sq[:, sl2 * TT:(sl2 + 1) * TT], ps_ap[:],
                                AF.Square), "S")
                            p.wait(p.t, "S", sv)
                            sq_t[fc] = p.inc(p.t.matmul(
                                stk[:], c.ones_col_f[:],
                                lnA.sq[:, sl2 * TT:(sl2 + 1) * TT],
                                start=(fc == 8), stop=(fc == 15)), "T")
                        else:
                            sv = p.inc(p.sc.activation(vtmp[:], ps_ap[:],
                                                       AF.Copy), "S")
                            p.wait(p.t, "S", sv)
                            for j4 in range(4):
                                kt128 = ttl * 4 + j4
                                tpv.step(vtmp[:, j4 * 128:(j4 + 1) * 128],
                                         c.ident_b[:],
                                         vtok[:, kt128 * D + (fc - 16) * 128:
                                              kt128 * D + (fc - 15) * 128])
                            p.wait(p.t, "V", p.val("V"))
                        pool.release(slot, "S" if fc < 16 else "V",
                                     p.val("S") if fc < 16 else p.val("V"))
                    p.wait(p.v, "T", p.val("T"))
                    for (st, off) in ((stq, 0), (stk, 1)):
                        dr = dualrow[:, off * TT:(off + 1) * TT]
                        p.v.tensor_copy(dr, st[:])
                        p.wait(p.sc, "V", p.val("V"))
                        p.inc(p.sc.activation(dr, dr, AF.Sqrt), "S")
                        p.wait(p.v, "S", p.val("S"))
                        p.v.tensor_scalar_max(dr, dr, 1e-12)
                        vv = p.inc(p.v.reciprocal(
                            rn2[:, off * TT:(off + 1) * TT], dr), "V")
                    stq_guard[0] = p.val("V")
                    lnA.last_v = max(lnA.last_v, stq_guard[0])
                    p.wait(p.t, "V", p.val("V"))
                    bcq, sq_ = pool.acquire()
                    p.t.matmul(bcq[:], c.ones_row_f[:],
                               rn2[:, 0:TT], start=True, stop=True)
                    bck, sk_ = pool.acquire()
                    tvv = p.inc(p.t.matmul(bck[:], c.ones_row_f[:],
                                           rn2[:, TT:2 * TT],
                                           start=True, stop=True), "T")
                    p.wait(p.v, "T", tvv)
                    for kc in range(KC):
                        p.v.scalar_tensor_tensor(
                            qh[:, kc * SEQ + ttl * TT:kc * SEQ + (ttl + 1) * TT],
                            qrv(kc), hfcol[:, kc:kc + 1], bcq[:], ALU.mult,
                            ALU.mult)
                        vv = p.inc(p.v.tensor_tensor(
                            kh[:, kc * SEQ + ttl * TT:kc * SEQ + (ttl + 1) * TT],
                            krv(kc), bck[:], ALU.mult), "V")
                    pool.release(sq_, "V", vv)
                    pool.release(sk_, "V", vv)
                    p.wait(p.sc, "T", p.val("T"))
                    p.wait(p.v, "T", p.val("T"))

            with p.scope():
                attn_o = p.sb("l1a_ao%d" % b, [128, KC * SEQ], BF16)
                spool = PsumPool(p, "l1a_sp%d" % b, 4, [128, TT], F32)
                mcol = p.sb("l1a_mcol%d" % b, [128, 2 * H * NKT], F32)
                mtmpM = p.sb("l1a_mtmpM%d" % b, [16, H * 128], BF16)
                mtmpD = p.sb("l1a_mtmpD%d" % b, [16, H * 128], F32)
                mred = p.sb("l1a_mred%d" % b, [128, 8], F32)
                mrowb2 = [p.sb(f"l1a_mrb{b}{i}", [1, SEQ], BF16)
                          for i in range(2)]
                drow2 = [p.sb(f"l1a_dr2{b}{i}", [1, SEQ], F32)
                         for i in range(2)]
                pbuf = [p.sb(f"l1a_pb{b}{i}", [128, TT], BF16)
                        for i in range(3)]
                dival = p.sb("l1a_div%d" % b, [128, TT], F32)

                def _slab(src, h_, t0, wd):
                    kc0 = ((h_ * DH) // 128) * SEQ
                    off = (h_ * DH) % 128
                    return src[off:off + DH, kc0 + t0:kc0 + t0 + wd]

                for h_ in range(H):
                    for qc in range(NKT):
                        tiles = []
                        for ktt in range(NQT4):
                            ps_ap, slot = spool.acquire()
                            tvv = p.inc(p.t.matmul(
                                ps_ap[:], _slab(qh, h_, qc * 128, 128),
                                _slab(kh, h_, ktt * TT, TT), start=True,
                                stop=True), "T")
                            tiles.append((ps_ap, slot))
                            p.wait(p.v, "T", tvv)
                            vv = p.inc(p.v.reduce_max(
                                mred[:, ktt:ktt + 1], ps_ap[:], axis=AX.X,
                                negate=True), "V")
                        p.v.tensor_tensor(mred[:, 0:1], mred[:, 0:1],
                                          mred[:, 1:2], ALU.min)
                        p.v.tensor_tensor(mred[:, 2:3], mred[:, 2:3],
                                          mred[:, 3:4], ALU.min)
                        mc = mcol[:, h_ * NKT + qc:h_ * NKT + qc + 1]
                        vv = p.inc(p.v.tensor_tensor(
                            mc, mred[:, 0:1], mred[:, 2:3], ALU.min), "V")
                        p.wait(p.sc, "V", vv)
                        for ktt in range(NQT4):
                            ps_ap, slot = tiles[ktt]
                            sv = p.inc(p.sc.activation(
                                dival[:], ps_ap[:], AF.Exp, bias=mc,
                                accum_out=mred[:, 4 + ktt:5 + ktt]), "S")
                            spool.release(slot, "S", sv)
                        p.wait(p.v, "S", p.val("S"))
                        p.v.tensor_tensor(mred[:, 4:5], mred[:, 4:5],
                                          mred[:, 5:6], ALU.add)
                        p.v.tensor_tensor(mred[:, 6:7], mred[:, 6:7],
                                          mred[:, 7:8], ALU.add)
                        vv = p.inc(p.v.tensor_tensor(
                            mcol[:, H * NKT + h_ * NKT + qc:
                                 H * NKT + h_ * NKT + qc + 1],
                            mred[:, 4:5], mred[:, 6:7], ALU.add), "V")
                p.wait(p.t, "V", p.val("V"))
                for g_ in range(2 * H):
                    ps_ap, slot = spool.acquire()
                    tvv = p.inc(p.t.transpose(
                        ps_ap[0:16, 0:128],
                        mcol[:, g_ * NKT:(g_ + 1) * NKT], c.ident_f[:]), "T")
                    p.wait(p.v, "T", tvv)
                    if g_ < H:
                        dst_ = mtmpM[:, g_ * 128:(g_ + 1) * 128]
                    else:
                        dst_ = mtmpD[:, (g_ - H) * 128:(g_ - H + 1) * 128]
                    vv = p.inc(p.v.tensor_copy(dst_, ps_ap[0:16, 0:128]), "V")
                    spool.release(slot, "V", vv)
                    p.wait(p.t, "V", vv)
                p.wait(p.sy, "V", p.val("V"))
                p.inc(p.sy.dma_start(mrow_hbm, mtmpM[:]), "O")
                ov = p.inc(p.sy.dma_start(drow_hbm, mtmpD[:]), "O")
                p.wait(p.sy, "O", ov)
                mr_guard = [0, 0]
                mr_ready = {}

                def fetch_mrow(h2):
                    sl = h2 % 2
                    if mr_guard[sl]:
                        p.wait(p.sy, "T", mr_guard[sl])
                    srcm = bass.AP(mrow_hbm.tensor, h2 * 128,
                                   [[1, 1], [H * 128, NKT], [1, 128]])
                    p.inc(p.sy.dma_start(
                        mrowb2[sl][:].rearrange("o (a b) -> o a b", b=128),
                        srcm), "L")
                    srcd = bass.AP(drow_hbm.tensor, h2 * 128,
                                   [[1, 1], [H * 128, NKT], [1, 128]])
                    p.inc(p.sy.dma_start(
                        drow2[sl][:].rearrange("o (a b) -> o a b", b=128),
                        srcd), "L")
                    lv2 = p.val("L")
                    p.wait(p.v, "L", lv2)
                    return p.inc(p.v.reciprocal(drow2[sl][:], drow2[sl][:]),
                                 "V")

                mr_ready[0] = fetch_mrow(0)
                for h_ in range(H):
                    kc0 = (h_ * DH) // 128
                    off = (h_ * DH) % 128
                    qslab = lambda t0, wd: qh[off:off + DH,
                                              kc0 * SEQ + t0:kc0 * SEQ + t0 + wd]
                    kslab = lambda t0, wd: kh[off:off + DH,
                                              kc0 * SEQ + t0:kc0 * SEQ + t0 + wd]
                    if h_ + 1 < H:
                        mr_ready[h_ + 1] = fetch_mrow(h_ + 1)
                    p.wait(p.t, "V", mr_ready[h_])
                    mrow_b = mrowb2[h_ % 2][:]
                    pb_free = [0, 0, 0]
                    for qt in range(NQT4):
                        qt0 = qt * TT
                        pat, sat = pool.acquire()
                        for kt in range(NKT):
                            pss, sls = spool.acquire()
                            p.t.matmul(pss[:], kslab(kt * 128, 128),
                                       qslab(qt0, TT), start=True, stop=False)
                            rec = p.t.matmul(pss[:], c.ones_row_b[:],
                                             mrow_b[0:1, qt0:qt0 + TT],
                                             start=False, stop=True)
                            tvv = p.inc(rec, "T")
                            par = kt % 3
                            p.wait(p.sc, "T", tvv)
                            if pb_free[par]:
                                p.wait(p.sc, "T", pb_free[par])
                            sv = p.inc(p.sc.activation(pbuf[par][:], pss[:],
                                                       AF.Exp), "S")
                            spool.release(sls, "S", sv)
                            p.wait(p.t, "S", sv)
                            rec = p.t.matmul(pat[0:DH, :],
                                             vtok[:, kt * D + h_ * DH:
                                                  kt * D + (h_ + 1) * DH],
                                             pbuf[par][:], start=(kt == 0),
                                             stop=(kt == NKT - 1))
                            tvv = p.inc(rec, "T")
                            pb_free[par] = tvv
                        mr_guard[h_ % 2] = tvv
                        bcd, sbd = spool.acquire()
                        tvv = p.inc(p.t.matmul(bcd[0:DH, :],
                                               c.ones_row_f[0:1, 0:DH],
                                               drow2[h_ % 2][:, qt0:qt0 + TT],
                                               start=True, stop=True), "T")
                        p.wait(p.sc, "T", tvv)
                        sv = p.inc(p.sc.activation(dival[0:DH, :],
                                                   pat[0:DH, :], AF.Copy),
                                   "S")
                        pool.release(sat, "S", sv)
                        p.wait(p.v, "S", sv)
                        p.wait(p.v, "T", tvv)
                        vv = p.inc(p.v.tensor_tensor(
                            attn_o[off:off + DH,
                                   kc0 * SEQ + qt0:kc0 * SEQ + qt0 + TT],
                            dival[0:DH, :], bcd[0:DH, :], ALU.mult), "V")
                        spool.release(sbd, "V", vv)

                p.wait(p.t, "V", p.val("V"))
                wpq = {}

                def load_wp(fc2):
                    sl = fc2 % 2
                    if wp_guard[sl]:
                        p.wait(p.sc, "T", wp_guard[sl])
                    return emit_load_wconv(p, cw_proj, wpb[sl], KC, [fc2],
                                           "Q")
                wpq[0] = load_wp(0)
                for fc in range(KC):
                    if fc + 1 < KC:
                        wpq[fc + 1] = load_wp(fc + 1)
                    wvv = wconv_view(wpb[fc % 2], KC)
                    p.wait(p.t, "Q", wpq[fc])
                    for ttl in range(NQT4):
                        ps_ap, slot = pool.acquire()
                        rec = None
                        for kc in range(KC):
                            rec = p.t.matmul(
                                ps_ap[:], wvv(kc, 0),
                                attn_o[:, kc * SEQ + ttl * TT:
                                       kc * SEQ + (ttl + 1) * TT],
                                start=(kc == 0), stop=(kc == KC - 1))
                        tval = p.inc(rec, "T")
                        wp_guard[fc % 2] = tval
                        vv2 = emit_residual_store(
                            p, c, res, xin_ap, xout_ap, fc, b * NQT4 + ttl,
                            ps_ap[:], 1.0, tval,
                            bias_col=projb[:, fc:fc + 1])
                        pool.release(slot, "V", vv2)
                p.wait(p.v, "T", p.val("T"))
                p.wait(p.sc, "T", p.val("T"))


def emit_l3_attn(p, c, xin_ap, xout_ap, w, cwq, cwk, cwv, cwo):
    NT3 = 16
    SB_ = 256
    H, DH = 16, 64
    nc = p.nc
    qhbm = nc.dram_tensor(p.name_prefix + "l3aq", [D, TOK], BF16,
                          kind="Internal").ap()
    vhbm = nc.dram_tensor(p.name_prefix + "l3av", [D, TOK], BF16,
                          kind="Internal").ap()
    with p.scope():
        pool = PsumPool(p, "l3a_ps", 4, [128, TT], F32)
        spool = PsumPool(p, "l3a_sp", 2, [128, TT], F32)
        bcols = p.sb("l3a_bcols", [128, 4 * KC], F32)
        for i, nm in enumerate(("bq", "bk", "bv", "bo")):
            lv = p.inc(p.sy.dma_start(
                bcols[:, i * KC:(i + 1) * KC],
                w[nm].rearrange("(c p) -> p c", p=128),
                allow_slow_non_contiguous=True), "L")
        p.wait(p.v, "L", lv)
        vv = p.inc(p.v.tensor_scalar_mul(bcols[:, 0:KC], bcols[:, 0:KC],
                                         0.125), "V")
        rkcols = p.sb("l3a_rkc", [128, NT3 * NT3], F32)
        rvcols = p.sb("l3a_rvc", [128, NT3 * NT3], F32)
        with p.scope():
            rtab = p.sb("l3a_rtab", [33, 64], F32)
            sel_np = np.zeros((33, 256), np.float32)
            for t_ in range(16):
                for s_ in range(16):
                    r_ = min(max(s_ - t_, -16), 16) + 16
                    sel_np[r_, t_ * 16 + s_] = 1.0
            sel_t = nc.inline_tensor(sel_np, p.name_prefix + "l3a_sel")
            sel = p.sb("l3a_selb", [33, 256], F32)
            lv = p.inc(p.sy.dma_start(sel[:], sel_t.ap()), "L")
            for (nm, dst) in (("rk_table", rkcols), ("rv_table", rvcols)):
                lv = p.inc(p.sy.dma_start(rtab[:], w[nm]), "L")
                p.wait(p.t, "L", lv)
                ps_ap, slot = pool.acquire()
                tvv = p.inc(p.t.matmul(ps_ap[0:64, 0:256], rtab[:], sel[:],
                                       start=True, stop=True), "T")
                p.wait(p.v, "T", tvv)
                p.v.tensor_copy(dst[0:64, :], ps_ap[0:64, 0:256])
                vv = p.inc(p.v.tensor_copy(dst[64:128, :],
                                           ps_ap[0:64, 0:256]), "V")
                pool.release(slot, "V", vv)
                p.wait(p.sy, "V", vv)
        hsT_np = np.zeros((128, KC * 16), ml_dtypes.bfloat16)
        for kc in range(KC):
            for pp in range(128):
                hsT_np[pp, kc * 16 + (kc * 128 + pp) // 64] = 1.0
        hsT_t = nc.inline_tensor(np.asarray(hsT_np), p.name_prefix + "l3a_hsTc")
        hselT = p.sb("l3a_hsT", [128, KC * 16], BF16)
        lv = p.inc(p.sy.dma_start(hselT[:], hsT_t.ap()), "L")
        hsB_np = np.zeros((16, 1024), ml_dtypes.bfloat16)
        for h_ in range(16):
            hsB_np[h_, h_ * 64:(h_ + 1) * 64] = 1.0
        hsB_t = nc.inline_tensor(np.asarray(hsB_np), p.name_prefix + "l3a_hsBc")
        hselB = p.sb("l3a_hsB", [16, 1024], BF16)
        lv = p.inc(p.sy.dma_start(hselB[:], hsB_t.ap()), "L")

        kbuf = p.sb("l3a_k", [128, KC * TOK], BF16)
        kv_ = chunks_view(kbuf, KC, TOK)

        with p.scope():
            lnA = LNState(p, "l3a_ln", pool)
            lnb = p.sb("l3a_lnb", [128, KC * TT], BF16)
            qtile = p.sb("l3a_qt", [128, KC * TT], BF16)
            wb = [p.sb(f"l3a_wb{i}", [128, KC * 128], BF16) for i in range(2)]
            lnv = chunks_view(lnb, KC, TT)
            qtv = chunks_view(qtile, KC, TT)
            wguard = [0, 0]
            q_o_guard = [0]

            def load_w(cw, fc):
                sl = fc % 2
                if wguard[sl]:
                    p.wait(p.sc, "T", wguard[sl])
                return emit_load_wconv(p, cw, wb[sl], KC, [fc], "Q")
            for tt in range(NTT):
                lnt = emit_ln_tile(p, c, lnA, xin_ap, tt, lambda kc: lnv(kc))
                p.wait(p.t, "V", lnt)
                for (wi, cw) in enumerate((cwq, cwk, cwv)):
                    qfc = {0: load_w(cw, 0)}
                    for fc in range(KC):
                        if fc + 1 < KC:
                            qfc[fc + 1] = load_w(cw, fc + 1)
                        ps_ap, slot = pool.acquire()
                        p.wait(p.t, "Q", qfc[fc])
                        rec = None
                        wvv = wconv_view(wb[fc % 2], KC)
                        for kc in range(KC):
                            rec = p.t.matmul(ps_ap[:], wvv(kc, 0), lnv(kc),
                                             start=(kc == 0),
                                             stop=(kc == KC - 1))
                        tvv = p.inc(rec, "T")
                        wguard[fc % 2] = tvv
                        p.wait(p.v, "T", tvv)
                        if wi == 1:
                            sv = p.inc(p.v.tensor_scalar_add(
                                kv_(fc, tt * TT, TT), ps_ap[:],
                                bcols[:, KC + fc:KC + fc + 1]), "V")
                        else:
                            if q_o_guard[0]:
                                p.wait(p.v, "O", q_o_guard[0])
                            boff = 0 if wi == 0 else 2 * KC
                            if wi == 0:
                                sv = p.inc(p.v.tensor_scalar(
                                    qtv(fc), ps_ap[:], 0.125,
                                    bcols[:, boff + fc:boff + fc + 1],
                                    ALU.mult, ALU.add), "V")
                            else:
                                sv = p.inc(p.v.tensor_scalar_add(
                                    qtv(fc), ps_ap[:],
                                    bcols[:, boff + fc:boff + fc + 1]), "V")
                            p.wait(p.sy, "V", sv)
                            hb = qhbm if wi == 0 else vhbm
                            q_o_guard[0] = p.inc(p.sy.dma_start(
                                feat_dram_view(hb, fc, tt * TT, TT),
                                qtv(fc)), "O")
                        pool.release(slot, "V", sv)
                p.wait(p.v, "T", p.val("T"))

        with p.scope():
            wo = p.sb("l3a_wo", [128, KC * KC * 128], BF16)
            qv_ = emit_load_wconv(p, cwo, wo, KC, range(KC), "Q")
            wov = wconv_view(wo, KC)
            qt2 = [p.sb(f"l3a_q2{i}", [128, KC * SB_], BF16) for i in range(2)]
            vsb = [p.sb(f"l3a_vs{i}", [128, KC * SB_], BF16) for i in range(2)]
            sc2 = p.sb("l3a_sc2", [16, NT3 * SB_], F32)
            sc3 = p.sb("l3a_sc3", [16, NT3 * SB_], F32)
            pnb = p.sb("l3a_pnb", [16, NT3 * SB_], BF16)
            srows = p.sb("l3a_srows", [16, 3 * SB_], F32)
            recipb = p.sb("l3a_recipb", [16, SB_], BF16)
            zb2 = [p.sb(f"l3a_z{i}", [128, KC * SB_], BF16) for i in range(2)]
            av2 = [p.sb(f"l3a_av{i}", [128, KC * SB_], F32) for i in range(2)]
            accf = p.sb("l3a_acc", [128, KC * SB_], F32)
            attb = p.sb("l3a_attb", [128, KC * SB_], BF16)
            res = ResidualState(p, "l3a")
            qld = [0, 0]
            vld_guard = [0, 0]
            vready = {}
            p.wait(p.t, "Q", qv_)

            def load_vs(s_, vi):
                sl = vi % 2
                if vld_guard[sl]:
                    p.wait(p.sy, "V", vld_guard[sl])
                src = vhbm.rearrange("(c p) t -> p c t",
                                     p=128)[:, :, s_ * SB_:(s_ + 1) * SB_]
                return p.inc(p.sy.dma_start(
                    vsb[sl][:].rearrange("p (c t) -> p c t", c=KC), src), "L")
            for t_ in range(NT3):
                par = t_ % 2
                if qld[par]:
                    p.wait(p.sy, "V", qld[par])
                src = qhbm.rearrange("(c p) t -> p c t",
                                     p=128)[:, :, t_ * SB_:(t_ + 1) * SB_]
                lv = p.inc(p.sy.dma_start(
                    qt2[par][:].rearrange("p (c t) -> p c t", c=KC), src), "L")
                p.wait(p.v, "L", lv)
                z_guard = [0, 0]
                for s_ in range(NT3):
                    zs = zb2[s_ % 2]
                    zv = zs[:].rearrange("p (c t) -> p c t", c=KC)
                    kslc = bass.AP(kbuf, s_ * SB_, [[KC * TOK, 128],
                                                    [TOK, KC], [1, SB_]])
                    qslc = qt2[par][:].rearrange("p (c t) -> p c t", c=KC)
                    if z_guard[s_ % 2]:
                        p.wait(p.v, "T", z_guard[s_ % 2])
                    vv = p.inc(p.v.scalar_tensor_tensor(
                        zv, kslc, rkcols[:, t_ * 16 + s_:t_ * 16 + s_ + 1],
                        qslc, ALU.add, ALU.mult), "V")
                    p.wait(p.t, "V", vv)
                    ps_ap, slot = spool.acquire()
                    rec = None
                    for kc in range(KC):
                        rec = p.t.matmul(ps_ap[0:16, 0:SB_],
                                         hselT[:, kc * 16:(kc + 1) * 16],
                                         zs[:, kc * SB_:(kc + 1) * SB_],
                                         start=(kc == 0), stop=(kc == KC - 1))
                    tvv = p.inc(rec, "T")
                    z_guard[s_ % 2] = tvv
                    p.wait(p.sc, "T", tvv)
                    sv = p.inc(p.sc.activation(sc2[:, s_ * SB_:(s_ + 1) * SB_],
                                               ps_ap[0:16, 0:SB_], AF.Copy),
                               "S")
                    spool.release(slot, "S", sv)
                p.wait(p.v, "S", p.val("S"))
                v2 = bass.AP(sc2, 0, [[NT3 * SB_, 16], [1, SB_], [SB_, NT3]])
                v3 = bass.AP(sc3, 0, [[NT3 * SB_, 16], [1, SB_], [SB_, NT3]])
                p.v.reduce_max(bass.AP(srows, 0, [[3 * SB_, 16], [1, SB_],
                                                  [0, 1]]), v2, axis=AX.X,
                               negate=True)
                vv = p.inc(p.v.tensor_tensor(
                    v3, v2, bass.AP(srows, 0, [[3 * SB_, 16], [1, SB_],
                                               [0, NT3]]), ALU.add), "V")
                p.wait(p.sc, "V", vv)
                sv = p.inc(p.sc.activation(pnb[:], sc3[:], AF.Exp), "S")
                p.wait(p.v, "S", sv)
                pv3 = bass.AP(pnb, 0, [[NT3 * SB_, 16], [1, SB_], [SB_, NT3]])
                p.v.reduce_sum(bass.AP(srows, SB_, [[3 * SB_, 16], [1, SB_],
                                                    [0, 1]]), pv3, axis=AX.X)
                vv = p.inc(p.v.reciprocal(srows[:, 2 * SB_:3 * SB_],
                                          srows[:, SB_:2 * SB_]), "V")
                vv = p.inc(p.v.tensor_copy(recipb[:],
                                           srows[:, 2 * SB_:3 * SB_]), "V")
                p.wait(p.t, "V", vv)
                vready[0] = load_vs(0, 0)
                av_guard = [0, 0]
                for s_ in range(NT3):
                    if s_ + 1 < NT3:
                        vready[s_ + 1] = load_vs(s_ + 1, s_ + 1)
                    rvc = rvcols[:, t_ * 16 + s_:t_ * 16 + s_ + 1]
                    vsl = vsb[s_ % 2]
                    avt = av2[s_ % 2]
                    p.wait(p.v, "L", vready[s_])
                    if av_guard[s_ % 2]:
                        p.wait(p.v, "P", av_guard[s_ % 2])
                    for kg in range(KC // 2):
                        bcp, bslot = pool.acquire()
                        last_vv = None
                        for j in range(2):
                            kc = kg * 2 + j
                            tvv = p.inc(p.t.matmul(
                                bcp[:, j * SB_:(j + 1) * SB_],
                                hselB[:, kc * 128:(kc + 1) * 128],
                                pnb[:, s_ * SB_:(s_ + 1) * SB_],
                                start=True, stop=True), "T")
                            p.wait(p.v, "T", tvv)
                            last_vv = p.inc(p.v.scalar_tensor_tensor(
                                avt[:, kc * SB_:(kc + 1) * SB_],
                                vsl[:, kc * SB_:(kc + 1) * SB_], rvc,
                                bcp[:, j * SB_:(j + 1) * SB_], ALU.add,
                                ALU.mult), "V")
                        pool.release(bslot, "V", last_vv)
                    p.wait(p.g, "V", p.val("V"))
                    if s_ == 0:
                        pv = p.inc(p.g.tensor_copy(accf[:], avt[:]), "P")
                    else:
                        pv = p.inc(p.g.tensor_tensor(accf[:], accf[:],
                                                     avt[:], ALU.add), "P")
                    av_guard[s_ % 2] = pv
                    vld_guard[s_ % 2] = p.val("V")
                p.wait(p.t, "V", p.val("V"))
                p.wait(p.v, "P", p.val("P"))
                for kc in range(KC):
                    bcp, bslot = pool.acquire()
                    tvv = p.inc(p.t.matmul(bcp[:, 0:SB_],
                                           hselB[:, kc * 128:(kc + 1) * 128],
                                           recipb[:],
                                           start=True, stop=True), "T")
                    p.wait(p.v, "T", tvv)
                    vv = p.inc(p.v.tensor_tensor(
                        attb[:, kc * SB_:(kc + 1) * SB_],
                        accf[:, kc * SB_:(kc + 1) * SB_], bcp[:, 0:SB_],
                        ALU.mult), "V")
                    pool.release(bslot, "V", vv)
                qld[par] = vv
                p.wait(p.t, "V", vv)
                for fc in range(KC):
                    ps_ap, slot = pool.acquire()
                    rec = None
                    for kc in range(KC):
                        rec = p.t.matmul(ps_ap[:, 0:SB_], wov(kc, fc),
                                         attb[:, kc * SB_:(kc + 1) * SB_],
                                         start=(kc == 0), stop=(kc == KC - 1))
                    tvv = p.inc(rec, "T")
                    slot_r = res.i % res.n
                    res.i += 1
                    buf = res.buf[slot_r]
                    if res.guard[slot_r] is not None:
                        p.wait(p.sy, "O", res.guard[slot_r])
                    lv = p.inc(p.sy.dma_start(
                        buf[:, 0:SB_],
                        feat_dram_view(xin_ap, fc, t_ * SB_, SB_)), "L")
                    p.wait(p.v, "L", lv)
                    p.wait(p.v, "T", tvv)
                    p.v.tensor_scalar(buf[:, TT:TT + SB_], ps_ap[:, 0:SB_],
                                      bcols[:, 3 * KC + fc:3 * KC + fc + 1],
                                      None, ALU.add)
                    vvr = p.inc(p.v.tensor_tensor(
                        buf[:, TT + SB_:TT + 2 * SB_], buf[:, TT:TT + SB_],
                        buf[:, 0:SB_], ALU.add), "V")
                    pool.release(slot, "V", vvr)
                    p.wait(p.sy, "V", vvr)
                    ov = p.inc(p.sy.dma_start(
                        feat_dram_view(xout_ap, fc, t_ * SB_, SB_),
                        buf[:, TT + SB_:TT + 2 * SB_]), "O")
                    res.guard[slot_r] = ov
                p.wait(p.sc, "T", p.val("T"))
                p.wait(p.v, "T", p.val("T"))


LAYER_SPECS_012 = [
    ("l0_ffn", "L0_ffn_"), ("l0_attn", "L0_attn_"), ("l0_ffn", "L0_ffn_"),
    ("l1_ffn", "L1_ffn_"), ("l1_attn", "L1_attn_"), ("l1_ffn", "L1_ffn_"),
    ("l2_ffn", "L2_ffn_"), ("l2_attn", "L2_attn_"), ("l2_ffn", "L2_ffn_"),
]
LAYER_SPECS_3 = [
    ("l3_ffn", "L3_ffn_"), ("l3_attn", "L3_attn_"), ("l3_ffn", "L3_ffn_"),
]

_prog_cache = {}


def _chain_prog(specs, a2a_before=None):
    key = (tuple(specs), a2a_before)
    if key not in _prog_cache:
        _prog_cache[key] = build_chain(list(specs), a2a_before=a2a_before)
    return _prog_cache[key]


def kernel(**inputs):
    x = np.asarray(inputs["L0_attn_x"], dtype=np.float32)
    weights = {}
    for k, v in inputs.items():
        if k == "L0_attn_x":
            continue
        a = np.asarray(v)
        shape, dt = INPUT_SPECS[k]
        if dt == I32:
            a = np.ascontiguousarray(a.reshape(shape), dtype=np.int32)
        else:
            a = np.ascontiguousarray(a.reshape(shape), dtype=np.float32)
        weights[k] = a

    shards = [np.ascontiguousarray(x[2 * cc:2 * cc + 2].reshape(4096, 1024).T)
              for cc in range(NCORES)]

    def run_specs(specs, cur, a2a_before=None):
        nc_prog, used = _chain_prog(tuple(specs), a2a_before)
        in_maps = []
        for cc in range(NCORES):
            m = {"xin": cur[cc]}
            for nm in used:
                m[nm] = weights[nm]
            in_maps.append(m)
        res = run_bass_kernel_spmd(nc_prog, in_maps,
                                   core_ids=list(range(NCORES)))
        return [np.asarray(r["xout"]) for r in res.results]

    cur3 = run_specs(LAYER_SPECS_012 + LAYER_SPECS_3, shards,
                     a2a_before=len(LAYER_SPECS_012))

    out = np.empty((16, 2048, 1024), np.float32)
    for cc in range(NCORES):
        o = cur3[cc].T.reshape(16, 256, 1024)
        out[:, cc * 256:(cc + 1) * 256, :] = o
    return out


# version 08
# speedup vs baseline: 1.0138x; 1.0138x over previous
import numpy as np
import ml_dtypes
from contextlib import ExitStack

import concourse.bass as bass
import concourse.mybir as mybir
from concourse.bass_utils import run_bass_kernel_spmd

F32 = mybir.dt.float32
BF16 = mybir.dt.bfloat16
I32 = mybir.dt.int32
AF = mybir.ActivationFunctionType
ALU = mybir.AluOpType
AX = mybir.AxisListType

ENGS = ("sync", "scalar", "vector", "tensor", "gpsimd")

D = 1024
KC = 8
TOK = 4096
TT = 512
NTT = TOK // TT
NCORES = 8
BIG = 1.0e30
_dbg = {}



class _OpRec:
    __slots__ = ("incs",)

    def __init__(self):
        self.incs = []

    def then_inc(self, sem_handle, n):
        self.incs.append((sem_handle, n))
        return self


_AUTOSYNC_OPS = ("tensor_", "reduce_", "reciprocal", "memset", "memzero",
                 "activation", "select", "copy_predicated", "bn_", "iota",
                 "affine_select", "scalar_tensor_tensor", "square", "sqrt",
                 "sign", "transpose")
_ENG_SEM = {"vector": "V", "scalar": "S", "gpsimd": "P"}


def _op_is_small(a):
    if not a:
        return True
    out = a[0]
    try:
        shape = out.shape
    except Exception:
        return True
    if len(shape) < 2:
        return True
    parts = shape[0]
    free = 1
    for s in shape[1:]:
        free *= s
    return parts < 128 or free < 512


class _EngProxy:
    def __init__(self, prog, eng):
        self._p = prog
        self._e = eng

    def __getattr__(self, name):
        p, e = self._p, self._e

        def call(*a, **k):
            rec = _OpRec()
            p.q[e].append((name, a, k, rec))
            if e in _ENG_SEM and name.startswith(_AUTOSYNC_OPS)                     and _op_is_small(a):
                sname = _ENG_SEM[e]
                p.cnt[sname] += 1
                rec.incs.append((p.sem[sname], 1))
                wrec = _OpRec()
                p.q[e].append(("wait_ge", (p.sem[sname], p.cnt[sname]), {},
                               wrec))
            return rec

        return call


class Prog:

    def __init__(self, nc, ctx):
        self.nc = nc
        self.ctx = ctx
        self.q = {e: [] for e in ENGS}
        self.sem = {}
        self.cnt = {}
        for name in ("L", "O", "G", "Q", "V", "T", "S", "P", "B"):
            self.sem[name] = ctx.enter_context(nc.semaphore("sem" + name))
            self.cnt[name] = 0
        self.sy = _EngProxy(self, "sync")
        self.sc = _EngProxy(self, "scalar")
        self.v = _EngProxy(self, "vector")
        self.t = _EngProxy(self, "tensor")
        self.g = _EngProxy(self, "gpsimd")
        self.proxy = {"sync": self.sy, "scalar": self.sc, "vector": self.v,
                      "tensor": self.t, "gpsimd": self.g}
        self.name_prefix = ""
        self.dma_sems = {"sync": ["L", "O"], "gpsimd": ["G"], "scalar": ["Q"]}

    def inc(self, rec, name, n=None):
        if n is None:
            n = 16 if name in ("L", "O", "G", "Q") else 1
        self.cnt[name] += n
        rec.then_inc(self.sem[name], n)
        return self.cnt[name]

    def val(self, name):
        return self.cnt[name]

    def wait(self, proxy, name, value=None):
        if value is None:
            value = self.cnt[name]
        if value > 0:
            proxy.wait_ge(self.sem[name], value)

    def barrier(self):
        for e in ENGS:
            px = self.proxy[e]
            for s in self.dma_sems.get(e, []):
                self.wait(px, s)
        self.cnt["B"] += len(ENGS)
        target = self.cnt["B"]
        for e in ENGS:
            self.proxy[e].sem_inc(self.sem["B"], 1)
        for e in ENGS:
            self.proxy[e].wait_ge(self.sem["B"], target)

    def scope(self):
        from contextlib import contextmanager

        @contextmanager
        def _scope():
            old = self.ctx
            with ExitStack() as es:
                self.ctx = es
                try:
                    yield
                finally:
                    self.ctx = old
        return _scope()

    def sb(self, name, shape, dtype=F32):
        return self.ctx.enter_context(
            self.nc.sbuf_tensor(self.name_prefix + name, shape, dtype))

    def ps(self, name, shape, dtype=F32):
        return self.ctx.enter_context(
            self.nc.psum_tensor(self.name_prefix + name, shape, dtype))

    def play(self):
        with self.nc.Block() as block:
            for ename in ENGS:
                def emit(eng, _q=self.q[ename]):
                    for (name, a, k, rec) in _q:
                        inst = getattr(eng, name)(*a, **k)
                        merged = {}
                        for (h, n) in rec.incs:
                            merged[h] = merged.get(h, 0) + n
                        for (h, n) in merged.items():
                            inst.then_inc(h, n)
                getattr(block, ename)(emit)


class PsumPool:

    def __init__(self, p, name, n, shape, dtype=F32):
        self.p = p
        self.tiles = [p.ps(f"{name}{i}", shape, dtype) for i in range(n)]
        self.n = n
        self.i = 0
        self.guard = [None] * n

    def acquire(self):
        slot = self.i % self.n
        self.i += 1
        if self.guard[slot] is not None:
            nm, v = self.guard[slot]
            self.p.wait(self.p.t, nm, v)
        return self.tiles[slot], slot

    def release(self, slot, sem_name, value):
        self.guard[slot] = (sem_name, value)


CONST_VALUES = [1e-5, 1e-6, 1e-12, 1e-9, 1.0 / 127.0, 0.5, -0.5, 2.0,
                1.0 / D, 1.0 / 16.0, 0.125, 1.0, -1.0, 0.1, 0.3, 0.7, 0.2,
                1.0 / 2048.0, -0.0625]


class TransPool:

    def __init__(self, p, name, dtype=BF16):
        self.p = p
        self.tiles = [p.ps(f"{name}{i}", [128, 128], dtype) for i in range(2)]
        self.i = 0
        self.guard = [0, 0]

    def step(self, in_ap, ident, out_ap, out_rows=128, out_cols=128):
        p = self.p
        sl = self.i % 2
        self.i += 1
        if self.guard[sl]:
            p.wait(p.t, "V", self.guard[sl])
        ps = self.tiles[sl]
        tv = p.inc(p.t.transpose(ps[0:out_rows, 0:out_cols], in_ap, ident),
                   "T")
        p.wait(p.v, "T", tv)
        vv = p.inc(p.v.tensor_copy(out_ap, ps[0:out_rows, 0:out_cols]), "V")
        self.guard[sl] = vv
        return vv


class Const:

    def __init__(self, p):
        nc = p.nc
        nvals = len(CONST_VALUES)
        self.cpool = p.sb("c_pool", [128, nvals], F32)
        self.ones_row_f = p.sb("c_ones_row_f", [1, 128], F32)
        self.ones_col_f = p.sb("c_ones_col_f", [128, 1], F32)
        self.ones_row_b = p.sb("c_ones_row_b", [1, 128], BF16)
        self.ones_col_b = p.sb("c_ones_col_b", [128, 1], BF16)
        self.ident_b = p.sb("c_ident_b", [128, 128], BF16)
        self.ident_f = p.sb("c_ident_f", [128, 128], F32)
        self.iota_pf = p.sb("c_iota_pf", [128, TT], F32)
        srcs = [
            (self.ones_row_f, np.ones((1, 128), np.float32)),
            (self.ones_col_f, np.ones((128, 1), np.float32)),
            (self.ones_row_b, np.ones((1, 128), ml_dtypes.bfloat16)),
            (self.ones_col_b, np.ones((128, 1), ml_dtypes.bfloat16)),
            (self.ident_b, np.eye(128, dtype=ml_dtypes.bfloat16)),
            (self.ident_f, np.eye(128, dtype=np.float32)),
            (self.iota_pf, (np.arange(128)[:, None] -
                            np.arange(TT)[None, :]).astype(np.float32)),
            (self.cpool, np.tile(np.asarray(CONST_VALUES, np.float32)[None, :],
                                 (128, 1))),
        ]
        for i, (dst, arr) in enumerate(srcs):
            h = p.nc.inline_tensor(arr, f"c_inl{i}")
            p.inc(p.sy.dma_start(dst[:], h.ap()), "L")

    def col(self, v):
        i = CONST_VALUES.index(v)
        return self.cpool[:, i:i + 1]

    def one(self, v):
        i = CONST_VALUES.index(v)
        return self.cpool[0:1, i:i + 1]



def chunks_view(t, nch=KC, tw=TOK):
    def view(c, t0=0, w=None):
        if w is None:
            w = tw
        return t[:, c * tw + t0: c * tw + t0 + w]
    return view


def feat_dram_view(ap2d, c, t0, w, p0=0, pn=128):
    return ap2d[c * 128 + p0: c * 128 + p0 + pn, t0: t0 + w]


def emit_weight_lhsT_load(p, hbm_f32, w_sb, din, dout, cast_engine="gpsimd"):
    nk = din // 128
    for kc in range(nk):
        dst = w_sb[:, kc * dout:(kc + 1) * dout]
        src = hbm_f32[kc * 128:(kc + 1) * 128, :]
        p.inc(p.g.dma_start(dst, src), "G")
    return p.val("G")


def w_view(w_sb, dout):
    def view(kc, fc, fw=128):
        return w_sb[:, kc * dout + fc * 128: kc * dout + fc * 128 + fw]
    return view




class LNState:

    def __init__(self, p, name, pool):
        self.p = p
        self.xs = p.sb(name + "_xs", [128, KC * TT], F32)
        self.sq = p.sb(name + "_sq", [128, 2 * TT], F32)
        self.rows = p.sb(name + "_rows", [1, 4 * TT], F32)
        self.st1 = p.ps(name + "_st1", [1, TT], F32)
        self.st2 = p.ps(name + "_st2", [1, TT], F32)
        self.bc = pool
        self.last_v = 0

    def row(self, i, t0=0, w=TT):
        return self.rows[:, i * TT + t0: i * TT + t0 + w]


def emit_ln_tile(p, c, ln, xin_ap, tt, ln_out_view, eps=1e-5, keep_x=False):
    t0 = tt * TT
    xsv = chunks_view(ln.xs, KC, TT)
    if ln.last_v:
        p.wait(p.sy, "V", ln.last_v)
    src = xin_ap.rearrange("(c p) t -> p c t", p=128)[:, :, t0:t0 + TT]
    dst = ln.xs[:].rearrange("p (c t) -> p c t", c=KC)
    lval = p.inc(p.sy.dma_start(dst, src), "L")
    p.wait(p.t, "L", lval)
    if ln.last_v:
        p.wait(p.t, "V", ln.last_v)
    rec = None
    for kc in range(KC):
        rec = p.t.matmul(ln.st1[:], c.ones_col_f[:], xsv(kc), start=(kc == 0),
                         stop=(kc == KC - 1))
    p.inc(rec, "T")
    p.wait(p.sc, "L", lval)
    tq = {}
    for kc in range(KC):
        sqv = ln.sq[:, (kc % 2) * TT:(kc % 2) * TT + TT]
        if kc >= 2:
            p.wait(p.sc, "T", tq[kc - 2])
        sk = p.inc(p.sc.activation(sqv, xsv(kc), AF.Square), "S")
        p.wait(p.t, "S", sk)
        tq[kc] = p.inc(p.t.matmul(ln.st2[:], c.ones_col_f[:], sqv,
                                  start=(kc == 0), stop=(kc == KC - 1)), "T")
    tsum = p.val("T")
    p.wait(p.v, "T", tsum)
    p.v.tensor_scalar_mul(ln.row(0), ln.st1[:], 1.0 / D)
    p.v.tensor_tensor(ln.row(3), ln.row(0), ln.row(0), ALU.mult)
    vv = p.inc(p.v.scalar_tensor_tensor(ln.row(1), ln.st2[:], 1.0 / D,
                                        ln.row(3), ALU.mult, ALU.subtract),
               "V")
    p.wait(p.sc, "V", vv)
    sv = p.inc(p.sc.activation(ln.row(3), ln.row(1), AF.Sqrt, bias=c.one(eps)), "S")
    p.wait(p.v, "S", sv)
    vv = p.inc(p.v.reciprocal(ln.row(2), ln.row(3)), "V")
    p.wait(p.t, "V", vv)
    bm, sm = ln.bc.acquire()
    p.t.matmul(bm[:], c.ones_row_f[:], ln.row(0), start=True, stop=True)
    br, sr = ln.bc.acquire()
    tb = p.inc(p.t.matmul(br[:], c.ones_row_f[:], ln.row(2), start=True,
                          stop=True), "T")
    p.wait(p.v, "T", tb)
    for kc in range(KC):
        tmp = ln.sq[:, (kc % 2) * TT:(kc % 2) * TT + TT]
        p.v.tensor_tensor(tmp, xsv(kc), bm[:], ALU.subtract)
        vv = p.inc(p.v.tensor_tensor(ln_out_view(kc), tmp, br[:], ALU.mult),
                   "V")
    ln.bc.release(sm, "V", vv)
    ln.bc.release(sr, "V", vv)
    ln.last_v = vv
    return vv


class ResidualState:
    def __init__(self, p, name, n=2):
        self.buf = [p.sb(f"{name}_res{i}", [128, 3 * TT], F32)
                    for i in range(n)]
        self.n = n
        self.i = 0
        self.guard = [None] * n


def emit_residual_store(p, c, res, xin_ap, xout_ap, fc, tt, psum_ap, alpha,
                        tval, bias_col=None, scale_col=None):
    if not isinstance(psum_ap, bass.AP):
        psum_ap = psum_ap[:]
    slot = res.i % res.n
    res.i += 1
    buf = res.buf[slot]
    if res.guard[slot] is not None:
        p.wait(p.sy, "O", res.guard[slot])
    lv = p.inc(p.sy.dma_start(buf[:, 0:TT],
                              feat_dram_view(xin_ap, fc, tt * TT, TT)), "L")
    p.wait(p.v, "L", lv)
    p.wait(p.v, "T", tval)
    if bias_col is None and scale_col is None:
        vv = p.inc(p.v.scalar_tensor_tensor(buf[:, TT:2 * TT], psum_ap, alpha,
                                            buf[:, 0:TT], ALU.mult, ALU.add),
                   "V")
    else:
        tmp = buf[:, 2 * TT:3 * TT]
        if bias_col is not None:
            p.v.tensor_scalar(tmp, psum_ap, bias_col, alpha, ALU.add, ALU.mult)
        else:
            p.v.tensor_scalar(tmp, psum_ap, scale_col, alpha, ALU.mult,
                              ALU.mult)
        vv = p.inc(p.v.tensor_tensor(buf[:, TT:2 * TT], tmp, buf[:, 0:TT],
                                     ALU.add), "V")
    p.wait(p.sy, "V", vv)
    ov = p.inc(p.sy.dma_start(feat_dram_view(xout_ap, fc, tt * TT, TT),
                              buf[:, TT:2 * TT]), "O")
    res.guard[slot] = ov
    return vv



def emit_proj(p, wv, nk, in_view, nf, epilogue, pool, tts=None):
    if tts is None:
        tts = range(NTT)
    for fc in range(nf // 128):
        for tt in tts:
            ps_ap, slot = pool.acquire()
            rec = None
            for kc in range(nk):
                rec = p.t.matmul(ps_ap[:], wv(kc, fc), in_view(kc, tt),
                                 start=(kc == 0), stop=(kc == nk - 1))
            tval = p.inc(rec, "T")
            epilogue(fc, tt, ps_ap, slot, tval)



class ConvWork:

    def __init__(self, p):
        self.stage = [p.sb(f"cv_st{i}", [128, 4096], BF16) for i in range(2)]
        self.tbuf = [p.sb(f"cv_tb{i}", [128, 4096], BF16) for i in range(2)]
        self.pst = [p.ps(f"cv_ps{i}", [128, 128], BF16) for i in range(2)]
        self.stage_guard = [0, 0]
        self.stage_cast_guard = [0, 0]
        self.tbuf_guard = [0, 0]
        self.pst_guard = [0, 0]


def emit_convert_lhsT(p, c, cw, src_ap, dst_ap, R, C):
    nr = R // 128
    ncc = C // 128
    for rc in range(nr):
        sl = rc % 2
        if cw.stage_guard[sl]:
            p.wait(p.g, "T", cw.stage_guard[sl])
        gv = p.inc(p.g.dma_start(cw.stage[sl][:, 0:C],
                                 src_ap[rc * 128:(rc + 1) * 128, :]), "G")
        p.wait(p.t, "G", gv)
        tbl = rc % 2
        if cw.tbuf_guard[tbl]:
            p.wait(p.v, "O", cw.tbuf_guard[tbl])
        vv = 0
        for cc in range(ncc):
            psl = cc % 2
            if cw.pst_guard[psl]:
                p.wait(p.t, "V", cw.pst_guard[psl])
            tv = p.inc(p.t.transpose(cw.pst[psl][:],
                                     cw.stage[sl][:, cc * 128:(cc + 1) * 128],
                                     c.ident_b[:]), "T")
            cw.stage_guard[sl] = tv
            p.wait(p.v, "T", tv)
            vv = p.inc(p.v.tensor_copy(
                cw.tbuf[tbl][:, cc * 128:(cc + 1) * 128], cw.pst[psl][:]),
                "V")
            cw.pst_guard[psl] = vv
        p.wait(p.sy, "V", vv)
        dstv = dst_ap[rc].rearrange("k p f -> p k f")
        srcv = cw.tbuf[tbl][:, 0:ncc * 128].rearrange("p (k f) -> p k f",
                                                      f=128)
        ov = p.inc(p.sy.dma_start(dstv, srcv), "O")
        cw.tbuf_guard[tbl] = ov
    return p.val("O")


def emit_load_wconv(p, cw_ap4, w_sb, nk, fcs, queue="Q", w_sb_off=0):
    px = p.sc if queue == "Q" else (p.sy if queue in ("L", "O") else p.g)
    for i, fc in enumerate(fcs):
        src = cw_ap4[fc].rearrange("k p f -> p k f")
        dst = w_sb[:, w_sb_off + i * nk * 128: w_sb_off + (i + 1) * nk * 128]
        dst = dst.rearrange("p (k f) -> p k f", f=128)
        p.inc(px.dma_start(dst, src), queue)
    return p.val(queue)


def wconv_view(w_sb, nk, off=0):
    def view(kc, i, fw=128):
        base = off + (i * nk + kc) * 128
        return w_sb[:, base: base + fw]
    return view


def emit_load_w(p, hbm_f32, w_sb, din, dout):
    nk = din // 128
    for kc in range(nk):
        dst = w_sb[:, kc * dout:(kc + 1) * dout]
        src = hbm_f32[kc * 128:(kc + 1) * 128, :]
        p.inc(p.g.dma_start(dst, src), "G")
    return p.val("G")



def emit_l1_ffn(p, c, xin_ap, xout_ap, cwg, cwu, cwd):
    H = 3072
    NH = H // 128
    with p.scope():
        wg = p.sb("l1f_wg", [128, NH * KC * 128], BF16)
        wu = p.sb("l1f_wu", [128, NH * KC * 128], BF16)
        wdb = [p.sb(f"l1f_wd{i}", [128, NH * 128], BF16) for i in range(2)]
        emit_load_wconv(p, cwg, wg, KC, range(NH), "Q")
        qv = emit_load_wconv(p, cwu, wu, KC, range(NH), "Q")
        wgv, wuv = wconv_view(wg, KC), wconv_view(wu, KC)
        wdv = [wconv_view(wdb[0], 1), wconv_view(wdb[1], 1)]
        pool = PsumPool(p, "l1f_ps", 4, [128, TT], F32)
        ln = LNState(p, "l1f_ln", pool)
        lnb = p.sb("l1f_lnb", [128, KC * TT], BF16)
        hbuf = p.sb("l1f_h", [128, NH * TT], BF16)
        silu_t = p.sb("l1f_silu", [128, TT], F32)
        res = ResidualState(p, "l1f")
        p.wait(p.t, "Q", qv)
        lnv = chunks_view(lnb, KC, TT)
        hv = chunks_view(hbuf, NH, TT)
        last_hv = None
        wd_guard = [0, 0]

        def load_wd(fc):
            sl = fc % 2
            if wd_guard[sl]:
                p.wait(p.sc, "T", wd_guard[sl])
            return emit_load_wconv(p, cwd, wdb[sl], NH, [fc], "Q")

        for tt in range(NTT):
            lnt = emit_ln_tile(p, c, ln, xin_ap, tt, lambda kc: lnv(kc))
            p.wait(p.t, "V", lnt)
            for fc in range(NH):
                psg, sg = pool.acquire()
                rec = None
                for kc in range(KC):
                    rec = p.t.matmul(psg[:], wgv(kc, fc), lnv(kc),
                                     start=(kc == 0), stop=(kc == KC - 1))
                tg = p.inc(rec, "T")
                psu, su = pool.acquire()
                rec = None
                for kc in range(KC):
                    rec = p.t.matmul(psu[:], wuv(kc, fc), lnv(kc),
                                     start=(kc == 0), stop=(kc == KC - 1))
                tu = p.inc(rec, "T")
                p.wait(p.sc, "T", tg)
                if last_hv is not None:
                    p.wait(p.sc, "V", last_hv)
                sv = p.inc(p.sc.activation(silu_t[:], psg[:], AF.Silu,
                                           scale=c.col(1.0 / 127.0)), "S")
                p.wait(p.v, "S", sv)
                p.wait(p.v, "T", tu)
                last_hv = p.inc(
                    p.v.scalar_tensor_tensor(hv(fc), psu[:], 1.0 / 127.0,
                                             silu_t[:], ALU.mult, ALU.mult),
                    "V")
                pool.release(sg, "S", sv)
                pool.release(su, "V", last_hv)
            p.wait(p.t, "V", last_hv)
            qfc = {0: load_wd(0)}
            for fc in range(KC):
                if fc + 1 < KC:
                    qfc[fc + 1] = load_wd(fc + 1)
                ps_ap, slot = pool.acquire()
                p.wait(p.t, "Q", qfc[fc])
                rec = None
                for kc in range(NH):
                    rec = p.t.matmul(ps_ap[:], wdv[fc % 2](kc, 0), hv(kc),
                                     start=(kc == 0), stop=(kc == NH - 1))
                tval = p.inc(rec, "T")
                wd_guard[fc % 2] = tval
                vv = emit_residual_store(p, c, res, xin_ap, xout_ap, fc, tt,
                                         ps_ap[:], 0.5 / 127.0, tval)
                pool.release(slot, "V", vv)
            p.wait(p.v, "T", p.val("T"))


def emit_l3_ffn(p, c, xin_ap, xout_ap, w):
    FF = 4096
    NF = FF // 128
    R = 256
    NR = R // 128
    with p.scope():
        u1 = p.sb("l3f_u1", [128, KC * R], BF16)
        v1 = p.sb("l3f_v1", [128, NR * FF], BF16)
        u2 = p.sb("l3f_u2", [128, NF * R], BF16)
        v2 = p.sb("l3f_v2", [128, NR * D], BF16)
        emit_load_w(p, w["U1"], u1, D, R)
        emit_load_w(p, w["V1"], v1, R, FF)
        emit_load_w(p, w["U2"], u2, FF, R)
        gv = emit_load_w(p, w["V2"], v2, R, D)
        b1sb = p.sb("l3f_b1", [128, NF], F32)
        b2sb = p.sb("l3f_b2", [128, KC], F32)
        lv = p.inc(p.sy.dma_start(
            b1sb[:], w["b1"].rearrange("(c p) -> p c", p=128),
            allow_slow_non_contiguous=True), "L")
        lv = p.inc(p.sy.dma_start(
            b2sb[:], w["b2"].rearrange("(c p) -> p c", p=128),
            allow_slow_non_contiguous=True), "L")
        u1v, v1v = w_view(u1, R), w_view(v1, FF)
        u2v, v2v = w_view(u2, R), w_view(v2, D)
        pool = PsumPool(p, "l3f_ps", 4, [128, TT], F32)
        ln = LNState(p, "l3f_ln", pool)
        lnb = p.sb("l3f_lnb", [128, KC * TT], BF16)
        pbuf = p.sb("l3f_p", [128, NR * TT], BF16)
        hbuf = p.sb("l3f_h", [128, NF * TT], BF16)
        qbuf = p.sb("l3f_q", [128, NR * TT], BF16)
        res = ResidualState(p, "l3f")
        p.wait(p.t, "G", gv)
        p.wait(p.t, "L", lv)
        lnv = chunks_view(lnb, KC, TT)
        pv = chunks_view(pbuf, NR, TT)
        hv = chunks_view(hbuf, NF, TT)
        qv = chunks_view(qbuf, NR, TT)

        for tt in range(NTT):
            lnt = emit_ln_tile(p, c, ln, xin_ap, tt, lambda kc: lnv(kc))
            p.wait(p.t, "V", lnt)

            def epi_p(fc, _tt, ps_ap, slot, tval):
                p.wait(p.sc, "T", tval)
                sv = p.inc(p.sc.activation(pv(fc), ps_ap[:], AF.Copy), "S")
                pool.release(slot, "S", sv)
            emit_proj(p, u1v, KC, lambda kc, _tt: lnv(kc), R, epi_p, pool,
                      tts=[tt])
            p.wait(p.t, "S", p.val("S"))

            def epi_h(fc, _tt, ps_ap, slot, tval):
                p.wait(p.sc, "T", tval)
                sv = p.inc(p.sc.activation(hv(fc), ps_ap[:], AF.Gelu_apprx_tanh,
                                           bias=b1sb[:, fc:fc + 1]), "S")
                pool.release(slot, "S", sv)
            emit_proj(p, v1v, NR, lambda kc, _tt: pv(kc), FF, epi_h, pool,
                      tts=[tt])
            p.wait(p.t, "S", p.val("S"))

            def epi_q(fc, _tt, ps_ap, slot, tval):
                p.wait(p.sc, "T", tval)
                sv = p.inc(p.sc.activation(qv(fc), ps_ap[:], AF.Copy), "S")
                pool.release(slot, "S", sv)
            emit_proj(p, u2v, NF, lambda kc, _tt: hv(kc), R, epi_q, pool,
                      tts=[tt])
            p.wait(p.t, "S", p.val("S"))

            def epi_o(fc, _tt, ps_ap, slot, tval):
                vv = emit_residual_store(p, c, res, xin_ap, xout_ap, fc, _tt,
                                         ps_ap, 0.5, tval,
                                         bias_col=b2sb[:, fc:fc + 1])
                pool.release(slot, "V", vv)
            emit_proj(p, v2v, NR, lambda kc, _tt: qv(kc), D, epi_o, pool,
                      tts=[tt])
            p.wait(p.sc, "T", p.val("T"))
            p.wait(p.v, "T", p.val("T"))



INPUT_SPECS = {
    "L0_attn_basis": ([1024, 256], F32), "L0_attn_q_coeffs": ([256, 256], F32),
    "L0_attn_k_coeffs": ([256, 256], F32), "L0_attn_v_coeffs": ([256, 256], F32),
    "L0_attn_o_coeffs": ([256, 256], F32), "L0_attn_decay_logit": ([1], F32),
    "L0_attn_out_scale": ([1], F32),
    "L0_ffn_cn_w1": ([1024, 1024], F32), "L0_ffn_cn_b1": ([1024], F32),
    "L0_ffn_cn_w2": ([1024, 512], F32), "L0_ffn_cn_b2": ([512], F32),
    "L0_ffn_cn_w3": ([512, 256], F32), "L0_ffn_cn_b3": ([256], F32),
    "L0_ffn_cn_w4": ([256, 1], F32), "L0_ffn_cn_b4": ([1], F32),
    "L0_ffn_gc_w1": ([1024, 512], F32), "L0_ffn_gc_b1": ([512], F32),
    "L0_ffn_gc_w2": ([512, 256], F32), "L0_ffn_gc_b2": ([256], F32),
    "L0_ffn_gc_w3": ([256, 1], F32), "L0_ffn_gc_b3": ([1], F32),
    "L0_ffn_adaptive_range": ([1], F32),
    "L1_attn_qkv_w": ([3072, 1024], F32), "L1_attn_temperature": ([16], F32),
    "L1_attn_proj_w": ([1024, 1024], F32), "L1_attn_proj_b": ([1024], F32),
    "L1_ffn_w_gate": ([3072, 1024], I32), "L1_ffn_w_up": ([3072, 1024], I32),
    "L1_ffn_w_down": ([1024, 3072], I32),
    "L2_attn_W_lambda": ([16, 1024], F32), "L2_attn_b_lambda": ([16], F32),
    "L2_attn_W_B": ([16, 1024], F32), "L2_attn_W_C": ([1024, 16], F32),
    "L2_attn_W_gate": ([1024, 1024], F32), "L2_attn_rms_weight": ([1024], F32),
    "L2_ffn_gate_w": ([8, 1024], F32), "L2_ffn_w1": ([8, 1024, 4096], F32),
    "L2_ffn_b1": ([8, 4096], F32), "L2_ffn_w2": ([8, 4096, 1024], F32),
    "L2_ffn_b2": ([8, 1024], F32),
    "L3_attn_Wq": ([1024, 1024], F32), "L3_attn_bq": ([1024], F32),
    "L3_attn_Wk": ([1024, 1024], F32), "L3_attn_bk": ([1024], F32),
    "L3_attn_Wv": ([1024, 1024], F32), "L3_attn_bv": ([1024], F32),
    "L3_attn_Wo": ([1024, 1024], F32), "L3_attn_bo": ([1024], F32),
    "L3_attn_rk_table": ([33, 64], F32), "L3_attn_rv_table": ([33, 64], F32),
    "L3_ffn_U1": ([1024, 256], F32), "L3_ffn_V1": ([256, 4096], F32),
    "L3_ffn_b1": ([4096], F32), "L3_ffn_U2": ([4096, 256], F32),
    "L3_ffn_V2": ([256, 1024], F32), "L3_ffn_b2": ([1024], F32),
}

def emit_convert_cast(p, cw, src2d, dst2d, R, C):
    for rc in range(R // 128):
        sl = rc % 2
        if cw.stage_cast_guard[sl]:
            p.wait(p.g, "O", cw.stage_cast_guard[sl])
        gv = p.inc(p.g.dma_start(cw.stage[sl][:, 0:C],
                                 src2d[rc * 128:(rc + 1) * 128, :]), "G")
        p.wait(p.sy, "G", gv)
        cw.stage_cast_guard[sl] = p.inc(
            p.sy.dma_start(dst2d[rc * 128:(rc + 1) * 128, :],
                           cw.stage[sl][:, 0:C]), "O")
    return p.val("O")


CONVERT_W = {
    "L1_ffn_w_gate": (3072, 1024), "L1_ffn_w_up": (3072, 1024),
    "L1_ffn_w_down": (1024, 3072),
    "L1_attn_qkv_w": (3072, 1024), "L1_attn_proj_w": (1024, 1024),
    "L2_attn_W_gate": (1024, 1024),
    "L3_attn_Wq": (1024, 1024), "L3_attn_Wk": (1024, 1024),
    "L3_attn_Wv": (1024, 1024), "L3_attn_Wo": (1024, 1024),
}

KIND_CONVERTS = {
    "l1_ffn": ["L1_ffn_w_gate", "L1_ffn_w_up", "L1_ffn_w_down"],
    "l1_attn": ["L1_attn_qkv_w", "L1_attn_proj_w"],
    "l2_attn": ["L2_attn_W_gate"],
    "l3_attn": ["L3_attn_Wq", "L3_attn_Wk", "L3_attn_Wv", "L3_attn_Wo"],
}

CAST_W = {"L2_ffn_w1": (8192, 4096), "L2_ffn_w2": (32768, 1024)}
KIND_CASTS = {"l2_ffn": ["L2_ffn_w1", "L2_ffn_w2"]}


def build_chain(specs, a2a_before=None):
    nc = bass.Bass("TRN2", target_bir_lowering=False)
    xin = nc.dram_tensor("xin", [D, TOK], F32, kind="ExternalInput").ap()
    xout = nc.dram_tensor("xout", [D, TOK], F32, kind="ExternalOutput").ap()
    used = set()
    for kind, pref in specs:
        for name in INPUT_SPECS:
            if name.startswith(pref):
                used.add(name)
    ins = {}
    for name in sorted(used):
        shape, dt = INPUT_SPECS[name]
        ins[name] = nc.dram_tensor(name, shape, dt, kind="ExternalInput").ap()

    with ExitStack() as ctx:
        p = Prog(nc, ctx)
        c = Const(p)
        p.barrier()
        conv = {}
        need_conv = []
        need_cast = []
        for kind, pref in specs:
            for nm in KIND_CONVERTS.get(kind, []):
                if nm in used and nm not in conv:
                    conv[nm] = None
                    need_conv.append(nm)
            for nm in KIND_CASTS.get(kind, []):
                if nm in used and nm not in conv:
                    conv[nm] = None
                    need_cast.append(nm)
        if need_conv or need_cast:
            with p.scope():
                cw = ConvWork(p)
                for nm in need_conv:
                    R, C = CONVERT_W[nm]
                    dst = nc.dram_tensor("cw_" + nm,
                                         [R // 128, C // 128, 128, 128], BF16,
                                         kind="Internal").ap()
                    emit_convert_lhsT(p, c, cw, ins[nm], dst, R, C)
                    conv[nm] = dst
                for nm in need_cast:
                    R, C = CAST_W[nm]
                    dst = nc.dram_tensor("cw_" + nm, [R, C], BF16,
                                         kind="Internal").ap()
                    src_flat = ins[nm]
                    if len(INPUT_SPECS[nm][0]) == 3:
                        src_flat = src_flat.rearrange("n a b -> (n a) b")
                    emit_convert_cast(p, cw, src_flat, dst, R, C)
                    conv[nm] = dst
        p.barrier()
        n = len(specs)
        mids = [nc.dram_tensor(f"xmid{i}", [D, TOK], F32, kind="Internal").ap()
                for i in range(n - 1)]
        xs = [xin] + mids + [xout]
        if a2a_before is not None:
            a2a_in = nc.dram_tensor("a2a_in", [NCORES * D, TT], F32,
                                    kind="Internal").ap()
            a2a_out = nc.dram_tensor("a2a_out", [NCORES * D, TT], F32,
                                     kind="Internal").ap()
        for i, (kind, pref) in enumerate(specs):
            if a2a_before is not None and i == a2a_before:
                p.name_prefix = f"a2a_"
                emit_a2a_reshard(p, xs[i], a2a_in, a2a_out)
                p.barrier()
            p.name_prefix = f"s{i}_"
            emit_sublayer(p, c, kind, pref, xs[i], xs[i + 1], ins, conv)
            p.barrier()
        p.name_prefix = ""
        from concourse.aot_env import allow_ici_all_to_all
        with allow_ici_all_to_all():
            p.play()
    return nc, sorted(used)


def emit_a2a_reshard(p, x_ap, a2a_in, a2a_out):
    for j in range(NCORES):
        for b in range(2):
            p.inc(p.sy.dma_start(
                a2a_in[j * D:(j + 1) * D, b * 256:(b + 1) * 256],
                x_ap[:, b * 2048 + j * 256:b * 2048 + (j + 1) * 256]), "O")
    p.wait(p.g, "O", p.val("O"))
    rec = p.g.collective_compute(
        "AllToAll", ALU.bypass, replica_groups=[list(range(NCORES))],
        ins=[a2a_in.opt()], outs=[a2a_out.opt()])
    p.inc(rec, "P", 1)
    p.wait(p.sy, "P", p.val("P"))
    for i in range(NCORES):
        for b in range(2):
            p.inc(p.sy.dma_start(
                x_ap[:, (2 * i + b) * 256:(2 * i + b + 1) * 256],
                a2a_out[i * D:(i + 1) * D, b * 256:(b + 1) * 256]), "O")
    p.wait(p.sy, "O", p.val("O"))


def emit_sublayer(p, card, kind, pref, xi, xo, ins, conv):
    if kind == "l1_ffn":
        emit_l1_ffn(p, card, xi, xo, conv[pref + "w_gate"],
                    conv[pref + "w_up"], conv[pref + "w_down"])
    elif kind == "l3_ffn":
        w = {k: ins[pref + k] for k in ("U1", "V1", "b1", "U2", "V2", "b2")}
        emit_l3_ffn(p, card, xi, xo, w)
    elif kind == "l3_attn":
        w = {k: ins[pref + k] for k in ("bq", "bk", "bv", "bo", "rk_table",
                                        "rv_table")}
        emit_l3_attn(p, card, xi, xo, w, conv[pref + "Wq"], conv[pref + "Wk"],
                     conv[pref + "Wv"], conv[pref + "Wo"])
    elif kind == "l1_attn":
        w = {k: ins[pref + k] for k in ("temperature", "proj_b")}
        emit_l1_attn(p, card, xi, xo, w, conv[pref + "qkv_w"],
                     conv[pref + "proj_w"])
    elif kind == "l0_attn":
        w = {k: ins[pref + k] for k in ("basis", "q_coeffs", "k_coeffs",
                                        "v_coeffs", "o_coeffs", "decay_logit",
                                        "out_scale")}
        emit_l0_attn(p, card, xi, xo, w)
    elif kind == "l2_ffn":
        w = {k: ins[pref + k] for k in ("gate_w", "b1", "b2")}
        emit_l2_ffn(p, card, xi, xo, w, conv[pref + "w1"], conv[pref + "w2"])
    elif kind == "l2_attn":
        w = {k: ins[pref + k] for k in ("W_lambda", "b_lambda", "W_B", "W_C",
                                        "rms_weight")}
        emit_l2_attn(p, card, xi, xo, w, conv[pref + "W_gate"])
    elif kind == "l0_ffn":
        keys = ["cn_w%d" % i for i in (1, 2, 3, 4)]
        keys += ["cn_b%d" % i for i in (1, 2, 3, 4)]
        keys += ["gc_w%d" % i for i in (1, 2, 3)]
        keys += ["gc_b%d" % i for i in (1, 2, 3)]
        keys += ["adaptive_range"]
        w = {k: ins[pref + k] for k in keys}
        emit_l0_ffn(p, card, xi, xo, w)
    else:
        raise NotImplementedError(kind)


def emit_l0_ffn(p, c, xin_ap, xout_ap, w):
    with p.scope():
        pool = PsumPool(p, "l0f_ps", 4, [128, TT], F32)
        prow = p.ps("l0f_prow", [1, TT], F32)
        ln = LNState(p, "l0f_ln", pool)
        w1 = p.sb("l0f_w1", [128, KC * 1024], BF16)
        w2 = p.sb("l0f_w2", [128, KC * 512], BF16)
        w3 = p.sb("l0f_w3", [128, 4 * 256], BF16)
        w4 = p.sb("l0f_w4", [128, 2 * 1], BF16)
        g1 = p.sb("l0f_g1", [128, KC * 512], BF16)
        g2 = p.sb("l0f_g2", [128, 4 * 256], BF16)
        g3 = p.sb("l0f_g3", [128, 2 * 1], BF16)
        emit_load_w(p, w["cn_w1"], w1, 1024, 1024)
        emit_load_w(p, w["cn_w2"], w2, 1024, 512)
        emit_load_w(p, w["cn_w3"], w3, 512, 256)
        emit_load_w(p, w["cn_w4"], w4, 256, 1)
        emit_load_w(p, w["gc_w1"], g1, 1024, 512)
        emit_load_w(p, w["gc_w2"], g2, 512, 256)
        gv = emit_load_w(p, w["gc_w3"], g3, 256, 1)
        b1c = p.sb("l0f_b1", [128, 8], F32)
        b2c = p.sb("l0f_b2", [128, 4], F32)
        b3c = p.sb("l0f_b3", [128, 2], F32)
        gb1c = p.sb("l0f_gb1", [128, 4], F32)
        gb2c = p.sb("l0f_gb2", [128, 2], F32)
        smalls = p.sb("l0f_small", [1, 8], F32)
        p.inc(p.sy.dma_start(b1c[:], w["cn_b1"].rearrange("(c p) -> p c", p=128),
                             allow_slow_non_contiguous=True), "L")
        p.inc(p.sy.dma_start(b2c[:], w["cn_b2"].rearrange("(c p) -> p c", p=128),
                             allow_slow_non_contiguous=True), "L")
        p.inc(p.sy.dma_start(b3c[:], w["cn_b3"].rearrange("(c p) -> p c", p=128),
                             allow_slow_non_contiguous=True), "L")
        p.inc(p.sy.dma_start(gb1c[:], w["gc_b1"].rearrange("(c p) -> p c", p=128),
                             allow_slow_non_contiguous=True), "L")
        p.inc(p.sy.dma_start(gb2c[:], w["gc_b2"].rearrange("(c p) -> p c", p=128),
                             allow_slow_non_contiguous=True), "L")
        p.inc(p.sy.dma_start(smalls[:, 0:1], w["cn_b4"][0:1]), "L")
        p.inc(p.sy.dma_start(smalls[:, 1:2], w["gc_b3"][0:1]), "L")
        lv = p.inc(p.sy.dma_start(smalls[:, 2:3], w["adaptive_range"][0:1]), "L")

        lnf = p.sb("l0f_lnf", [128, KC * TT], F32)
        out1 = p.sb("l0f_out1", [128, KC * TT], F32)
        outb = p.sb("l0f_outb", [128, KC * TT], BF16)
        h1 = p.sb("l0f_h1", [128, KC * TT], BF16)
        h2 = p.sb("l0f_h2", [128, 4 * TT], BF16)
        h3 = p.sb("l0f_h3", [128, 2 * TT], BF16)
        rows = p.sb("l0f_rows", [1, 4 * TT], F32)
        gacc = p.sb("l0f_gacc", [128, 2 * KC], F32)
        gred = p.sb("l0f_gred", [128, KC], F32)
        gcb = p.sb("l0f_gcb", [128, 2 * KC], BF16)
        gh1 = p.sb("l0f_gh1", [128, 4 * 2], BF16)
        gh2 = p.sb("l0f_gh2", [128, 2 * 2], BF16)
        res = ResidualState(p, "l0f")
        twtmp = p.sb("l0f_twtmp", [128, TT], F32)
        lnfv = chunks_view(lnf, KC, TT)
        out1v = chunks_view(out1, KC, TT)
        outbv = chunks_view(outb, KC, TT)
        h1v = chunks_view(h1, KC, TT)
        h2v = chunks_view(h2, 4, TT)
        h3v = chunks_view(h3, 2, TT)
        w1v, w2v = w_view(w1, 1024), w_view(w2, 512)
        w3v, w4v = w_view(w3, 256), w_view(w4, 1)
        g1v, g2v, g3v = w_view(g1, 512), w_view(g2, 256), w_view(g3, 1)
        p.wait(p.t, "G", gv)
        p.wait(p.t, "L", lv)

        def row(i, w_=TT):
            return rows[:, i * TT:i * TT + w_]

        for tt in range(NTT):
            b = tt // (NTT // 2)
            lnt = emit_ln_tile(p, c, ln, xin_ap, tt, lambda kc: lnfv(kc))
            for kc in range(KC):
                p.v.reduce_sum(gred[:, kc:kc + 1], lnfv(kc), axis=AX.X)
            if tt % 4 == 0:
                vv = p.inc(p.v.tensor_copy(gacc[:, b * KC:(b + 1) * KC],
                                           gred[:]), "V")
            else:
                vv = p.inc(p.v.tensor_tensor(gacc[:, b * KC:(b + 1) * KC],
                                             gacc[:, b * KC:(b + 1) * KC],
                                             gred[:], ALU.add), "V")
            ln.last_v = vv
        src_r = bass.AP(gacc, 0, [[2 * KC, 128], [1, KC], [KC, 2]])
        dst_r = gcb[:].rearrange("p (k b) -> p k b", b=2)
        vv = p.inc(p.v.tensor_scalar_mul(dst_r, src_r, 1.0 / 2048.0), "V")
        p.wait(p.t, "V", vv)

        def gc_rhs(kc):
            return gcb[:, kc * 2:kc * 2 + 2]
        for fc in range(4):
            ps_ap, slot = pool.acquire()
            rec = None
            for kc in range(KC):
                rec = p.t.matmul(ps_ap[:, 0:2], g1v(kc, fc), gc_rhs(kc),
                                 start=(kc == 0), stop=(kc == KC - 1))
            tvv = p.inc(rec, "T")
            p.wait(p.sc, "T", tvv)
            sv = p.inc(p.sc.activation(gh1[:, fc * 2:fc * 2 + 2], ps_ap[:, 0:2],
                                       AF.Gelu, bias=gb1c[:, fc:fc + 1]), "S")
            pool.release(slot, "S", sv)

        def gh1_rhs(kc):
            return bass.AP(gh1, kc * 2, [[8, 128], [1, 2]])
        p.wait(p.t, "S", p.val("S"))
        for fc in range(2):
            ps_ap, slot = pool.acquire()
            rec = None
            for kc in range(4):
                rec = p.t.matmul(ps_ap[:, 0:2], g2v(kc, fc), gh1_rhs(kc),
                                 start=(kc == 0), stop=(kc == 3))
            tvv = p.inc(rec, "T")
            p.wait(p.sc, "T", tvv)
            sv = p.inc(p.sc.activation(gh2[:, fc * 2:fc * 2 + 2], ps_ap[:, 0:2],
                                       AF.Gelu, bias=gb2c[:, fc:fc + 1]), "S")
            pool.release(slot, "S", sv)

        def gh2_rhs(kc):
            return bass.AP(gh2, kc * 2, [[4, 128], [1, 2]])
        p.wait(p.t, "S", p.val("S"))
        rec = None
        for kc in range(2):
            rec = p.t.matmul(prow[:, 0:2], g3v(kc, 0, 1), gh2_rhs(kc),
                             start=(kc == 0), stop=(kc == 1))
        tvv = p.inc(rec, "T")
        p.wait(p.sc, "T", tvv)
        sv = p.inc(p.sc.activation(smalls[:, 4:6], prow[:, 0:2], AF.Sigmoid,
                                   bias=smalls[:, 1:2]), "S")
        p.wait(p.v, "S", sv)
        p.v.tensor_scalar_mul(smalls[:, 4:6], smalls[:, 4:6], 0.7)
        vv = p.inc(p.v.tensor_scalar_mul(smalls[:, 3:4], smalls[:, 2:3], 2.0),
                   "V")

        TWIST = [(4, 1.0), (5, 1.0), (6, -1.0), (7, -1.0),
                 (0, -1.0), (1, -1.0), (2, 1.0), (3, 1.0)]
        p.wait(p.sc, "V", vv)
        for tt in range(NTT):
            b = tt // (NTT // 2)
            lnt = emit_ln_tile(p, c, ln, xin_ap, tt, lambda kc: lnfv(kc))
            cur, nxt = lnf, out1
            for cyc in range(3):
                curv = chunks_view(cur, KC, TT)
                nxtv = chunks_view(nxt, KC, TT)
                for kc in range(KC):
                    vv = p.inc(p.v.tensor_copy(outbv(kc), curv(kc)), "V")
                p.wait(p.t, "V", vv)

                def epi1(fc, _tt, ps_ap, slot, tval):
                    p.wait(p.sc, "T", tval)
                    sv = p.inc(p.sc.activation(h1v(fc), ps_ap[:], AF.Gelu,
                                               bias=b1c[:, fc:fc + 1]), "S")
                    pool.release(slot, "S", sv)
                emit_proj(p, w1v, KC, lambda kc, _: outbv(kc), 1024, epi1,
                          pool, tts=[tt])
                p.wait(p.t, "S", p.val("S"))

                def epi2(fc, _tt, ps_ap, slot, tval):
                    p.wait(p.sc, "T", tval)
                    sv = p.inc(p.sc.activation(h2v(fc), ps_ap[:], AF.Gelu,
                                               bias=b2c[:, fc:fc + 1]), "S")
                    pool.release(slot, "S", sv)
                emit_proj(p, w2v, KC, lambda kc, _: h1v(kc), 512, epi2,
                          pool, tts=[tt])
                p.wait(p.t, "S", p.val("S"))

                def epi3(fc, _tt, ps_ap, slot, tval):
                    p.wait(p.sc, "T", tval)
                    sv = p.inc(p.sc.activation(h3v(fc), ps_ap[:], AF.Gelu,
                                               bias=b3c[:, fc:fc + 1]), "S")
                    pool.release(slot, "S", sv)
                emit_proj(p, w3v, 4, lambda kc, _: h2v(kc), 256, epi3,
                          pool, tts=[tt])
                p.wait(p.t, "S", p.val("S"))

                rec = None
                for kc in range(2):
                    rec = p.t.matmul(prow[:], w4v(kc, 0, 1), h3v(kc),
                                     start=(kc == 0), stop=(kc == 1))
                tvv = p.inc(rec, "T")
                p.wait(p.sc, "T", tvv)
                sv = p.inc(p.sc.activation(row(0), prow[:], AF.Sigmoid,
                                           bias=smalls[:, 0:1]), "S")
                p.wait(p.v, "S", sv)
                p.v.tensor_scalar(row(2), row(0), 0.3, smalls[:, 4 + b:5 + b],
                                  ALU.mult, ALU.add)
                p.v.tensor_scalar(row(2), row(2), -0.5, None, ALU.add)
                vv = p.inc(p.v.tensor_scalar(row(1), row(2),
                                             smalls[:, 3:4], 0.1,
                                             ALU.mult, ALU.add), "V")
                p.wait(p.t, "V", vv)
                bcp, bslot = pool.acquire()
                tvv = p.inc(p.t.matmul(bcp[:], c.ones_row_f[:], row(1),
                                       start=True, stop=True), "T")
                p.wait(p.v, "T", tvv)
                for kc in range(KC):
                    srckc, sgn = TWIST[kc]
                    op = ALU.add if sgn > 0 else ALU.subtract
                    p.v.tensor_tensor(twtmp[:], bcp[:], curv(srckc), ALU.mult)
                    vv = p.inc(p.v.tensor_tensor(nxtv(kc), curv(kc), twtmp[:],
                                                 op), "V")
                pool.release(bslot, "V", vv)
                cur, nxt = nxt, (lnf if cur is out1 else out1)
            curv = chunks_view(cur, KC, TT)
            for kc in range(KC):
                slot = res.i % res.n
                res.i += 1
                buf = res.buf[slot]
                if res.guard[slot] is not None:
                    p.wait(p.sy, "O", res.guard[slot])
                lv = p.inc(p.sy.dma_start(buf[:, 0:TT],
                                          feat_dram_view(xin_ap, kc, tt * TT,
                                                         TT)), "L")
                p.wait(p.v, "L", lv)
                vv = p.inc(p.v.scalar_tensor_tensor(buf[:, TT:2 * TT],
                                                    curv(kc), 0.5,
                                                    buf[:, 0:TT], ALU.mult,
                                                    ALU.add), "V")
                p.wait(p.sy, "V", vv)
                ov = p.inc(p.sy.dma_start(feat_dram_view(xout_ap, kc, tt * TT,
                                                         TT),
                                          buf[:, TT:2 * TT]), "O")
                res.guard[slot] = ov
            ln.last_v = p.val("V")
            p.wait(p.v, "T", p.val("T"))



def emit_l2_attn(p, c, xin_ap, xout_ap, w, cw_gate):
    N = 16
    SEQ = 2048
    with p.scope():
        pool = PsumPool(p, "l2a_ps", 3, [128, TT], F32)
        prow = p.ps("l2a_prow", [1, TT], F32)
        tp = TransPool(p, "l2a_tp")
        ln = LNState(p, "l2a_ln", pool)
        wgate = p.sb("l2a_wg", [128, KC * KC * 128], BF16)
        emit_load_wconv(p, cw_gate, wgate, KC, range(KC), "Q")
        wgv = wconv_view(wgate, KC)
        wl_f = p.sb("l2a_wlf", [16, 1024], F32)
        wl_b = p.sb("l2a_wlb", [16, 1024], BF16)
        wlamT = p.sb("l2a_wlT", [128, KC * N], BF16)
        wbT = p.sb("l2a_wbT", [128, KC * N], BF16)
        wc_f = p.sb("l2a_wcf", [128, KC * N], F32)
        wc_b = p.sb("l2a_wcb", [128, KC * N], BF16)
        wcT = p.sb("l2a_wcT", [16, KC * 128], BF16)
        bcol = p.sb("l2a_bcol", [16, 1], F32)
        rmsw = p.sb("l2a_rmsw", [128, KC], F32)
        lv = p.inc(p.sy.dma_start(bcol[:],
                                  w["b_lambda"].rearrange("(p o) -> p o", o=1)),
                   "L")
        lv = p.inc(p.sy.dma_start(rmsw[:],
                                  w["rms_weight"].rearrange("(c p) -> p c",
                                                            p=128),
                                  allow_slow_non_contiguous=True), "L")
        for (wf, wname) in ((wl_f, "W_lambda"),):
            pass
        lv = p.inc(p.sy.dma_start(wl_f[:], w["W_lambda"]), "L")
        lv = p.inc(p.sy.dma_start(
            wc_f[:].rearrange("p (c n) -> p c n", n=N),
            w["W_C"].rearrange("(c p) n -> p c n", p=128)), "L")
        p.wait(p.v, "L", lv)
        vv = p.inc(p.v.tensor_copy(wl_b[:], wl_f[:]), "V")
        vv = p.inc(p.v.tensor_copy(wc_b[:], wc_f[:]), "V")
        p.wait(p.t, "V", vv)
        tprev = []
        for kc in range(KC):
            tp.step(wl_b[:, kc * 128:(kc + 1) * 128], c.ident_b[0:16, 0:16],
                    wlamT[:, kc * N:(kc + 1) * N], out_rows=128, out_cols=16)
        p.wait(p.sy, "V", p.val("V"))
        lv = p.inc(p.sy.dma_start(wl_f[:], w["W_B"]), "L")
        p.wait(p.v, "L", lv)
        p.wait(p.v, "T", p.val("T"))
        vv = p.inc(p.v.tensor_copy(wl_b[:], wl_f[:]), "V")
        p.wait(p.t, "V", vv)
        for kc in range(KC):
            tp.step(wl_b[:, kc * 128:(kc + 1) * 128], c.ident_b[0:16, 0:16],
                    wbT[:, kc * N:(kc + 1) * N], out_rows=128, out_cols=16)
        for kc in range(KC):
            tp.step(wc_b[:, kc * N:(kc + 1) * N], c.ident_b[:],
                    wcT[:, kc * 128:(kc + 1) * 128], out_rows=16,
                    out_cols=128)
        p.wait(p.t, "V", p.val("V"))

        lnfull = p.sb("l2a_lnfull", [128, KC * TOK], BF16)
        lam = p.sb("l2a_lam", [16, TOK], F32)
        bx = p.sb("l2a_bx", [16, TOK], F32)
        hsb = p.sb("l2a_hsb", [16, TOK], BF16)
        gateb = p.sb("l2a_gate", [128, KC * TT], BF16)
        ygb = p.sb("l2a_yg", [128, KC * TT], F32)
        res = ResidualState(p, "l2a")
        lnv = chunks_view(lnfull, KC, TOK)
        gv_ = chunks_view(gateb, KC, TT)
        ygv = chunks_view(ygb, KC, TT)

        for tt in range(NTT):
            lnt = emit_ln_tile(p, c, ln, xin_ap, tt,
                               lambda kc, _t=tt: lnv(kc, _t * TT, TT))
            p.wait(p.t, "V", lnt)
            for (wT, dst, act) in ((wlamT, lam, "sig"), (wbT, bx, "copy")):
                ps_ap, slot = pool.acquire()
                rec = None
                for kc in range(KC):
                    rec = p.t.matmul(ps_ap[0:16, :], wT[:, kc * N:(kc + 1) * N],
                                     lnv(kc, tt * TT, TT), start=(kc == 0),
                                     stop=(kc == KC - 1))
                tvv = p.inc(rec, "T")
                p.wait(p.sc, "T", tvv)
                if act == "sig":
                    sv = p.inc(p.sc.activation(dst[:, tt * TT:(tt + 1) * TT],
                                               ps_ap[0:16, :], AF.Sigmoid,
                                               bias=bcol[:]), "S")
                else:
                    sv = p.inc(p.sc.activation(dst[:, tt * TT:(tt + 1) * TT],
                                               ps_ap[0:16, :], AF.Copy), "S")
                pool.release(slot, "S", sv)
        p.wait(p.v, "S", p.val("S"))
        for b in range(2):
            sl = slice(b * SEQ, (b + 1) * SEQ)
            vv = p.inc(p.v.tensor_tensor_scan(hsb[:, sl], lam[:, sl],
                                              bx[:, sl], 0.0, ALU.mult,
                                              ALU.add), "V")
        hs_done = vv
        p.wait(p.t, "V", hs_done)
        prow_guard = [0]
        for tt in range(NTT):
            def epi_g(fc, _tt, ps_ap, slot, tval):
                p.wait(p.sc, "T", tval)
                sv = p.inc(p.sc.activation(gv_(fc), ps_ap[:], AF.Sigmoid), "S")
                pool.release(slot, "S", sv)
            emit_proj(p, wgv, KC, lambda kc, _t=tt: lnv(kc, _t * TT, TT), D,
                      epi_g, pool, tts=[tt])
            p.wait(p.t, "S", p.val("S"))
            for fc in range(KC):
                ps_ap, slot = pool.acquire()
                tvv = p.inc(p.t.matmul(ps_ap[:],
                                       wcT[:, fc * 128:(fc + 1) * 128],
                                       hsb[:, tt * TT:(tt + 1) * TT],
                                       start=True, stop=True), "T")
                p.wait(p.v, "T", tvv)
                vv = p.inc(p.v.tensor_tensor(ygv(fc), ps_ap[:], gv_(fc),
                                             ALU.mult), "V")
                pool.release(slot, "V", vv)
            p.wait(p.sc, "V", p.val("V"))
            tq = {}
            for kc in range(KC):
                sqv = ln.sq[:, (kc % 2) * TT:(kc % 2) * TT + TT]
                if kc >= 2:
                    p.wait(p.sc, "T", tq[kc - 2])
                sk = p.inc(p.sc.activation(sqv, ygv(kc), AF.Square), "S")
                p.wait(p.t, "S", sk)
                if kc == 0 and prow_guard[0]:
                    p.wait(p.t, "V", prow_guard[0])
                tq[kc] = p.inc(p.t.matmul(prow[:], c.ones_col_f[:], sqv,
                                          start=(kc == 0), stop=(kc == KC - 1)),
                               "T")
            tsum = p.val("T")
            p.wait(p.v, "T", tsum)
            p.v.tensor_scalar_mul(ln.row(0), prow[:], 1.0 / D)
            vv = p.inc(p.v.tensor_copy(ln.row(1), ln.row(0)), "V")
            prow_guard[0] = vv
            p.wait(p.sc, "V", vv)
            sv = p.inc(p.sc.activation(ln.row(2), ln.row(1), AF.Sqrt,
                                       bias=c.one(1e-6)), "S")
            p.wait(p.v, "S", sv)
            vv = p.inc(p.v.reciprocal(ln.row(3), ln.row(2)), "V")
            p.wait(p.t, "V", vv)
            bcp, bslot = pool.acquire()
            tvv = p.inc(p.t.matmul(bcp[:], c.ones_row_f[:], ln.row(3),
                                   start=True, stop=True), "T")
            p.wait(p.v, "T", tvv)
            for kc in range(KC):
                slot = res.i % res.n
                res.i += 1
                buf = res.buf[slot]
                if res.guard[slot] is not None:
                    p.wait(p.sy, "O", res.guard[slot])
                lv = p.inc(p.sy.dma_start(buf[:, 0:TT],
                                          feat_dram_view(xin_ap, kc, tt * TT,
                                                         TT)), "L")
                p.wait(p.v, "L", lv)
                p.v.tensor_tensor(buf[:, 2 * TT:3 * TT], ygv(kc), bcp[:],
                                  ALU.mult)
                vv = p.inc(p.v.scalar_tensor_tensor(buf[:, TT:2 * TT],
                                                    buf[:, 2 * TT:3 * TT],
                                                    rmsw[:, kc:kc + 1],
                                                    buf[:, 0:TT], ALU.mult,
                                                    ALU.add), "V")
                p.wait(p.sy, "V", vv)
                ov = p.inc(p.sy.dma_start(feat_dram_view(xout_ap, kc, tt * TT,
                                                         TT),
                                          buf[:, TT:2 * TT]), "O")
                res.guard[slot] = ov
            pool.release(bslot, "V", p.val("V"))
            p.wait(p.sc, "T", p.val("T"))
            p.wait(p.v, "T", p.val("T"))


def emit_l2_ffn(p, c, xin_ap, xout_ap, w, cw1, cw2):
    NE, FF = 8, 4096
    NFF = FF // 128
    nc = p.nc
    lnhbm = nc.dram_tensor(p.name_prefix + "lnhbm", [D, TOK], BF16,
                           kind="Internal").ap()
    acc = nc.dram_tensor(p.name_prefix + "acc", [D, TOK], F32,
                         kind="Internal").ap()
    _dbg.update({"acc": acc, "lnhbm": lnhbm})
    with p.scope():
        wrow = p.sb("wrow", [NE, TOK], F32)
        _dbg["wrow"] = wrow
        b2sb = p.sb("b2sb", [NE, D], F32)
        b1sb = p.sb("b1sb", [128, NE * NFF], F32)
        lv = p.inc(p.sy.dma_start(b2sb[:], w["b2"]), "L")
        lv = p.inc(p.sy.dma_start(
            b1sb[:].rearrange("p (n c) -> p n c", n=NE),
            w["b1"].rearrange("n (c p) -> p n c", p=128),
            allow_slow_non_contiguous=True), "L")

        with p.scope():
            pool = PsumPool(p, "ps1", 4, [128, TT], F32)
            ln = LNState(p, "ln", pool)
            lnb = p.sb("lnb", [128, KC * TT], BF16)
            gw_f = p.sb("gwf", [NE, D], F32)
            gw_b = p.sb("gwb", [NE, D], BF16)
            gwT = p.sb("gwT", [128, KC * NE], BF16)
            tpg = TransPool(p, "tpg")
            grow = p.sb("grow", [128, 4 * NE], F32)
            gcol = p.sb("gcol", [128, 4], F32)
            lv2 = p.inc(p.sy.dma_start(gw_f[:], w["gate_w"]), "L")
            p.wait(p.v, "L", lv2)
            vv = p.inc(p.v.tensor_copy(gw_b[:], gw_f[:]), "V")
            p.wait(p.t, "V", vv)
            for kc in range(KC):
                tpg.step(gw_b[:, kc * 128:(kc + 1) * 128],
                         c.ident_b[0:NE, 0:NE],
                         gwT[:, kc * NE:(kc + 1) * NE], out_rows=128,
                         out_cols=NE)
            p.wait(p.t, "V", p.val("V"))
            lnv = chunks_view(lnb, KC, TT)
            P_, GE = grow[:, 0:NE], grow[:, NE:2 * NE]
            MK, WW = grow[:, 2 * NE:3 * NE], grow[:, 3 * NE:4 * NE]
            for tt in range(NTT):
                lnt = emit_ln_tile(p, c, ln, xin_ap, tt, lambda kc: lnv(kc))
                p.wait(p.sy, "V", lnt)
                dst = lnhbm.rearrange("(c p) t -> p c t",
                                      p=128)[:, :, tt * TT:(tt + 1) * TT]
                p.inc(p.sy.dma_start(dst,
                                     lnb[:].rearrange("p (c t) -> p c t",
                                                      c=KC)), "O")
                p.wait(p.t, "V", lnt)
                for tc in range(TT // 128):
                    gtc = tt * 4 + tc
                    ps_ap, slot = pool.acquire()
                    rec = None
                    for kc in range(KC):
                        rec = p.t.matmul(ps_ap[:, 0:NE], lnv(kc, tc * 128, 128),
                                         gwT[:, kc * NE:(kc + 1) * NE],
                                         start=(kc == 0), stop=(kc == KC - 1))
                    tvv = p.inc(rec, "T")
                    p.wait(p.v, "T", tvv)
                    vv = p.inc(p.v.reduce_max(gcol[:, 0:1], ps_ap[:, 0:NE],
                                              axis=AX.X, negate=True), "V")
                    p.wait(p.sc, "V", vv)
                    sv = p.inc(p.sc.activation(P_, ps_ap[:, 0:NE], AF.Exp,
                                               bias=gcol[:, 0:1]), "S")
                    pool.release(slot, "S", sv)
                    p.wait(p.v, "S", sv)
                    p.v.reduce_sum(gcol[:, 1:2], P_, axis=AX.X)
                    p.v.reciprocal(gcol[:, 2:3], gcol[:, 1:2])
                    p.v.tensor_scalar_mul(P_, P_, gcol[:, 2:3])
                    p.v.reduce_max(gcol[:, 0:1], P_, axis=AX.X)
                    p.v.tensor_scalar(GE, P_, gcol[:, 0:1], None, ALU.is_equal)
                    p.v.scalar_tensor_tensor(MK, GE, -2.0, P_, ALU.mult,
                                             ALU.add)
                    p.v.reduce_max(gcol[:, 0:1], MK, axis=AX.X)
                    p.v.tensor_scalar(GE, P_, gcol[:, 0:1], None, ALU.is_ge)
                    p.v.tensor_tensor(WW, P_, GE, ALU.mult)
                    p.v.reduce_sum(gcol[:, 1:2], WW, axis=AX.X)
                    p.v.tensor_scalar(gcol[:, 1:2], gcol[:, 1:2], 1e-9, None,
                                      ALU.add)
                    p.v.reciprocal(gcol[:, 2:3], gcol[:, 1:2])
                    vv = p.inc(p.v.tensor_scalar_mul(WW, WW, gcol[:, 2:3]),
                               "V")
                    p.wait(p.t, "V", vv)
                    ps2, slot2 = pool.acquire()
                    tvv = p.inc(p.t.transpose(ps2[0:NE, 0:128], WW,
                                              c.ident_f[:]), "T")
                    p.wait(p.v, "T", tvv)
                    vv = p.inc(p.v.tensor_copy(
                        wrow[:, gtc * 128:(gtc + 1) * 128],
                        ps2[0:NE, 0:128]), "V")
                    pool.release(slot2, "V", vv)
                    p.wait(p.t, "V", vv)
                p.wait(p.v, "T", p.val("T"))

        with p.scope():
            pool = PsumPool(p, "ps2", 7, [128, TT], F32)
            ebc = p.sb("ebc", [NE, NE * 128], F32)
            ebc_np = np.zeros((NE, NE * 128), np.float32)
            for n_ in range(NE):
                ebc_np[n_, n_ * 128:(n_ + 1) * 128] = 1.0
            h_ = p.nc.inline_tensor(ebc_np, p.name_prefix + "ebc_c")
            p.wait(p.sy, "V", p.val("V"))
            ebv = p.inc(p.sy.dma_start(ebc[:], h_.ap()), "L")
            p.wait(p.t, "L", ebv)
            w1sb = p.sb("w1sb", [128, KC * FF], BF16)
            w2sb = p.sb("w2sb", [128, NFF * D], BF16)
            lnt2 = [p.sb("lnt0", [128, KC * TT], BF16)]
            hbuf = p.sb("hbuf", [128, NFF * TT], BF16)
            wnb = p.sb("wnb", [128, TT], F32)
            outsb = [p.sb(f"outsb{i}", [128, TT], F32) for i in range(4)]
            w1v, w2v = w_view(w1sb, FF), w_view(w2sb, D)
            hv = chunks_view(hbuf, NFF, TT)
            ln_guard = [0, 0]
            out_guard = [0, 0, 0, 0]
            acc_guard = {}
            w1_guard = 0
            w2_guard = 0
            for n in range(NE):
                if w1_guard:
                    p.wait(p.sc, "T", w1_guard)
                qv1 = 0
                for kc in range(KC):
                    qv1 = p.inc(p.sc.dma_start(
                        w1sb[:, kc * FF:(kc + 1) * FF],
                        cw1[n * 1024 + kc * 128:n * 1024 + (kc + 1) * 128, :]),
                        "Q")
                if w2_guard:
                    p.wait(p.sc, "T", w2_guard)
                qv = 0
                for kc in range(NFF):
                    qv = p.inc(p.sc.dma_start(
                        w2sb[:, kc * D:(kc + 1) * D],
                        cw2[n * FF + kc * 128:n * FF + (kc + 1) * 128, :]),
                        "Q")
                p.wait(p.t, "Q", qv1)
                first_tt = [True]
                for tt in range(NTT):
                    par = 0
                    if ln_guard[par]:
                        p.wait(p.sy, "T", ln_guard[par])
                    src = lnhbm.rearrange("(c p) t -> p c t",
                                          p=128)[:, :, tt * TT:(tt + 1) * TT]
                    lv = p.inc(p.sy.dma_start(
                        lnt2[par][:].rearrange("p (c t) -> p c t", c=KC), src),
                        "L")
                    lv2 = chunks_view(lnt2[par], KC, TT)
                    ps_b, sl_b = pool.acquire()
                    tvv = p.inc(p.t.matmul(ps_b[:],
                                           ebc[:, n * 128:(n + 1) * 128],
                                           wrow[:, tt * TT:(tt + 1) * TT],
                                           start=True, stop=True), "T")
                    p.wait(p.v, "T", tvv)
                    vv = p.inc(p.v.tensor_copy(wnb[:], ps_b[:]), "V")
                    pool.release(sl_b, "V", vv)
                    p.wait(p.t, "L", lv)
                    for fc in range(NFF):
                        psh, slh = pool.acquire()
                        rec = None
                        for kc in range(KC):
                            rec = p.t.matmul(psh[:], w1v(kc, fc), lv2(kc),
                                             start=(kc == 0),
                                             stop=(kc == KC - 1))
                        tvv = p.inc(rec, "T")
                        p.wait(p.sc, "T", tvv)
                        sv = p.inc(p.sc.activation(
                            hv(fc), psh[:], AF.Gelu,
                            bias=b1sb[:, n * NFF + fc:n * NFF + fc + 1]), "S")
                        pool.release(slh, "S", sv)
                    ln_guard[par] = p.val("T")
                    w1_guard = p.val("T")
                    p.wait(p.t, "S", p.val("S"))
                    if first_tt[0]:
                        p.wait(p.t, "Q", qv)
                        first_tt[0] = False
                    for fc in range(KC):
                        pso, slo = pool.acquire()
                        rec = None
                        for kc in range(NFF):
                            rec = p.t.matmul(pso[:], w2v(kc, fc), hv(kc),
                                             start=(kc == 0),
                                             stop=(kc == NFF - 1))
                        tvv = p.inc(rec, "T")
                        osl = fc % 4
                        if out_guard[osl]:
                            p.wait(p.v, "G", out_guard[osl])
                        p.wait(p.v, "T", tvv)
                        vv = p.inc(p.v.tensor_tensor(outsb[osl][:], pso[:],
                                                     wnb[:], ALU.mult), "V")
                        pool.release(slo, "V", vv)
                        p.wait(p.g, "V", vv)
                        key = (fc, tt)
                        if key in acc_guard:
                            p.wait(p.g, "G", acc_guard[key])
                        gv = p.inc(p.g.dma_start(
                            feat_dram_view(acc, fc, tt * TT, TT),
                            outsb[osl][:],
                            accum_op=(ALU.add if n > 0 else ALU.bypass)), "G")
                        out_guard[osl] = gv
                        acc_guard[key] = gv
                    w2_guard = p.val("T")
            final_g = p.val("G")

        with p.scope():
            pool3 = PsumPool(p, "ps3", 2, [128, TT], F32)
            res = ResidualState(p, "res")
            accb = [p.sb(f"accb{i}", [128, TT], F32) for i in range(2)]
            acc_g = [0, 0]
            p.wait(p.sy, "G", final_g)
            p.wait(p.t, "L", p.val("L"))
            for tt in range(NTT):
                for fc in range(KC):
                    par = (tt * KC + fc) % 2
                    if acc_g[par]:
                        p.wait(p.sy, "V", acc_g[par])
                    lv = p.inc(p.sy.dma_start(accb[par][:],
                                              feat_dram_view(acc, fc, tt * TT,
                                                             TT)), "L")
                    ps_ap, slot = pool3.acquire()
                    tvv = p.inc(p.t.matmul(
                        ps_ap[:], b2sb[:, fc * 128:(fc + 1) * 128],
                        wrow[:, tt * TT:(tt + 1) * TT], start=True, stop=True),
                        "T")
                    slot_r = res.i % res.n
                    res.i += 1
                    buf = res.buf[slot_r]
                    if res.guard[slot_r] is not None:
                        p.wait(p.sy, "O", res.guard[slot_r])
                    lv = p.inc(p.sy.dma_start(
                        buf[:, 0:TT], feat_dram_view(xin_ap, fc, tt * TT, TT)),
                        "L")
                    p.wait(p.v, "L", lv)
                    p.wait(p.v, "T", tvv)
                    p.v.tensor_tensor(buf[:, 2 * TT:3 * TT], accb[par][:],
                                      ps_ap[:], ALU.add)
                    vv = p.inc(p.v.scalar_tensor_tensor(
                        buf[:, TT:2 * TT], buf[:, 2 * TT:3 * TT], 0.5,
                        buf[:, 0:TT], ALU.mult, ALU.add), "V")
                    pool3.release(slot, "V", vv)
                    acc_g[par] = vv
                    p.wait(p.sy, "V", vv)
                    ov = p.inc(p.sy.dma_start(
                        feat_dram_view(xout_ap, fc, tt * TT, TT),
                        buf[:, TT:2 * TT]), "O")
                    res.guard[slot_r] = ov


def emit_l0_attn(p, c, xin_ap, xout_ap, w, prep_only=False):
    CH = 256
    NC2 = 2
    SEQ = 2048
    NKT = SEQ // 128
    NQT = SEQ // TT
    with p.scope():
        pool = PsumPool(p, "l0a_ps", 3, [128, TT], F32)
        ln = LNState(p, "l0a_ln", pool)
        qw = p.sb("l0a_qw", [128, KC * CH], BF16)
        kw = p.sb("l0a_kw", [128, KC * CH], BF16)
        vw = p.sb("l0a_vw", [128, KC * CH], BF16)
        owT = p.sb("l0a_owT", [128, NC2 * D], BF16)
        smalls = p.sb("l0a_small", [1, 4], F32)
        lcol = p.sb("l0a_lcol", [128, 1], F32)
        oscol = p.sb("l0a_oscol", [128, 1], F32)
        prep = p.scope() if not prep_only else None
        if prep is not None:
            prep.__enter__()
        tp = TransPool(p, "l0a_tp")
        bsb = p.sb("l0a_bsb", [128, KC * CH], F32)
        bsb_b = p.sb("l0a_bsbb", [128, KC * CH], BF16)
        basisT = p.sb("l0a_bT", [128, NC2 * D], BF16)
        coefT = p.sb("l0a_cT", [128, 4 * NC2 * CH], BF16)
        cf = p.sb("l0a_cf", [128, NC2 * CH], F32)
        cfb = p.sb("l0a_cfb", [128, NC2 * CH], BF16)
        lv = p.inc(p.sy.dma_start(
            bsb[:].rearrange("p (c n) -> p c n", n=CH),
            w["basis"].rearrange("(c p) n -> p c n", p=128)), "L")
        p.wait(p.v, "L", lv)
        vv = p.inc(p.v.tensor_copy(bsb_b[:], bsb[:]), "V")
        p.wait(p.t, "V", vv)
        for kc in range(KC):
            for cc in range(NC2):
                tp.step(bsb_b[:, kc * CH + cc * 128:kc * CH + (cc + 1) * 128],
                        c.ident_b[:],
                        basisT[:, cc * D + kc * 128:cc * D + (kc + 1) * 128])
        for i, nm in enumerate(("q_coeffs", "k_coeffs", "v_coeffs", "o_coeffs")):
            p.wait(p.sy, "V", p.val("V"))
            lv = p.inc(p.sy.dma_start(
                cf[:].rearrange("p (c n) -> p c n", n=CH),
                w[nm].rearrange("(c p) n -> p c n", p=128)), "L")
            p.wait(p.v, "L", lv)
            vv = p.inc(p.v.tensor_copy(cfb[:], cf[:]), "V")
            p.wait(p.t, "V", vv)
            for kc in range(NC2):
                for cc in range(NC2):
                    tp.step(cfb[:, kc * CH + cc * 128:kc * CH + (cc + 1) * 128],
                            c.ident_b[:],
                            coefT[:, i * NC2 * CH + cc * CH + kc * 128:
                                  i * NC2 * CH + cc * CH + (kc + 1) * 128])
        p.wait(p.t, "V", p.val("V"))
        for i, dstw in enumerate((qw, kw, vw)):
            for vc in range(KC):
                ps_ap, slot = pool.acquire()
                rec = None
                for bc in range(NC2):
                    rec = p.t.matmul(
                        ps_ap[:, 0:CH],
                        basisT[:, bc * D + vc * 128:bc * D + (vc + 1) * 128],
                        coefT[:, i * NC2 * CH + bc * CH:(i * NC2 + bc + 1) * CH],
                        start=(bc == 0), stop=(bc == NC2 - 1))
                tvv = p.inc(rec, "T")
                p.wait(p.v, "T", tvv)
                vv = p.inc(p.v.tensor_copy(dstw[:, vc * CH:(vc + 1) * CH],
                                           ps_ap[:, 0:CH]), "V")
                pool.release(slot, "V", vv)
        for ccO in range(NC2):
            for vs in range(2):
                ps_ap, slot = pool.acquire()
                rec = None
                for bc in range(NC2):
                    rec = p.t.matmul(
                        ps_ap[:],
                        coefT[:, 3 * NC2 * CH + bc * CH + ccO * 128:
                              3 * NC2 * CH + bc * CH + (ccO + 1) * 128],
                        basisT[:, bc * D + vs * TT:bc * D + (vs + 1) * TT],
                        start=(bc == 0), stop=(bc == NC2 - 1))
                tvv = p.inc(rec, "T")
                p.wait(p.v, "T", tvv)
                vv = p.inc(p.v.tensor_copy(
                    owT[:, ccO * D + vs * TT:ccO * D + (vs + 1) * TT],
                    ps_ap[:]), "V")
                pool.release(slot, "V", vv)
        p.wait(p.sy, "V", p.val("V"))
        p.wait(p.sy, "S", p.val("S"))
        lv = p.inc(p.sy.dma_start(smalls[:, 0:1], w["decay_logit"][0:1]), "L")
        lv = p.inc(p.sy.dma_start(smalls[:, 1:2], w["out_scale"][0:1]), "L")
        p.wait(p.sc, "L", lv)
        sv = p.inc(p.sc.activation(smalls[:, 2:3], smalls[:, 0:1], AF.Sigmoid),
                   "S")
        p.wait(p.v, "S", sv)
        vv = p.inc(p.v.tensor_scalar_max(smalls[:, 2:3], smalls[:, 2:3],
                                         1e-38), "V")
        p.wait(p.sc, "V", vv)
        sv = p.inc(p.sc.activation(smalls[:, 2:3], smalls[:, 2:3], AF.Ln), "S")
        p.wait(p.t, "S", sv)
        ps_ap, slot = pool.acquire()
        p.t.matmul(ps_ap[:, 0:1], c.ones_row_f[:], smalls[:, 2:3], start=True,
                   stop=True)
        tvv = p.inc(p.t.matmul(ps_ap[:, 1:2], c.ones_row_f[:], smalls[:, 1:2],
                               start=False, stop=True,
                               skip_group_check=True), "T")
        p.wait(p.v, "T", tvv)
        p.v.tensor_copy(lcol[:], ps_ap[:, 0:1])
        vv = p.inc(p.v.tensor_copy(oscol[:], ps_ap[:, 1:2]), "V")
        pool.release(slot, "V", vv)
        if prep_only:
            _dbg.update({"coefT": coefT, "basisT": basisT, "owT": owT})
            return
        prep.__exit__(None, None, None)

        lnfull = p.sb("l0a_lnfull", [128, KC * TOK], BF16)
        lnv = chunks_view(lnfull, KC, TOK)
        for tt in range(NTT):
            emit_ln_tile(p, c, ln, xin_ap, tt,
                         lambda kc, _t=tt: lnv(kc, _t * TT, TT))
        qb = p.sb("l0a_qb", [128, NC2 * TOK], BF16)
        kb = p.sb("l0a_kb", [128, NC2 * TOK], BF16)
        vb = p.sb("l0a_vb", [128, NC2 * TOK], BF16)
        vtok = p.sb("l0a_vtok", [128, (TOK // 128) * CH], BF16)
        _dbg.update({"qw": qw, "kw": kw, "vw": vw, "owT": owT, "qb": qb,
                     "kb": kb, "vtok": vtok, "lcol": lcol, "oscol": oscol})
        p.wait(p.t, "V", p.val("V"))
        for (wsb, dst) in ((qw, qb), (kw, kb), (vw, vb)):
            wv_ = w_view(wsb, CH)

            def epi(fc, tt_, ps_ap2, slot2, tval, _dst=dst):
                p.wait(p.sc, "T", tval)
                sv2 = p.inc(p.sc.activation(
                    _dst[:, fc * TOK + tt_ * TT:fc * TOK + (tt_ + 1) * TT],
                    ps_ap2[:], AF.Copy), "S")
                pool.release(slot2, "S", sv2)
            emit_proj(p, wv_, KC, lambda kc, _t: lnv(kc, _t * TT, TT), CH,
                      epi, pool)
        p.wait(p.t, "S", p.val("S"))
        with p.scope():
            tp2 = TransPool(p, "l0a_tp2")
            for t128 in range(TOK // 128):
                for cc in range(NC2):
                    tp2.step(
                        vb[:, cc * TOK + t128 * 128:cc * TOK + (t128 + 1) * 128],
                        c.ident_b[:],
                        vtok[:, t128 * CH + cc * 128:t128 * CH + (cc + 1) * 128])
            p.wait(p.t, "V", p.val("V"))
        spool = PsumPool(p, "l0a_ss", 2, [128, TT], F32)

        wdt = p.sb("l0a_wdt", [128, TT], F32)
        msk = p.sb("l0a_msk", [128, TT], F32)
        wex = p.sb("l0a_wex", [128, TT], F32)
        pbuf = [p.sb(f"l0a_pb{i}", [128, TT], BF16) for i in range(2)]
        rb = p.sb("l0a_rb", [128, NC2 * TT], BF16)
        res = ResidualState(p, "l0a")
        owv = w_view(owT, D)
        wexm = p.sb("l0a_wexm", [128, TT], F32)
        pb_guard = [0, 0]
        for b in range(2):
            for qt in range(NQT):
                qt0 = qt * TT
                glob_q = b * SEQ + qt0
                pso = [None, None]
                slo = [None, None]
                for cc in range(NC2):
                    pso[cc], slo[cc] = pool.acquire()
                kt_start = qt0 // 128
                for kt in range(kt_start, NKT):
                    kt0 = kt * 128
                    pss, sls = spool.acquire()
                    rec = None
                    for ccq in range(NC2):
                        rec = p.t.matmul(
                            pss[:], kb[:, ccq * TOK + b * SEQ + kt0:
                                        ccq * TOK + b * SEQ + kt0 + 128],
                            qb[:, ccq * TOK + glob_q:ccq * TOK + glob_q + TT],
                            start=(ccq == 0), stop=(ccq == NC2 - 1))
                    tvv = p.inc(rec, "T")
                    c0 = float(kt0 - qt0 - 1)
                    p.v.tensor_scalar(wdt[:], c.iota_pf[:], c0, 0.0, ALU.add,
                                      ALU.max)
                    vv = p.inc(p.v.tensor_single_scalar(msk[:], c.iota_pf[:],
                                                        -c0, ALU.is_ge), "V")
                    p.wait(p.sc, "V", vv)
                    sv = p.inc(p.sc.activation(wex[:], wdt[:], AF.Exp,
                                               scale=lcol[:]), "S")
                    par = kt % 2
                    if pb_guard[par]:
                        p.wait(p.v, "T", pb_guard[par])
                    p.wait(p.v, "S", sv)
                    p.wait(p.v, "T", tvv)
                    p.v.tensor_tensor(wexm[:], wex[:], msk[:], ALU.mult)
                    vv = p.inc(p.v.tensor_tensor(pbuf[par][:], pss[:], wexm[:],
                                                 ALU.mult), "V")
                    spool.release(sls, "V", vv)
                    p.wait(p.t, "V", vv)
                    for cc in range(NC2):
                        rec = p.t.matmul(
                            pso[cc][:],
                            vtok[:, (b * NKT + kt) * CH + cc * 128:
                                 (b * NKT + kt) * CH + (cc + 1) * 128],
                            pbuf[par][:], start=(kt == kt_start),
                            stop=(kt == NKT - 1))
                    pb_guard[par] = p.inc(rec, "T")
                for cc in range(NC2):
                    p.wait(p.sc, "T", p.val("T"))
                    sv = p.inc(p.sc.activation(rb[:, cc * TT:(cc + 1) * TT],
                                               pso[cc][:], AF.Copy), "S")
                    pool.release(slo[cc], "S", sv)
                p.wait(p.t, "S", p.val("S"))

                def epi_o(fc, _tt, ps_ap2, slot2, tval):
                    vv2 = emit_residual_store(p, c, res, xin_ap, xout_ap, fc,
                                              _tt, ps_ap2[:], 1.0, tval,
                                              scale_col=oscol[:])
                    pool.release(slot2, "V", vv2)
                emit_proj(p, owv, NC2, lambda kc, _t: rb[:, kc * TT:(kc + 1) * TT],
                          D, epi_o, pool, tts=[b * NQT + qt])
                p.wait(p.v, "T", p.val("T"))


def emit_l1_attn(p, c, xin_ap, xout_ap, w, cw_qkv, cw_proj):
    H, DH = 16, 64
    SEQ = 2048
    NKT = SEQ // 128
    NQT4 = SEQ // TT
    nc = p.nc
    mrow_hbm = nc.dram_tensor(p.name_prefix + "mrow", [16, 16 * 128], BF16,
                              kind="Internal").ap()
    drow_hbm = nc.dram_tensor(p.name_prefix + "drow", [16, 16 * 128], F32,
                              kind="Internal").ap()
    with p.scope():
        pool = PsumPool(p, "l1a_ps", 3, [128, TT], F32)
        wpb = [p.sb(f"l1a_wpb{i}", [128, KC * 128], BF16) for i in range(2)]
        wp_guard = [0, 0]
        tcol = p.sb("l1a_tcol", [16, 1], F32)
        hfcol = p.sb("l1a_hf", [128, KC], F32)
        projb = p.sb("l1a_projb", [128, KC], F32)
        with p.scope():
            hsel_np = np.zeros((16, 1024), np.float32)
            for h_ in range(16):
                hsel_np[h_, h_ * 64:(h_ + 1) * 64] = 1.0
            hsel_t = nc.inline_tensor(hsel_np, p.name_prefix + "hsel")
            hsel = p.sb("l1a_hsel", [16, 1024], F32)
            lv = p.inc(p.sy.dma_start(hsel[:], hsel_t.ap()), "L")
            lv = p.inc(p.sy.dma_start(tcol[:],
                                      w["temperature"].rearrange(
                                          "(p o) -> p o", o=1)), "L")
            lv = p.inc(p.sy.dma_start(projb[:],
                                      w["proj_b"].rearrange("(c p) -> p c",
                                                            p=128),
                                      allow_slow_non_contiguous=True), "L")
            p.wait(p.v, "L", lv)
            p.v.tensor_scalar_mul(tcol[:], tcol[:], 64.0)
            vv = p.inc(p.v.reciprocal(tcol[:], tcol[:]), "V")
            p.wait(p.t, "V", vv)
            for kc in range(KC):
                ps_ap, slot = pool.acquire()
                tvv = p.inc(p.t.matmul(ps_ap[:, 0:1],
                                       hsel[:, kc * 128:(kc + 1) * 128],
                                       tcol[:], start=True, stop=True), "T")
                p.wait(p.v, "T", tvv)
                vv = p.inc(p.v.tensor_copy(hfcol[:, kc:kc + 1],
                                           ps_ap[:, 0:1]), "V")
                pool.release(slot, "V", vv)
            p.wait(p.sy, "V", p.val("V"))

        qh = p.sb("l1a_qh", [128, KC * SEQ], BF16)
        kh = p.sb("l1a_kh", [128, KC * SEQ], BF16)
        vtok = p.sb("l1a_vtok", [128, NKT * D], BF16)
        dualrow = p.sb("l1a_dual", [1, 2 * TT], F32)
        rn2 = p.sb("l1a_rn2", [1, 2 * TT], F32)
        res = ResidualState(p, "l1a")

        for b in range(2):
            with p.scope():
                lnA = LNState(p, "l1a_ln%d" % b, pool)
                lnb = p.sb("l1a_lnb%d" % b, [128, KC * TT], BF16)
                qraw = p.sb("l1a_qraw%d" % b, [128, KC * TT], BF16)
                kraw = p.sb("l1a_kraw%d" % b, [128, KC * TT], BF16)
                vtmp = p.sb("l1a_vtmp%d" % b, [128, TT], BF16)
                wqb = [p.sb(f"l1a_wq{b}{i}", [128, KC * 128], BF16)
                       for i in range(2)]
                stq, stk = lnA.st1, lnA.st2
                tpv = TransPool(p, "l1a_tpv%d" % b)
                lnv = chunks_view(lnb, KC, TT)
                qrv = chunks_view(qraw, KC, TT)
                krv = chunks_view(kraw, KC, TT)
                stq_guard = [0]
                wq_guard = [0, 0]
                qfc = {}

                def load_wq(fc):
                    sl = fc % 2
                    if wq_guard[sl]:
                        p.wait(p.sc, "T", wq_guard[sl])
                    return emit_load_wconv(p, cw_qkv, wqb[sl], KC, [fc], "Q")
                for ttl in range(NQT4):
                    gt = b * NQT4 + ttl
                    lnt = emit_ln_tile(p, c, lnA, xin_ap, gt,
                                       lambda kc: lnv(kc))
                    p.wait(p.t, "V", lnt)
                    if stq_guard[0]:
                        p.wait(p.t, "V", stq_guard[0])
                    sq_t = {}
                    qfc[0] = load_wq(0)
                    for fc in range(24):
                        if fc + 1 < 24:
                            qfc[fc + 1] = load_wq(fc + 1)
                        wqv_ = wconv_view(wqb[fc % 2], KC)
                        ps_ap, slot = pool.acquire()
                        p.wait(p.t, "Q", qfc[fc])
                        rec = None
                        for kc in range(KC):
                            rec = p.t.matmul(ps_ap[:], wqv_(kc, 0),
                                             lnv(kc), start=(kc == 0),
                                             stop=(kc == KC - 1))
                        tvv = p.inc(rec, "T")
                        wq_guard[fc % 2] = tvv
                        p.wait(p.sc, "T", tvv)
                        if fc < 8:
                            sv = p.inc(p.sc.activation(qrv(fc), ps_ap[:],
                                                       AF.Copy), "S")
                            sl2 = fc % 2
                            if fc >= 2:
                                p.wait(p.sc, "T", sq_t[fc - 2])
                            sv = p.inc(p.sc.activation(
                                lnA.sq[:, sl2 * TT:(sl2 + 1) * TT], ps_ap[:],
                                AF.Square), "S")
                            p.wait(p.t, "S", sv)
                            sq_t[fc] = p.inc(p.t.matmul(
                                stq[:], c.ones_col_f[:],
                                lnA.sq[:, sl2 * TT:(sl2 + 1) * TT],
                                start=(fc == 0), stop=(fc == 7)), "T")
                        elif fc < 16:
                            sv = p.inc(p.sc.activation(krv(fc - 8), ps_ap[:],
                                                       AF.Copy), "S")
                            sl2 = fc % 2
                            p.wait(p.sc, "T", sq_t[fc - 2])
                            sv = p.inc(p.sc.activation(
                                lnA.sq[:, sl2 * TT:(sl2 + 1) * TT], ps_ap[:],
                                AF.Square), "S")
                            p.wait(p.t, "S", sv)
                            sq_t[fc] = p.inc(p.t.matmul(
                                stk[:], c.ones_col_f[:],
                                lnA.sq[:, sl2 * TT:(sl2 + 1) * TT],
                                start=(fc == 8), stop=(fc == 15)), "T")
                        else:
                            sv = p.inc(p.sc.activation(vtmp[:], ps_ap[:],
                                                       AF.Copy), "S")
                            p.wait(p.t, "S", sv)
                            for j4 in range(4):
                                kt128 = ttl * 4 + j4
                                tpv.step(vtmp[:, j4 * 128:(j4 + 1) * 128],
                                         c.ident_b[:],
                                         vtok[:, kt128 * D + (fc - 16) * 128:
                                              kt128 * D + (fc - 15) * 128])
                            p.wait(p.t, "V", p.val("V"))
                        pool.release(slot, "S" if fc < 16 else "V",
                                     p.val("S") if fc < 16 else p.val("V"))
                    p.wait(p.v, "T", p.val("T"))
                    for (st, off) in ((stq, 0), (stk, 1)):
                        dr = dualrow[:, off * TT:(off + 1) * TT]
                        p.v.tensor_copy(dr, st[:])
                        p.wait(p.sc, "V", p.val("V"))
                        p.inc(p.sc.activation(dr, dr, AF.Sqrt), "S")
                        p.wait(p.v, "S", p.val("S"))
                        p.v.tensor_scalar_max(dr, dr, 1e-12)
                        vv = p.inc(p.v.reciprocal(
                            rn2[:, off * TT:(off + 1) * TT], dr), "V")
                    stq_guard[0] = p.val("V")
                    lnA.last_v = max(lnA.last_v, stq_guard[0])
                    p.wait(p.t, "V", p.val("V"))
                    bcq, sq_ = pool.acquire()
                    p.t.matmul(bcq[:], c.ones_row_f[:],
                               rn2[:, 0:TT], start=True, stop=True)
                    bck, sk_ = pool.acquire()
                    tvv = p.inc(p.t.matmul(bck[:], c.ones_row_f[:],
                                           rn2[:, TT:2 * TT],
                                           start=True, stop=True), "T")
                    p.wait(p.v, "T", tvv)
                    for kc in range(KC):
                        p.v.scalar_tensor_tensor(
                            qh[:, kc * SEQ + ttl * TT:kc * SEQ + (ttl + 1) * TT],
                            qrv(kc), hfcol[:, kc:kc + 1], bcq[:], ALU.mult,
                            ALU.mult)
                        vv = p.inc(p.v.tensor_tensor(
                            kh[:, kc * SEQ + ttl * TT:kc * SEQ + (ttl + 1) * TT],
                            krv(kc), bck[:], ALU.mult), "V")
                    pool.release(sq_, "V", vv)
                    pool.release(sk_, "V", vv)
                    p.wait(p.sc, "T", p.val("T"))
                    p.wait(p.v, "T", p.val("T"))

            with p.scope():
                attn_o = p.sb("l1a_ao%d" % b, [128, KC * SEQ], BF16)
                spool = PsumPool(p, "l1a_sp%d" % b, 4, [128, TT], F32)
                mcol = p.sb("l1a_mcol%d" % b, [128, 2 * H * NKT], F32)
                mtmpM = p.sb("l1a_mtmpM%d" % b, [16, H * 128], BF16)
                mtmpD = p.sb("l1a_mtmpD%d" % b, [16, H * 128], F32)
                mred = p.sb("l1a_mred%d" % b, [128, 8], F32)
                mrowb2 = [p.sb(f"l1a_mrb{b}{i}", [1, SEQ], BF16)
                          for i in range(2)]
                drow2 = [p.sb(f"l1a_dr2{b}{i}", [1, SEQ], F32)
                         for i in range(2)]
                pbuf = [p.sb(f"l1a_pb{b}{i}", [128, TT], BF16)
                        for i in range(3)]
                dival = p.sb("l1a_div%d" % b, [128, TT], F32)

                def _slab(src, h_, t0, wd):
                    kc0 = ((h_ * DH) // 128) * SEQ
                    off = (h_ * DH) % 128
                    return src[off:off + DH, kc0 + t0:kc0 + t0 + wd]

                bigp = PsumPool.__new__(PsumPool)
                bigp.p = p
                bigp.tiles = spool.tiles + pool.tiles
                bigp.n = len(bigp.tiles)
                bigp.i = 0
                bigp.guard = spool.guard[:] + pool.guard[:]
                for h_ in range(H):
                    for qc in range(NKT):
                        tiles = []
                        for ktt in range(NQT4):
                            ps_ap, slot = bigp.acquire()
                            tvv = p.inc(p.t.matmul(
                                ps_ap[:], _slab(qh, h_, qc * 128, 128),
                                _slab(kh, h_, ktt * TT, TT), start=True,
                                stop=True), "T")
                            tiles.append((ps_ap, slot))
                            p.wait(p.v, "T", tvv)
                            vv = p.inc(p.v.reduce_max(
                                mred[:, ktt:ktt + 1], ps_ap[:],
                                axis=AX.X), "V")
                        mc = mcol[:, h_ * NKT + qc:h_ * NKT + qc + 1]
                        vv = p.inc(p.v.reduce_max(
                            mc, mred[:, 0:4], axis=AX.X, negate=True), "V")
                        p.wait(p.sc, "V", vv)
                        for ktt in range(NQT4):
                            ps_ap, slot = tiles[ktt]
                            sv = p.inc(p.sc.activation(
                                dival[:], ps_ap[:], AF.Exp, bias=mc,
                                accum_out=mred[:, 4 + ktt:5 + ktt]), "S")
                            bigp.release(slot, "S", sv)
                        p.wait(p.v, "S", p.val("S"))
                        vv = p.inc(p.v.reduce_sum(
                            mcol[:, H * NKT + h_ * NKT + qc:
                                 H * NKT + h_ * NKT + qc + 1],
                            mred[:, 4:8], axis=AX.X), "V")
                spool.guard[:] = bigp.guard[0:spool.n]
                pool.guard[:] = bigp.guard[spool.n:]
                spool.i = 0
                pool.i = 0
                p.wait(p.t, "V", p.val("V"))
                for g_ in range(2 * H):
                    ps_ap, slot = spool.acquire()
                    tvv = p.inc(p.t.transpose(
                        ps_ap[0:16, 0:128],
                        mcol[:, g_ * NKT:(g_ + 1) * NKT], c.ident_f[:]), "T")
                    p.wait(p.v, "T", tvv)
                    if g_ < H:
                        dst_ = mtmpM[:, g_ * 128:(g_ + 1) * 128]
                    else:
                        dst_ = mtmpD[:, (g_ - H) * 128:(g_ - H + 1) * 128]
                    vv = p.inc(p.v.tensor_copy(dst_, ps_ap[0:16, 0:128]), "V")
                    spool.release(slot, "V", vv)
                    p.wait(p.t, "V", vv)
                p.wait(p.sy, "V", p.val("V"))
                p.inc(p.sy.dma_start(mrow_hbm, mtmpM[:]), "O")
                ov = p.inc(p.sy.dma_start(drow_hbm, mtmpD[:]), "O")
                p.wait(p.sy, "O", ov)
                mr_guard = [0, 0]
                mr_ready = {}

                def fetch_mrow(h2):
                    sl = h2 % 2
                    if mr_guard[sl]:
                        p.wait(p.sy, "T", mr_guard[sl])
                    srcm = bass.AP(mrow_hbm.tensor, h2 * 128,
                                   [[1, 1], [H * 128, NKT], [1, 128]])
                    p.inc(p.sy.dma_start(
                        mrowb2[sl][:].rearrange("o (a b) -> o a b", b=128),
                        srcm), "L")
                    srcd = bass.AP(drow_hbm.tensor, h2 * 128,
                                   [[1, 1], [H * 128, NKT], [1, 128]])
                    p.inc(p.sy.dma_start(
                        drow2[sl][:].rearrange("o (a b) -> o a b", b=128),
                        srcd), "L")
                    lv2 = p.val("L")
                    p.wait(p.v, "L", lv2)
                    return p.inc(p.v.reciprocal(drow2[sl][:], drow2[sl][:]),
                                 "V")

                mr_ready[0] = fetch_mrow(0)
                for h_ in range(H):
                    kc0 = (h_ * DH) // 128
                    off = (h_ * DH) % 128
                    qslab = lambda t0, wd: qh[off:off + DH,
                                              kc0 * SEQ + t0:kc0 * SEQ + t0 + wd]
                    kslab = lambda t0, wd: kh[off:off + DH,
                                              kc0 * SEQ + t0:kc0 * SEQ + t0 + wd]
                    if h_ + 1 < H:
                        mr_ready[h_ + 1] = fetch_mrow(h_ + 1)
                    p.wait(p.t, "V", mr_ready[h_])
                    mrow_b = mrowb2[h_ % 2][:]
                    pb_free = [0, 0, 0]
                    for qt in range(NQT4):
                        qt0 = qt * TT
                        pat, sat = pool.acquire()
                        for kt in range(NKT):
                            pss, sls = spool.acquire()
                            p.t.matmul(pss[:], kslab(kt * 128, 128),
                                       qslab(qt0, TT), start=True, stop=False)
                            rec = p.t.matmul(pss[:], c.ones_row_b[:],
                                             mrow_b[0:1, qt0:qt0 + TT],
                                             start=False, stop=True)
                            tvv = p.inc(rec, "T")
                            par = kt % 3
                            p.wait(p.sc, "T", tvv)
                            if pb_free[par]:
                                p.wait(p.sc, "T", pb_free[par])
                            sv = p.inc(p.sc.activation(pbuf[par][:], pss[:],
                                                       AF.Exp), "S")
                            spool.release(sls, "S", sv)
                            p.wait(p.t, "S", sv)
                            rec = p.t.matmul(pat[0:DH, :],
                                             vtok[:, kt * D + h_ * DH:
                                                  kt * D + (h_ + 1) * DH],
                                             pbuf[par][:], start=(kt == 0),
                                             stop=(kt == NKT - 1))
                            tvv = p.inc(rec, "T")
                            pb_free[par] = tvv
                        mr_guard[h_ % 2] = tvv
                        bcd, sbd = spool.acquire()
                        tvv = p.inc(p.t.matmul(bcd[0:DH, :],
                                               c.ones_row_f[0:1, 0:DH],
                                               drow2[h_ % 2][:, qt0:qt0 + TT],
                                               start=True, stop=True), "T")
                        p.wait(p.sc, "T", tvv)
                        sv = p.inc(p.sc.activation(dival[0:DH, :],
                                                   pat[0:DH, :], AF.Copy),
                                   "S")
                        pool.release(sat, "S", sv)
                        p.wait(p.v, "S", sv)
                        p.wait(p.v, "T", tvv)
                        vv = p.inc(p.v.tensor_tensor(
                            attn_o[off:off + DH,
                                   kc0 * SEQ + qt0:kc0 * SEQ + qt0 + TT],
                            dival[0:DH, :], bcd[0:DH, :], ALU.mult), "V")
                        spool.release(sbd, "V", vv)

                p.wait(p.t, "V", p.val("V"))
                wpq = {}

                def load_wp(fc2):
                    sl = fc2 % 2
                    if wp_guard[sl]:
                        p.wait(p.sc, "T", wp_guard[sl])
                    return emit_load_wconv(p, cw_proj, wpb[sl], KC, [fc2],
                                           "Q")
                wpq[0] = load_wp(0)
                for fc in range(KC):
                    if fc + 1 < KC:
                        wpq[fc + 1] = load_wp(fc + 1)
                    wvv = wconv_view(wpb[fc % 2], KC)
                    p.wait(p.t, "Q", wpq[fc])
                    for ttl in range(NQT4):
                        ps_ap, slot = pool.acquire()
                        rec = None
                        for kc in range(KC):
                            rec = p.t.matmul(
                                ps_ap[:], wvv(kc, 0),
                                attn_o[:, kc * SEQ + ttl * TT:
                                       kc * SEQ + (ttl + 1) * TT],
                                start=(kc == 0), stop=(kc == KC - 1))
                        tval = p.inc(rec, "T")
                        wp_guard[fc % 2] = tval
                        vv2 = emit_residual_store(
                            p, c, res, xin_ap, xout_ap, fc, b * NQT4 + ttl,
                            ps_ap[:], 1.0, tval,
                            bias_col=projb[:, fc:fc + 1])
                        pool.release(slot, "V", vv2)
                p.wait(p.v, "T", p.val("T"))
                p.wait(p.sc, "T", p.val("T"))


def emit_l3_attn(p, c, xin_ap, xout_ap, w, cwq, cwk, cwv, cwo):
    NT3 = 16
    SB_ = 256
    H, DH = 16, 64
    nc = p.nc
    qhbm = nc.dram_tensor(p.name_prefix + "l3aq", [D, TOK], BF16,
                          kind="Internal").ap()
    vhbm = nc.dram_tensor(p.name_prefix + "l3av", [D, TOK], BF16,
                          kind="Internal").ap()
    with p.scope():
        pool = PsumPool(p, "l3a_ps", 4, [128, TT], F32)
        spool = PsumPool(p, "l3a_sp", 2, [128, TT], F32)
        bcols = p.sb("l3a_bcols", [128, 4 * KC], F32)
        for i, nm in enumerate(("bq", "bk", "bv", "bo")):
            lv = p.inc(p.sy.dma_start(
                bcols[:, i * KC:(i + 1) * KC],
                w[nm].rearrange("(c p) -> p c", p=128),
                allow_slow_non_contiguous=True), "L")
        p.wait(p.v, "L", lv)
        vv = p.inc(p.v.tensor_scalar_mul(bcols[:, 0:KC], bcols[:, 0:KC],
                                         0.125), "V")
        rkcols = p.sb("l3a_rkc", [128, NT3 * NT3], F32)
        rvcols = p.sb("l3a_rvc", [128, NT3 * NT3], F32)
        with p.scope():
            rtab = p.sb("l3a_rtab", [33, 64], F32)
            sel_np = np.zeros((33, 256), np.float32)
            for t_ in range(16):
                for s_ in range(16):
                    r_ = min(max(s_ - t_, -16), 16) + 16
                    sel_np[r_, t_ * 16 + s_] = 1.0
            sel_t = nc.inline_tensor(sel_np, p.name_prefix + "l3a_sel")
            sel = p.sb("l3a_selb", [33, 256], F32)
            lv = p.inc(p.sy.dma_start(sel[:], sel_t.ap()), "L")
            for (nm, dst) in (("rk_table", rkcols), ("rv_table", rvcols)):
                lv = p.inc(p.sy.dma_start(rtab[:], w[nm]), "L")
                p.wait(p.t, "L", lv)
                ps_ap, slot = pool.acquire()
                tvv = p.inc(p.t.matmul(ps_ap[0:64, 0:256], rtab[:], sel[:],
                                       start=True, stop=True), "T")
                p.wait(p.v, "T", tvv)
                p.v.tensor_copy(dst[0:64, :], ps_ap[0:64, 0:256])
                vv = p.inc(p.v.tensor_copy(dst[64:128, :],
                                           ps_ap[0:64, 0:256]), "V")
                pool.release(slot, "V", vv)
                p.wait(p.sy, "V", vv)
        hsT_np = np.zeros((128, KC * 16), ml_dtypes.bfloat16)
        for kc in range(KC):
            for pp in range(128):
                hsT_np[pp, kc * 16 + (kc * 128 + pp) // 64] = 1.0
        hsT_t = nc.inline_tensor(np.asarray(hsT_np), p.name_prefix + "l3a_hsTc")
        hselT = p.sb("l3a_hsT", [128, KC * 16], BF16)
        lv = p.inc(p.sy.dma_start(hselT[:], hsT_t.ap()), "L")
        hsB_np = np.zeros((16, 1024), ml_dtypes.bfloat16)
        for h_ in range(16):
            hsB_np[h_, h_ * 64:(h_ + 1) * 64] = 1.0
        hsB_t = nc.inline_tensor(np.asarray(hsB_np), p.name_prefix + "l3a_hsBc")
        hselB = p.sb("l3a_hsB", [16, 1024], BF16)
        lv = p.inc(p.sy.dma_start(hselB[:], hsB_t.ap()), "L")

        kbuf = p.sb("l3a_k", [128, KC * TOK], BF16)
        kv_ = chunks_view(kbuf, KC, TOK)

        with p.scope():
            lnA = LNState(p, "l3a_ln", pool)
            lnb = p.sb("l3a_lnb", [128, KC * TT], BF16)
            qtile = p.sb("l3a_qt", [128, KC * TT], BF16)
            wb = [p.sb(f"l3a_wb{i}", [128, KC * 128], BF16) for i in range(2)]
            lnv = chunks_view(lnb, KC, TT)
            qtv = chunks_view(qtile, KC, TT)
            wguard = [0, 0]
            q_o_guard = [0]

            def load_w(cw, fc):
                sl = fc % 2
                if wguard[sl]:
                    p.wait(p.sc, "T", wguard[sl])
                return emit_load_wconv(p, cw, wb[sl], KC, [fc], "Q")
            for tt in range(NTT):
                lnt = emit_ln_tile(p, c, lnA, xin_ap, tt, lambda kc: lnv(kc))
                p.wait(p.t, "V", lnt)
                for (wi, cw) in enumerate((cwq, cwk, cwv)):
                    qfc = {0: load_w(cw, 0)}
                    for fc in range(KC):
                        if fc + 1 < KC:
                            qfc[fc + 1] = load_w(cw, fc + 1)
                        ps_ap, slot = pool.acquire()
                        p.wait(p.t, "Q", qfc[fc])
                        rec = None
                        wvv = wconv_view(wb[fc % 2], KC)
                        for kc in range(KC):
                            rec = p.t.matmul(ps_ap[:], wvv(kc, 0), lnv(kc),
                                             start=(kc == 0),
                                             stop=(kc == KC - 1))
                        tvv = p.inc(rec, "T")
                        wguard[fc % 2] = tvv
                        p.wait(p.v, "T", tvv)
                        if wi == 1:
                            sv = p.inc(p.v.tensor_scalar_add(
                                kv_(fc, tt * TT, TT), ps_ap[:],
                                bcols[:, KC + fc:KC + fc + 1]), "V")
                        else:
                            if q_o_guard[0]:
                                p.wait(p.v, "O", q_o_guard[0])
                            boff = 0 if wi == 0 else 2 * KC
                            if wi == 0:
                                sv = p.inc(p.v.tensor_scalar(
                                    qtv(fc), ps_ap[:], 0.125,
                                    bcols[:, boff + fc:boff + fc + 1],
                                    ALU.mult, ALU.add), "V")
                            else:
                                sv = p.inc(p.v.tensor_scalar_add(
                                    qtv(fc), ps_ap[:],
                                    bcols[:, boff + fc:boff + fc + 1]), "V")
                            p.wait(p.sy, "V", sv)
                            hb = qhbm if wi == 0 else vhbm
                            q_o_guard[0] = p.inc(p.sy.dma_start(
                                feat_dram_view(hb, fc, tt * TT, TT),
                                qtv(fc)), "O")
                        pool.release(slot, "V", sv)
                p.wait(p.v, "T", p.val("T"))

        with p.scope():
            wo = p.sb("l3a_wo", [128, KC * KC * 128], BF16)
            qv_ = emit_load_wconv(p, cwo, wo, KC, range(KC), "Q")
            wov = wconv_view(wo, KC)
            qt2 = [p.sb(f"l3a_q2{i}", [128, KC * SB_], BF16) for i in range(2)]
            vsb = [p.sb(f"l3a_vs{i}", [128, KC * SB_], BF16) for i in range(2)]
            sc2 = p.sb("l3a_sc2", [16, NT3 * SB_], F32)
            sc3 = p.sb("l3a_sc3", [16, NT3 * SB_], F32)
            pnb = p.sb("l3a_pnb", [16, NT3 * SB_], BF16)
            srows = p.sb("l3a_srows", [16, 3 * SB_], F32)
            recipb = p.sb("l3a_recipb", [16, SB_], BF16)
            zb2 = [p.sb(f"l3a_z{i}", [128, KC * SB_], BF16) for i in range(2)]
            av2 = [p.sb(f"l3a_av{i}", [128, KC * SB_], F32) for i in range(2)]
            accf = p.sb("l3a_acc", [128, KC * SB_], F32)
            attb = p.sb("l3a_attb", [128, KC * SB_], BF16)
            res = ResidualState(p, "l3a")
            qld = [0, 0]
            vld_guard = [0, 0]
            vready = {}
            p.wait(p.t, "Q", qv_)

            def load_vs(s_, vi):
                sl = vi % 2
                if vld_guard[sl]:
                    p.wait(p.sy, "V", vld_guard[sl])
                src = vhbm.rearrange("(c p) t -> p c t",
                                     p=128)[:, :, s_ * SB_:(s_ + 1) * SB_]
                return p.inc(p.sy.dma_start(
                    vsb[sl][:].rearrange("p (c t) -> p c t", c=KC), src), "L")
            for t_ in range(NT3):
                par = t_ % 2
                if qld[par]:
                    p.wait(p.sy, "V", qld[par])
                src = qhbm.rearrange("(c p) t -> p c t",
                                     p=128)[:, :, t_ * SB_:(t_ + 1) * SB_]
                lv = p.inc(p.sy.dma_start(
                    qt2[par][:].rearrange("p (c t) -> p c t", c=KC), src), "L")
                p.wait(p.v, "L", lv)
                z_guard = [0, 0]
                for s_ in range(NT3):
                    zs = zb2[s_ % 2]
                    zv = zs[:].rearrange("p (c t) -> p c t", c=KC)
                    kslc = bass.AP(kbuf, s_ * SB_, [[KC * TOK, 128],
                                                    [TOK, KC], [1, SB_]])
                    qslc = qt2[par][:].rearrange("p (c t) -> p c t", c=KC)
                    if z_guard[s_ % 2]:
                        p.wait(p.v, "T", z_guard[s_ % 2])
                    vv = p.inc(p.v.scalar_tensor_tensor(
                        zv, kslc, rkcols[:, t_ * 16 + s_:t_ * 16 + s_ + 1],
                        qslc, ALU.add, ALU.mult), "V")
                    p.wait(p.t, "V", vv)
                    ps_ap, slot = spool.acquire()
                    rec = None
                    for kc in range(KC):
                        rec = p.t.matmul(ps_ap[0:16, 0:SB_],
                                         hselT[:, kc * 16:(kc + 1) * 16],
                                         zs[:, kc * SB_:(kc + 1) * SB_],
                                         start=(kc == 0), stop=(kc == KC - 1))
                    tvv = p.inc(rec, "T")
                    z_guard[s_ % 2] = tvv
                    p.wait(p.sc, "T", tvv)
                    sv = p.inc(p.sc.activation(sc2[:, s_ * SB_:(s_ + 1) * SB_],
                                               ps_ap[0:16, 0:SB_], AF.Copy),
                               "S")
                    spool.release(slot, "S", sv)
                p.wait(p.v, "S", p.val("S"))
                v2 = bass.AP(sc2, 0, [[NT3 * SB_, 16], [1, SB_], [SB_, NT3]])
                v3 = bass.AP(sc3, 0, [[NT3 * SB_, 16], [1, SB_], [SB_, NT3]])
                p.v.reduce_max(bass.AP(srows, 0, [[3 * SB_, 16], [1, SB_],
                                                  [0, 1]]), v2, axis=AX.X,
                               negate=True)
                vv = p.inc(p.v.tensor_tensor(
                    v3, v2, bass.AP(srows, 0, [[3 * SB_, 16], [1, SB_],
                                               [0, NT3]]), ALU.add), "V")
                p.wait(p.sc, "V", vv)
                sv = p.inc(p.sc.activation(pnb[:], sc3[:], AF.Exp), "S")
                p.wait(p.v, "S", sv)
                pv3 = bass.AP(pnb, 0, [[NT3 * SB_, 16], [1, SB_], [SB_, NT3]])
                p.v.reduce_sum(bass.AP(srows, SB_, [[3 * SB_, 16], [1, SB_],
                                                    [0, 1]]), pv3, axis=AX.X)
                vv = p.inc(p.v.reciprocal(srows[:, 2 * SB_:3 * SB_],
                                          srows[:, SB_:2 * SB_]), "V")
                vv = p.inc(p.v.tensor_copy(recipb[:],
                                           srows[:, 2 * SB_:3 * SB_]), "V")
                p.wait(p.t, "V", vv)
                vready[0] = load_vs(0, 0)
                av_guard = [0, 0]
                for s_ in range(NT3):
                    if s_ + 1 < NT3:
                        vready[s_ + 1] = load_vs(s_ + 1, s_ + 1)
                    rvc = rvcols[:, t_ * 16 + s_:t_ * 16 + s_ + 1]
                    vsl = vsb[s_ % 2]
                    avt = av2[s_ % 2]
                    p.wait(p.v, "L", vready[s_])
                    if av_guard[s_ % 2]:
                        p.wait(p.v, "P", av_guard[s_ % 2])
                    for kg in range(KC // 2):
                        bcp, bslot = pool.acquire()
                        last_vv = None
                        for j in range(2):
                            kc = kg * 2 + j
                            tvv = p.inc(p.t.matmul(
                                bcp[:, j * SB_:(j + 1) * SB_],
                                hselB[:, kc * 128:(kc + 1) * 128],
                                pnb[:, s_ * SB_:(s_ + 1) * SB_],
                                start=True, stop=True), "T")
                            p.wait(p.v, "T", tvv)
                            last_vv = p.inc(p.v.scalar_tensor_tensor(
                                avt[:, kc * SB_:(kc + 1) * SB_],
                                vsl[:, kc * SB_:(kc + 1) * SB_], rvc,
                                bcp[:, j * SB_:(j + 1) * SB_], ALU.add,
                                ALU.mult), "V")
                        pool.release(bslot, "V", last_vv)
                    p.wait(p.g, "V", p.val("V"))
                    if s_ == 0:
                        pv = p.inc(p.g.tensor_copy(accf[:], avt[:]), "P")
                    else:
                        pv = p.inc(p.g.tensor_tensor(accf[:], accf[:],
                                                     avt[:], ALU.add), "P")
                    av_guard[s_ % 2] = pv
                    vld_guard[s_ % 2] = p.val("V")
                p.wait(p.t, "V", p.val("V"))
                p.wait(p.v, "P", p.val("P"))
                for kc in range(KC):
                    bcp, bslot = pool.acquire()
                    tvv = p.inc(p.t.matmul(bcp[:, 0:SB_],
                                           hselB[:, kc * 128:(kc + 1) * 128],
                                           recipb[:],
                                           start=True, stop=True), "T")
                    p.wait(p.v, "T", tvv)
                    vv = p.inc(p.v.tensor_tensor(
                        attb[:, kc * SB_:(kc + 1) * SB_],
                        accf[:, kc * SB_:(kc + 1) * SB_], bcp[:, 0:SB_],
                        ALU.mult), "V")
                    pool.release(bslot, "V", vv)
                qld[par] = vv
                p.wait(p.t, "V", vv)
                for fc in range(KC):
                    ps_ap, slot = pool.acquire()
                    rec = None
                    for kc in range(KC):
                        rec = p.t.matmul(ps_ap[:, 0:SB_], wov(kc, fc),
                                         attb[:, kc * SB_:(kc + 1) * SB_],
                                         start=(kc == 0), stop=(kc == KC - 1))
                    tvv = p.inc(rec, "T")
                    slot_r = res.i % res.n
                    res.i += 1
                    buf = res.buf[slot_r]
                    if res.guard[slot_r] is not None:
                        p.wait(p.sy, "O", res.guard[slot_r])
                    lv = p.inc(p.sy.dma_start(
                        buf[:, 0:SB_],
                        feat_dram_view(xin_ap, fc, t_ * SB_, SB_)), "L")
                    p.wait(p.v, "L", lv)
                    p.wait(p.v, "T", tvv)
                    p.v.tensor_scalar(buf[:, TT:TT + SB_], ps_ap[:, 0:SB_],
                                      bcols[:, 3 * KC + fc:3 * KC + fc + 1],
                                      None, ALU.add)
                    vvr = p.inc(p.v.tensor_tensor(
                        buf[:, TT + SB_:TT + 2 * SB_], buf[:, TT:TT + SB_],
                        buf[:, 0:SB_], ALU.add), "V")
                    pool.release(slot, "V", vvr)
                    p.wait(p.sy, "V", vvr)
                    ov = p.inc(p.sy.dma_start(
                        feat_dram_view(xout_ap, fc, t_ * SB_, SB_),
                        buf[:, TT + SB_:TT + 2 * SB_]), "O")
                    res.guard[slot_r] = ov
                p.wait(p.sc, "T", p.val("T"))
                p.wait(p.v, "T", p.val("T"))


LAYER_SPECS_012 = [
    ("l0_ffn", "L0_ffn_"), ("l0_attn", "L0_attn_"), ("l0_ffn", "L0_ffn_"),
    ("l1_ffn", "L1_ffn_"), ("l1_attn", "L1_attn_"), ("l1_ffn", "L1_ffn_"),
    ("l2_ffn", "L2_ffn_"), ("l2_attn", "L2_attn_"), ("l2_ffn", "L2_ffn_"),
]
LAYER_SPECS_3 = [
    ("l3_ffn", "L3_ffn_"), ("l3_attn", "L3_attn_"), ("l3_ffn", "L3_ffn_"),
]

_prog_cache = {}


def _chain_prog(specs, a2a_before=None):
    key = (tuple(specs), a2a_before)
    if key not in _prog_cache:
        _prog_cache[key] = build_chain(list(specs), a2a_before=a2a_before)
    return _prog_cache[key]


def kernel(**inputs):
    x = np.asarray(inputs["L0_attn_x"], dtype=np.float32)
    weights = {}
    for k, v in inputs.items():
        if k == "L0_attn_x":
            continue
        a = np.asarray(v)
        shape, dt = INPUT_SPECS[k]
        if dt == I32:
            a = np.ascontiguousarray(a.reshape(shape), dtype=np.int32)
        else:
            a = np.ascontiguousarray(a.reshape(shape), dtype=np.float32)
        weights[k] = a

    shards = [np.ascontiguousarray(x[2 * cc:2 * cc + 2].reshape(4096, 1024).T)
              for cc in range(NCORES)]

    def run_specs(specs, cur, a2a_before=None):
        nc_prog, used = _chain_prog(tuple(specs), a2a_before)
        in_maps = []
        for cc in range(NCORES):
            m = {"xin": cur[cc]}
            for nm in used:
                m[nm] = weights[nm]
            in_maps.append(m)
        res = run_bass_kernel_spmd(nc_prog, in_maps,
                                   core_ids=list(range(NCORES)))
        return [np.asarray(r["xout"]) for r in res.results]

    cur3 = run_specs(LAYER_SPECS_012 + LAYER_SPECS_3, shards,
                     a2a_before=len(LAYER_SPECS_012))

    out = np.empty((16, 2048, 1024), np.float32)
    for cc in range(NCORES):
        o = cur3[cc].T.reshape(16, 256, 1024)
        out[:, cc * 256:(cc + 1) * 256, :] = o
    return out


# version 09 (final)
# speedup vs baseline: 1.0162x; 1.0024x over previous
import numpy as np
import ml_dtypes
from contextlib import ExitStack

import concourse.bass as bass
import concourse.mybir as mybir
from concourse.bass_utils import run_bass_kernel_spmd

F32 = mybir.dt.float32
BF16 = mybir.dt.bfloat16
I32 = mybir.dt.int32
AF = mybir.ActivationFunctionType
ALU = mybir.AluOpType
AX = mybir.AxisListType

ENGS = ("sync", "scalar", "vector", "tensor", "gpsimd")

D = 1024
KC = 8
TOK = 4096
TT = 512
NTT = TOK // TT
NCORES = 8
BIG = 1.0e30
_dbg = {}



class _OpRec:
    __slots__ = ("incs",)

    def __init__(self):
        self.incs = []

    def then_inc(self, sem_handle, n):
        self.incs.append((sem_handle, n))
        return self


_AUTOSYNC_OPS = ("tensor_", "reduce_", "reciprocal", "memset", "memzero",
                 "activation", "select", "copy_predicated", "bn_", "iota",
                 "affine_select", "scalar_tensor_tensor", "square", "sqrt",
                 "sign", "transpose")
_ENG_SEM = {"vector": "V", "scalar": "S", "gpsimd": "P"}


def _op_is_small(a):
    if not a:
        return True
    out = a[0]
    try:
        shape = out.shape
    except Exception:
        return True
    if len(shape) < 2:
        return True
    parts = shape[0]
    free = 1
    for s in shape[1:]:
        free *= s
    return parts < 128 or free < 512


class _EngProxy:
    def __init__(self, prog, eng):
        self._p = prog
        self._e = eng

    def __getattr__(self, name):
        p, e = self._p, self._e

        def call(*a, **k):
            rec = _OpRec()
            p.q[e].append((name, a, k, rec))
            if e in _ENG_SEM and name.startswith(_AUTOSYNC_OPS)                     and _op_is_small(a):
                sname = _ENG_SEM[e]
                p.cnt[sname] += 1
                rec.incs.append((p.sem[sname], 1))
                wrec = _OpRec()
                p.q[e].append(("wait_ge", (p.sem[sname], p.cnt[sname]), {},
                               wrec))
            return rec

        return call


class Prog:

    def __init__(self, nc, ctx):
        self.nc = nc
        self.ctx = ctx
        self.q = {e: [] for e in ENGS}
        self.sem = {}
        self.cnt = {}
        for name in ("L", "O", "G", "Q", "V", "T", "S", "P", "B"):
            self.sem[name] = ctx.enter_context(nc.semaphore("sem" + name))
            self.cnt[name] = 0
        self.sy = _EngProxy(self, "sync")
        self.sc = _EngProxy(self, "scalar")
        self.v = _EngProxy(self, "vector")
        self.t = _EngProxy(self, "tensor")
        self.g = _EngProxy(self, "gpsimd")
        self.proxy = {"sync": self.sy, "scalar": self.sc, "vector": self.v,
                      "tensor": self.t, "gpsimd": self.g}
        self.name_prefix = ""
        self.dma_sems = {"sync": ["L", "O"], "gpsimd": ["G"], "scalar": ["Q"]}

    def inc(self, rec, name, n=None):
        if n is None:
            n = 16 if name in ("L", "O", "G", "Q") else 1
        self.cnt[name] += n
        rec.then_inc(self.sem[name], n)
        return self.cnt[name]

    def val(self, name):
        return self.cnt[name]

    def wait(self, proxy, name, value=None):
        if value is None:
            value = self.cnt[name]
        if value > 0:
            proxy.wait_ge(self.sem[name], value)

    def barrier(self):
        for e in ENGS:
            px = self.proxy[e]
            for s in self.dma_sems.get(e, []):
                self.wait(px, s)
        self.cnt["B"] += len(ENGS)
        target = self.cnt["B"]
        for e in ENGS:
            self.proxy[e].sem_inc(self.sem["B"], 1)
        for e in ENGS:
            self.proxy[e].wait_ge(self.sem["B"], target)

    def scope(self):
        from contextlib import contextmanager

        @contextmanager
        def _scope():
            old = self.ctx
            with ExitStack() as es:
                self.ctx = es
                try:
                    yield
                finally:
                    self.ctx = old
        return _scope()

    def sb(self, name, shape, dtype=F32):
        return self.ctx.enter_context(
            self.nc.sbuf_tensor(self.name_prefix + name, shape, dtype))

    def ps(self, name, shape, dtype=F32):
        return self.ctx.enter_context(
            self.nc.psum_tensor(self.name_prefix + name, shape, dtype))

    def play(self):
        with self.nc.Block() as block:
            for ename in ENGS:
                def emit(eng, _q=self.q[ename]):
                    for (name, a, k, rec) in _q:
                        inst = getattr(eng, name)(*a, **k)
                        merged = {}
                        for (h, n) in rec.incs:
                            merged[h] = merged.get(h, 0) + n
                        for (h, n) in merged.items():
                            inst.then_inc(h, n)
                getattr(block, ename)(emit)


class PsumPool:

    def __init__(self, p, name, n, shape, dtype=F32):
        self.p = p
        self.tiles = [p.ps(f"{name}{i}", shape, dtype) for i in range(n)]
        self.n = n
        self.i = 0
        self.guard = [None] * n

    def acquire(self):
        slot = self.i % self.n
        self.i += 1
        if self.guard[slot] is not None:
            nm, v = self.guard[slot]
            self.p.wait(self.p.t, nm, v)
        return self.tiles[slot], slot

    def release(self, slot, sem_name, value):
        self.guard[slot] = (sem_name, value)


CONST_VALUES = [1e-5, 1e-6, 1e-12, 1e-9, 1.0 / 127.0, 0.5, -0.5, 2.0,
                1.0 / D, 1.0 / 16.0, 0.125, 1.0, -1.0, 0.1, 0.3, 0.7, 0.2,
                1.0 / 2048.0, -0.0625]


class TransPool:

    def __init__(self, p, name, dtype=BF16):
        self.p = p
        self.tiles = [p.ps(f"{name}{i}", [128, 128], dtype) for i in range(2)]
        self.i = 0
        self.guard = [0, 0]

    def step(self, in_ap, ident, out_ap, out_rows=128, out_cols=128):
        p = self.p
        sl = self.i % 2
        self.i += 1
        if self.guard[sl]:
            p.wait(p.t, "V", self.guard[sl])
        ps = self.tiles[sl]
        tv = p.inc(p.t.transpose(ps[0:out_rows, 0:out_cols], in_ap, ident),
                   "T")
        p.wait(p.v, "T", tv)
        vv = p.inc(p.v.tensor_copy(out_ap, ps[0:out_rows, 0:out_cols]), "V")
        self.guard[sl] = vv
        return vv


class Const:

    def __init__(self, p):
        nc = p.nc
        nvals = len(CONST_VALUES)
        self.cpool = p.sb("c_pool", [128, nvals], F32)
        self.ones_row_f = p.sb("c_ones_row_f", [1, 128], F32)
        self.ones_col_f = p.sb("c_ones_col_f", [128, 1], F32)
        self.ones_row_b = p.sb("c_ones_row_b", [1, 128], BF16)
        self.ones_col_b = p.sb("c_ones_col_b", [128, 1], BF16)
        self.ident_b = p.sb("c_ident_b", [128, 128], BF16)
        self.ident_f = p.sb("c_ident_f", [128, 128], F32)
        self.iota_pf = p.sb("c_iota_pf", [128, TT], F32)
        srcs = [
            (self.ones_row_f, np.ones((1, 128), np.float32)),
            (self.ones_col_f, np.ones((128, 1), np.float32)),
            (self.ones_row_b, np.ones((1, 128), ml_dtypes.bfloat16)),
            (self.ones_col_b, np.ones((128, 1), ml_dtypes.bfloat16)),
            (self.ident_b, np.eye(128, dtype=ml_dtypes.bfloat16)),
            (self.ident_f, np.eye(128, dtype=np.float32)),
            (self.iota_pf, (np.arange(128)[:, None] -
                            np.arange(TT)[None, :]).astype(np.float32)),
            (self.cpool, np.tile(np.asarray(CONST_VALUES, np.float32)[None, :],
                                 (128, 1))),
        ]
        for i, (dst, arr) in enumerate(srcs):
            h = p.nc.inline_tensor(arr, f"c_inl{i}")
            p.inc(p.sy.dma_start(dst[:], h.ap()), "L")

    def col(self, v):
        i = CONST_VALUES.index(v)
        return self.cpool[:, i:i + 1]

    def one(self, v):
        i = CONST_VALUES.index(v)
        return self.cpool[0:1, i:i + 1]



def chunks_view(t, nch=KC, tw=TOK):
    def view(c, t0=0, w=None):
        if w is None:
            w = tw
        return t[:, c * tw + t0: c * tw + t0 + w]
    return view


def feat_dram_view(ap2d, c, t0, w, p0=0, pn=128):
    return ap2d[c * 128 + p0: c * 128 + p0 + pn, t0: t0 + w]


def emit_weight_lhsT_load(p, hbm_f32, w_sb, din, dout, cast_engine="gpsimd"):
    nk = din // 128
    for kc in range(nk):
        dst = w_sb[:, kc * dout:(kc + 1) * dout]
        src = hbm_f32[kc * 128:(kc + 1) * 128, :]
        p.inc(p.g.dma_start(dst, src), "G")
    return p.val("G")


def w_view(w_sb, dout):
    def view(kc, fc, fw=128):
        return w_sb[:, kc * dout + fc * 128: kc * dout + fc * 128 + fw]
    return view




class LNState:

    def __init__(self, p, name, pool):
        self.p = p
        self.xs = p.sb(name + "_xs", [128, KC * TT], F32)
        self.sq = p.sb(name + "_sq", [128, 2 * TT], F32)
        self.rows = p.sb(name + "_rows", [1, 4 * TT], F32)
        self.st1 = p.ps(name + "_st1", [1, TT], F32)
        self.st2 = p.ps(name + "_st2", [1, TT], F32)
        self.bc = pool
        self.last_v = 0

    def row(self, i, t0=0, w=TT):
        return self.rows[:, i * TT + t0: i * TT + t0 + w]


def emit_ln_tile(p, c, ln, xin_ap, tt, ln_out_view, eps=1e-5, keep_x=False):
    t0 = tt * TT
    xsv = chunks_view(ln.xs, KC, TT)
    if ln.last_v:
        p.wait(p.sy, "V", ln.last_v)
    src = xin_ap.rearrange("(c p) t -> p c t", p=128)[:, :, t0:t0 + TT]
    dst = ln.xs[:].rearrange("p (c t) -> p c t", c=KC)
    lval = p.inc(p.sy.dma_start(dst, src), "L")
    p.wait(p.t, "L", lval)
    if ln.last_v:
        p.wait(p.t, "V", ln.last_v)
    rec = None
    for kc in range(KC):
        rec = p.t.matmul(ln.st1[:], c.ones_col_f[:], xsv(kc), start=(kc == 0),
                         stop=(kc == KC - 1))
    p.inc(rec, "T")
    p.wait(p.sc, "L", lval)
    tq = {}
    for kc in range(KC):
        sqv = ln.sq[:, (kc % 2) * TT:(kc % 2) * TT + TT]
        if kc >= 2:
            p.wait(p.sc, "T", tq[kc - 2])
        sk = p.inc(p.sc.activation(sqv, xsv(kc), AF.Square), "S")
        p.wait(p.t, "S", sk)
        tq[kc] = p.inc(p.t.matmul(ln.st2[:], c.ones_col_f[:], sqv,
                                  start=(kc == 0), stop=(kc == KC - 1)), "T")
    tsum = p.val("T")
    p.wait(p.v, "T", tsum)
    p.v.tensor_scalar_mul(ln.row(0), ln.st1[:], 1.0 / D)
    p.v.tensor_tensor(ln.row(3), ln.row(0), ln.row(0), ALU.mult)
    vv = p.inc(p.v.scalar_tensor_tensor(ln.row(1), ln.st2[:], 1.0 / D,
                                        ln.row(3), ALU.mult, ALU.subtract),
               "V")
    p.wait(p.sc, "V", vv)
    sv = p.inc(p.sc.activation(ln.row(3), ln.row(1), AF.Sqrt, bias=c.one(eps)), "S")
    p.wait(p.v, "S", sv)
    vv = p.inc(p.v.reciprocal(ln.row(2), ln.row(3)), "V")
    p.wait(p.t, "V", vv)
    bm, sm = ln.bc.acquire()
    p.t.matmul(bm[:], c.ones_row_f[:], ln.row(0), start=True, stop=True)
    br, sr = ln.bc.acquire()
    tb = p.inc(p.t.matmul(br[:], c.ones_row_f[:], ln.row(2), start=True,
                          stop=True), "T")
    p.wait(p.v, "T", tb)
    for kc in range(KC):
        tmp = ln.sq[:, (kc % 2) * TT:(kc % 2) * TT + TT]
        p.v.tensor_tensor(tmp, xsv(kc), bm[:], ALU.subtract)
        vv = p.inc(p.v.tensor_tensor(ln_out_view(kc), tmp, br[:], ALU.mult),
                   "V")
    ln.bc.release(sm, "V", vv)
    ln.bc.release(sr, "V", vv)
    ln.last_v = vv
    return vv


class ResidualState:
    def __init__(self, p, name, n=2):
        self.buf = [p.sb(f"{name}_res{i}", [128, 3 * TT], F32)
                    for i in range(n)]
        self.n = n
        self.i = 0
        self.guard = [None] * n


def emit_residual_store(p, c, res, xin_ap, xout_ap, fc, tt, psum_ap, alpha,
                        tval, bias_col=None, scale_col=None):
    if not isinstance(psum_ap, bass.AP):
        psum_ap = psum_ap[:]
    slot = res.i % res.n
    res.i += 1
    buf = res.buf[slot]
    if res.guard[slot] is not None:
        p.wait(p.sy, "O", res.guard[slot])
    lv = p.inc(p.sy.dma_start(buf[:, 0:TT],
                              feat_dram_view(xin_ap, fc, tt * TT, TT)), "L")
    p.wait(p.v, "L", lv)
    p.wait(p.v, "T", tval)
    if bias_col is None and scale_col is None:
        vv = p.inc(p.v.scalar_tensor_tensor(buf[:, TT:2 * TT], psum_ap, alpha,
                                            buf[:, 0:TT], ALU.mult, ALU.add),
                   "V")
    else:
        tmp = buf[:, 2 * TT:3 * TT]
        if bias_col is not None:
            p.v.tensor_scalar(tmp, psum_ap, bias_col, alpha, ALU.add, ALU.mult)
        else:
            p.v.tensor_scalar(tmp, psum_ap, scale_col, alpha, ALU.mult,
                              ALU.mult)
        vv = p.inc(p.v.tensor_tensor(buf[:, TT:2 * TT], tmp, buf[:, 0:TT],
                                     ALU.add), "V")
    p.wait(p.sy, "V", vv)
    ov = p.inc(p.sy.dma_start(feat_dram_view(xout_ap, fc, tt * TT, TT),
                              buf[:, TT:2 * TT]), "O")
    res.guard[slot] = ov
    return vv



def emit_proj(p, wv, nk, in_view, nf, epilogue, pool, tts=None):
    if tts is None:
        tts = range(NTT)
    for fc in range(nf // 128):
        for tt in tts:
            ps_ap, slot = pool.acquire()
            rec = None
            for kc in range(nk):
                rec = p.t.matmul(ps_ap[:], wv(kc, fc), in_view(kc, tt),
                                 start=(kc == 0), stop=(kc == nk - 1))
            tval = p.inc(rec, "T")
            epilogue(fc, tt, ps_ap, slot, tval)



class ConvWork:

    def __init__(self, p):
        self.stage = [p.sb(f"cv_st{i}", [128, 4096], BF16) for i in range(2)]
        self.tbuf = [p.sb(f"cv_tb{i}", [128, 4096], BF16) for i in range(2)]
        self.pst = [p.ps(f"cv_ps{i}", [128, 128], BF16) for i in range(2)]
        self.stage_guard = [0, 0]
        self.stage_cast_guard = [0, 0]
        self.tbuf_guard = [0, 0]
        self.pst_guard = [0, 0]


def emit_convert_lhsT(p, c, cw, src_ap, dst_ap, R, C):
    nr = R // 128
    ncc = C // 128
    for rc in range(nr):
        sl = rc % 2
        if cw.stage_guard[sl]:
            p.wait(p.g, "T", cw.stage_guard[sl])
        gv = p.inc(p.g.dma_start(cw.stage[sl][:, 0:C],
                                 src_ap[rc * 128:(rc + 1) * 128, :]), "G")
        p.wait(p.t, "G", gv)
        tbl = rc % 2
        if cw.tbuf_guard[tbl]:
            p.wait(p.v, "O", cw.tbuf_guard[tbl])
        vv = 0
        for cc in range(ncc):
            psl = cc % 2
            if cw.pst_guard[psl]:
                p.wait(p.t, "V", cw.pst_guard[psl])
            tv = p.inc(p.t.transpose(cw.pst[psl][:],
                                     cw.stage[sl][:, cc * 128:(cc + 1) * 128],
                                     c.ident_b[:]), "T")
            cw.stage_guard[sl] = tv
            p.wait(p.v, "T", tv)
            vv = p.inc(p.v.tensor_copy(
                cw.tbuf[tbl][:, cc * 128:(cc + 1) * 128], cw.pst[psl][:]),
                "V")
            cw.pst_guard[psl] = vv
        p.wait(p.sy, "V", vv)
        dstv = dst_ap[rc].rearrange("k p f -> p k f")
        srcv = cw.tbuf[tbl][:, 0:ncc * 128].rearrange("p (k f) -> p k f",
                                                      f=128)
        ov = p.inc(p.sy.dma_start(dstv, srcv), "O")
        cw.tbuf_guard[tbl] = ov
    return p.val("O")


def emit_load_wconv(p, cw_ap4, w_sb, nk, fcs, queue="Q", w_sb_off=0):
    px = p.sc if queue == "Q" else (p.sy if queue in ("L", "O") else p.g)
    for i, fc in enumerate(fcs):
        src = cw_ap4[fc].rearrange("k p f -> p k f")
        dst = w_sb[:, w_sb_off + i * nk * 128: w_sb_off + (i + 1) * nk * 128]
        dst = dst.rearrange("p (k f) -> p k f", f=128)
        p.inc(px.dma_start(dst, src), queue)
    return p.val(queue)


def wconv_view(w_sb, nk, off=0):
    def view(kc, i, fw=128):
        base = off + (i * nk + kc) * 128
        return w_sb[:, base: base + fw]
    return view


def emit_load_w(p, hbm_f32, w_sb, din, dout):
    nk = din // 128
    for kc in range(nk):
        dst = w_sb[:, kc * dout:(kc + 1) * dout]
        src = hbm_f32[kc * 128:(kc + 1) * 128, :]
        p.inc(p.g.dma_start(dst, src), "G")
    return p.val("G")



def emit_l1_ffn(p, c, xin_ap, xout_ap, cwg, cwu, cwd):
    H = 3072
    NH = H // 128
    with p.scope():
        wg = p.sb("l1f_wg", [128, NH * KC * 128], BF16)
        wu = p.sb("l1f_wu", [128, NH * KC * 128], BF16)
        wdb = [p.sb(f"l1f_wd{i}", [128, NH * 128], BF16) for i in range(2)]
        emit_load_wconv(p, cwg, wg, KC, range(NH), "Q")
        qv = emit_load_wconv(p, cwu, wu, KC, range(NH), "Q")
        wgv, wuv = wconv_view(wg, KC), wconv_view(wu, KC)
        wdv = [wconv_view(wdb[0], 1), wconv_view(wdb[1], 1)]
        pool = PsumPool(p, "l1f_ps", 4, [128, TT], F32)
        ln = LNState(p, "l1f_ln", pool)
        lnb = p.sb("l1f_lnb", [128, KC * TT], BF16)
        hbuf = p.sb("l1f_h", [128, NH * TT], BF16)
        silu_t = p.sb("l1f_silu", [128, TT], F32)
        res = ResidualState(p, "l1f")
        p.wait(p.t, "Q", qv)
        lnv = chunks_view(lnb, KC, TT)
        hv = chunks_view(hbuf, NH, TT)
        last_hv = None
        wd_guard = [0, 0]

        def load_wd(fc):
            sl = fc % 2
            if wd_guard[sl]:
                p.wait(p.sc, "T", wd_guard[sl])
            return emit_load_wconv(p, cwd, wdb[sl], NH, [fc], "Q")

        for tt in range(NTT):
            lnt = emit_ln_tile(p, c, ln, xin_ap, tt, lambda kc: lnv(kc))
            p.wait(p.t, "V", lnt)
            for fc in range(NH):
                psg, sg = pool.acquire()
                rec = None
                for kc in range(KC):
                    rec = p.t.matmul(psg[:], wgv(kc, fc), lnv(kc),
                                     start=(kc == 0), stop=(kc == KC - 1))
                tg = p.inc(rec, "T")
                psu, su = pool.acquire()
                rec = None
                for kc in range(KC):
                    rec = p.t.matmul(psu[:], wuv(kc, fc), lnv(kc),
                                     start=(kc == 0), stop=(kc == KC - 1))
                tu = p.inc(rec, "T")
                p.wait(p.sc, "T", tg)
                if last_hv is not None:
                    p.wait(p.sc, "V", last_hv)
                sv = p.inc(p.sc.activation(silu_t[:], psg[:], AF.Silu,
                                           scale=c.col(1.0 / 127.0)), "S")
                p.wait(p.v, "S", sv)
                p.wait(p.v, "T", tu)
                last_hv = p.inc(
                    p.v.scalar_tensor_tensor(hv(fc), psu[:], 1.0 / 127.0,
                                             silu_t[:], ALU.mult, ALU.mult),
                    "V")
                pool.release(sg, "S", sv)
                pool.release(su, "V", last_hv)
            p.wait(p.t, "V", last_hv)
            qfc = {0: load_wd(0)}
            for fc in range(KC):
                if fc + 1 < KC:
                    qfc[fc + 1] = load_wd(fc + 1)
                ps_ap, slot = pool.acquire()
                p.wait(p.t, "Q", qfc[fc])
                rec = None
                for kc in range(NH):
                    rec = p.t.matmul(ps_ap[:], wdv[fc % 2](kc, 0), hv(kc),
                                     start=(kc == 0), stop=(kc == NH - 1))
                tval = p.inc(rec, "T")
                wd_guard[fc % 2] = tval
                vv = emit_residual_store(p, c, res, xin_ap, xout_ap, fc, tt,
                                         ps_ap[:], 0.5 / 127.0, tval)
                pool.release(slot, "V", vv)
            p.wait(p.v, "T", p.val("T"))


def emit_l3_ffn(p, c, xin_ap, xout_ap, w):
    FF = 4096
    NF = FF // 128
    R = 256
    NR = R // 128
    with p.scope():
        u1 = p.sb("l3f_u1", [128, KC * R], BF16)
        v1 = p.sb("l3f_v1", [128, NR * FF], BF16)
        u2 = p.sb("l3f_u2", [128, NF * R], BF16)
        v2 = p.sb("l3f_v2", [128, NR * D], BF16)
        emit_load_w(p, w["U1"], u1, D, R)
        emit_load_w(p, w["V1"], v1, R, FF)
        emit_load_w(p, w["U2"], u2, FF, R)
        gv = emit_load_w(p, w["V2"], v2, R, D)
        b1sb = p.sb("l3f_b1", [128, NF], F32)
        b2sb = p.sb("l3f_b2", [128, KC], F32)
        lv = p.inc(p.sy.dma_start(
            b1sb[:], w["b1"].rearrange("(c p) -> p c", p=128),
            allow_slow_non_contiguous=True), "L")
        lv = p.inc(p.sy.dma_start(
            b2sb[:], w["b2"].rearrange("(c p) -> p c", p=128),
            allow_slow_non_contiguous=True), "L")
        u1v, v1v = w_view(u1, R), w_view(v1, FF)
        u2v, v2v = w_view(u2, R), w_view(v2, D)
        pool = PsumPool(p, "l3f_ps", 4, [128, TT], F32)
        ln = LNState(p, "l3f_ln", pool)
        lnb = p.sb("l3f_lnb", [128, KC * TT], BF16)
        pbuf = p.sb("l3f_p", [128, NR * TT], BF16)
        hbuf = p.sb("l3f_h", [128, NF * TT], BF16)
        qbuf = p.sb("l3f_q", [128, NR * TT], BF16)
        res = ResidualState(p, "l3f")
        p.wait(p.t, "G", gv)
        p.wait(p.t, "L", lv)
        lnv = chunks_view(lnb, KC, TT)
        pv = chunks_view(pbuf, NR, TT)
        hv = chunks_view(hbuf, NF, TT)
        qv = chunks_view(qbuf, NR, TT)

        for tt in range(NTT):
            lnt = emit_ln_tile(p, c, ln, xin_ap, tt, lambda kc: lnv(kc))
            p.wait(p.t, "V", lnt)

            def epi_p(fc, _tt, ps_ap, slot, tval):
                p.wait(p.sc, "T", tval)
                sv = p.inc(p.sc.activation(pv(fc), ps_ap[:], AF.Copy), "S")
                pool.release(slot, "S", sv)
            emit_proj(p, u1v, KC, lambda kc, _tt: lnv(kc), R, epi_p, pool,
                      tts=[tt])
            p.wait(p.t, "S", p.val("S"))

            def epi_h(fc, _tt, ps_ap, slot, tval):
                p.wait(p.sc, "T", tval)
                sv = p.inc(p.sc.activation(hv(fc), ps_ap[:], AF.Gelu_apprx_tanh,
                                           bias=b1sb[:, fc:fc + 1]), "S")
                pool.release(slot, "S", sv)
            emit_proj(p, v1v, NR, lambda kc, _tt: pv(kc), FF, epi_h, pool,
                      tts=[tt])
            p.wait(p.t, "S", p.val("S"))

            def epi_q(fc, _tt, ps_ap, slot, tval):
                p.wait(p.sc, "T", tval)
                sv = p.inc(p.sc.activation(qv(fc), ps_ap[:], AF.Copy), "S")
                pool.release(slot, "S", sv)
            emit_proj(p, u2v, NF, lambda kc, _tt: hv(kc), R, epi_q, pool,
                      tts=[tt])
            p.wait(p.t, "S", p.val("S"))

            def epi_o(fc, _tt, ps_ap, slot, tval):
                vv = emit_residual_store(p, c, res, xin_ap, xout_ap, fc, _tt,
                                         ps_ap, 0.5, tval,
                                         bias_col=b2sb[:, fc:fc + 1])
                pool.release(slot, "V", vv)
            emit_proj(p, v2v, NR, lambda kc, _tt: qv(kc), D, epi_o, pool,
                      tts=[tt])
            p.wait(p.sc, "T", p.val("T"))
            p.wait(p.v, "T", p.val("T"))



INPUT_SPECS = {
    "L0_attn_basis": ([1024, 256], F32), "L0_attn_q_coeffs": ([256, 256], F32),
    "L0_attn_k_coeffs": ([256, 256], F32), "L0_attn_v_coeffs": ([256, 256], F32),
    "L0_attn_o_coeffs": ([256, 256], F32), "L0_attn_decay_logit": ([1], F32),
    "L0_attn_out_scale": ([1], F32),
    "L0_ffn_cn_w1": ([1024, 1024], F32), "L0_ffn_cn_b1": ([1024], F32),
    "L0_ffn_cn_w2": ([1024, 512], F32), "L0_ffn_cn_b2": ([512], F32),
    "L0_ffn_cn_w3": ([512, 256], F32), "L0_ffn_cn_b3": ([256], F32),
    "L0_ffn_cn_w4": ([256, 1], F32), "L0_ffn_cn_b4": ([1], F32),
    "L0_ffn_gc_w1": ([1024, 512], F32), "L0_ffn_gc_b1": ([512], F32),
    "L0_ffn_gc_w2": ([512, 256], F32), "L0_ffn_gc_b2": ([256], F32),
    "L0_ffn_gc_w3": ([256, 1], F32), "L0_ffn_gc_b3": ([1], F32),
    "L0_ffn_adaptive_range": ([1], F32),
    "L1_attn_qkv_w": ([3072, 1024], F32), "L1_attn_temperature": ([16], F32),
    "L1_attn_proj_w": ([1024, 1024], F32), "L1_attn_proj_b": ([1024], F32),
    "L1_ffn_w_gate": ([3072, 1024], I32), "L1_ffn_w_up": ([3072, 1024], I32),
    "L1_ffn_w_down": ([1024, 3072], I32),
    "L2_attn_W_lambda": ([16, 1024], F32), "L2_attn_b_lambda": ([16], F32),
    "L2_attn_W_B": ([16, 1024], F32), "L2_attn_W_C": ([1024, 16], F32),
    "L2_attn_W_gate": ([1024, 1024], F32), "L2_attn_rms_weight": ([1024], F32),
    "L2_ffn_gate_w": ([8, 1024], F32), "L2_ffn_w1": ([8, 1024, 4096], F32),
    "L2_ffn_b1": ([8, 4096], F32), "L2_ffn_w2": ([8, 4096, 1024], F32),
    "L2_ffn_b2": ([8, 1024], F32),
    "L3_attn_Wq": ([1024, 1024], F32), "L3_attn_bq": ([1024], F32),
    "L3_attn_Wk": ([1024, 1024], F32), "L3_attn_bk": ([1024], F32),
    "L3_attn_Wv": ([1024, 1024], F32), "L3_attn_bv": ([1024], F32),
    "L3_attn_Wo": ([1024, 1024], F32), "L3_attn_bo": ([1024], F32),
    "L3_attn_rk_table": ([33, 64], F32), "L3_attn_rv_table": ([33, 64], F32),
    "L3_ffn_U1": ([1024, 256], F32), "L3_ffn_V1": ([256, 4096], F32),
    "L3_ffn_b1": ([4096], F32), "L3_ffn_U2": ([4096, 256], F32),
    "L3_ffn_V2": ([256, 1024], F32), "L3_ffn_b2": ([1024], F32),
}

def emit_convert_cast(p, cw, src2d, dst2d, R, C):
    for rc in range(R // 128):
        sl = rc % 2
        if cw.stage_cast_guard[sl]:
            p.wait(p.g, "O", cw.stage_cast_guard[sl])
        gv = p.inc(p.g.dma_start(cw.stage[sl][:, 0:C],
                                 src2d[rc * 128:(rc + 1) * 128, :]), "G")
        p.wait(p.sy, "G", gv)
        cw.stage_cast_guard[sl] = p.inc(
            p.sy.dma_start(dst2d[rc * 128:(rc + 1) * 128, :],
                           cw.stage[sl][:, 0:C]), "O")
    return p.val("O")


CONVERT_W = {
    "L1_ffn_w_gate": (3072, 1024), "L1_ffn_w_up": (3072, 1024),
    "L1_ffn_w_down": (1024, 3072),
    "L1_attn_qkv_w": (3072, 1024), "L1_attn_proj_w": (1024, 1024),
    "L2_attn_W_gate": (1024, 1024),
    "L3_attn_Wq": (1024, 1024), "L3_attn_Wk": (1024, 1024),
    "L3_attn_Wv": (1024, 1024), "L3_attn_Wo": (1024, 1024),
}

KIND_CONVERTS = {
    "l1_ffn": ["L1_ffn_w_gate", "L1_ffn_w_up", "L1_ffn_w_down"],
    "l1_attn": ["L1_attn_qkv_w", "L1_attn_proj_w"],
    "l2_attn": ["L2_attn_W_gate"],
    "l3_attn": ["L3_attn_Wq", "L3_attn_Wk", "L3_attn_Wv", "L3_attn_Wo"],
}

CAST_W = {"L2_ffn_w1": (8192, 4096), "L2_ffn_w2": (32768, 1024)}
KIND_CASTS = {"l2_ffn": ["L2_ffn_w1", "L2_ffn_w2"]}


def build_chain(specs, a2a_before=None):
    nc = bass.Bass("TRN2", target_bir_lowering=False)
    xin = nc.dram_tensor("xin", [D, TOK], F32, kind="ExternalInput").ap()
    xout = nc.dram_tensor("xout", [D, TOK], F32, kind="ExternalOutput").ap()
    used = set()
    for kind, pref in specs:
        for name in INPUT_SPECS:
            if name.startswith(pref):
                used.add(name)
    ins = {}
    for name in sorted(used):
        shape, dt = INPUT_SPECS[name]
        ins[name] = nc.dram_tensor(name, shape, dt, kind="ExternalInput").ap()

    with ExitStack() as ctx:
        p = Prog(nc, ctx)
        c = Const(p)
        p.barrier()
        conv = {}
        need_conv = []
        need_cast = []
        for kind, pref in specs:
            for nm in KIND_CONVERTS.get(kind, []):
                if nm in used and nm not in conv:
                    conv[nm] = None
                    need_conv.append(nm)
            for nm in KIND_CASTS.get(kind, []):
                if nm in used and nm not in conv:
                    conv[nm] = None
                    need_cast.append(nm)
        if need_conv or need_cast:
            with p.scope():
                cw = ConvWork(p)
                for nm in need_conv:
                    R, C = CONVERT_W[nm]
                    dst = nc.dram_tensor("cw_" + nm,
                                         [R // 128, C // 128, 128, 128], BF16,
                                         kind="Internal").ap()
                    emit_convert_lhsT(p, c, cw, ins[nm], dst, R, C)
                    conv[nm] = dst
                for nm in need_cast:
                    R, C = CAST_W[nm]
                    dst = nc.dram_tensor("cw_" + nm, [R, C], BF16,
                                         kind="Internal").ap()
                    src_flat = ins[nm]
                    if len(INPUT_SPECS[nm][0]) == 3:
                        src_flat = src_flat.rearrange("n a b -> (n a) b")
                    emit_convert_cast(p, cw, src_flat, dst, R, C)
                    conv[nm] = dst
        p.barrier()
        n = len(specs)
        mids = [nc.dram_tensor(f"xmid{i}", [D, TOK], F32, kind="Internal").ap()
                for i in range(n - 1)]
        xs = [xin] + mids + [xout]
        if a2a_before is not None:
            a2a_in = nc.dram_tensor("a2a_in", [NCORES * D, TT], F32,
                                    kind="Internal").ap()
            a2a_out = nc.dram_tensor("a2a_out", [NCORES * D, TT], F32,
                                     kind="Internal").ap()
        for i, (kind, pref) in enumerate(specs):
            if a2a_before is not None and i == a2a_before:
                p.name_prefix = f"a2a_"
                emit_a2a_reshard(p, xs[i], a2a_in, a2a_out)
                p.barrier()
            p.name_prefix = f"s{i}_"
            emit_sublayer(p, c, kind, pref, xs[i], xs[i + 1], ins, conv)
            p.barrier()
        p.name_prefix = ""
        from concourse.aot_env import allow_ici_all_to_all
        with allow_ici_all_to_all():
            p.play()
    return nc, sorted(used)


def emit_a2a_reshard(p, x_ap, a2a_in, a2a_out):
    for j in range(NCORES):
        for b in range(2):
            p.inc(p.sy.dma_start(
                a2a_in[j * D:(j + 1) * D, b * 256:(b + 1) * 256],
                x_ap[:, b * 2048 + j * 256:b * 2048 + (j + 1) * 256]), "O")
    p.wait(p.g, "O", p.val("O"))
    rec = p.g.collective_compute(
        "AllToAll", ALU.bypass, replica_groups=[list(range(NCORES))],
        ins=[a2a_in.opt()], outs=[a2a_out.opt()])
    p.inc(rec, "P", 1)
    p.wait(p.sy, "P", p.val("P"))
    for i in range(NCORES):
        for b in range(2):
            p.inc(p.sy.dma_start(
                x_ap[:, (2 * i + b) * 256:(2 * i + b + 1) * 256],
                a2a_out[i * D:(i + 1) * D, b * 256:(b + 1) * 256]), "O")
    p.wait(p.sy, "O", p.val("O"))


def emit_sublayer(p, card, kind, pref, xi, xo, ins, conv):
    if kind == "l1_ffn":
        emit_l1_ffn(p, card, xi, xo, conv[pref + "w_gate"],
                    conv[pref + "w_up"], conv[pref + "w_down"])
    elif kind == "l3_ffn":
        w = {k: ins[pref + k] for k in ("U1", "V1", "b1", "U2", "V2", "b2")}
        emit_l3_ffn(p, card, xi, xo, w)
    elif kind == "l3_attn":
        w = {k: ins[pref + k] for k in ("bq", "bk", "bv", "bo", "rk_table",
                                        "rv_table")}
        emit_l3_attn(p, card, xi, xo, w, conv[pref + "Wq"], conv[pref + "Wk"],
                     conv[pref + "Wv"], conv[pref + "Wo"])
    elif kind == "l1_attn":
        w = {k: ins[pref + k] for k in ("temperature", "proj_b")}
        emit_l1_attn(p, card, xi, xo, w, conv[pref + "qkv_w"],
                     conv[pref + "proj_w"])
    elif kind == "l0_attn":
        w = {k: ins[pref + k] for k in ("basis", "q_coeffs", "k_coeffs",
                                        "v_coeffs", "o_coeffs", "decay_logit",
                                        "out_scale")}
        emit_l0_attn(p, card, xi, xo, w)
    elif kind == "l2_ffn":
        w = {k: ins[pref + k] for k in ("gate_w", "b1", "b2")}
        emit_l2_ffn(p, card, xi, xo, w, conv[pref + "w1"], conv[pref + "w2"])
    elif kind == "l2_attn":
        w = {k: ins[pref + k] for k in ("W_lambda", "b_lambda", "W_B", "W_C",
                                        "rms_weight")}
        emit_l2_attn(p, card, xi, xo, w, conv[pref + "W_gate"])
    elif kind == "l0_ffn":
        keys = ["cn_w%d" % i for i in (1, 2, 3, 4)]
        keys += ["cn_b%d" % i for i in (1, 2, 3, 4)]
        keys += ["gc_w%d" % i for i in (1, 2, 3)]
        keys += ["gc_b%d" % i for i in (1, 2, 3)]
        keys += ["adaptive_range"]
        w = {k: ins[pref + k] for k in keys}
        emit_l0_ffn(p, card, xi, xo, w)
    else:
        raise NotImplementedError(kind)


def emit_l0_ffn(p, c, xin_ap, xout_ap, w):
    with p.scope():
        pool = PsumPool(p, "l0f_ps", 4, [128, TT], F32)
        prow = p.ps("l0f_prow", [1, TT], F32)
        ln = LNState(p, "l0f_ln", pool)
        w1 = p.sb("l0f_w1", [128, KC * 1024], BF16)
        w2 = p.sb("l0f_w2", [128, KC * 512], BF16)
        w3 = p.sb("l0f_w3", [128, 4 * 256], BF16)
        w4 = p.sb("l0f_w4", [128, 2 * 1], BF16)
        g1 = p.sb("l0f_g1", [128, KC * 512], BF16)
        g2 = p.sb("l0f_g2", [128, 4 * 256], BF16)
        g3 = p.sb("l0f_g3", [128, 2 * 1], BF16)
        emit_load_w(p, w["cn_w1"], w1, 1024, 1024)
        emit_load_w(p, w["cn_w2"], w2, 1024, 512)
        emit_load_w(p, w["cn_w3"], w3, 512, 256)
        emit_load_w(p, w["cn_w4"], w4, 256, 1)
        emit_load_w(p, w["gc_w1"], g1, 1024, 512)
        emit_load_w(p, w["gc_w2"], g2, 512, 256)
        gv = emit_load_w(p, w["gc_w3"], g3, 256, 1)
        b1c = p.sb("l0f_b1", [128, 8], F32)
        b2c = p.sb("l0f_b2", [128, 4], F32)
        b3c = p.sb("l0f_b3", [128, 2], F32)
        gb1c = p.sb("l0f_gb1", [128, 4], F32)
        gb2c = p.sb("l0f_gb2", [128, 2], F32)
        smalls = p.sb("l0f_small", [1, 8], F32)
        p.inc(p.sy.dma_start(b1c[:], w["cn_b1"].rearrange("(c p) -> p c", p=128),
                             allow_slow_non_contiguous=True), "L")
        p.inc(p.sy.dma_start(b2c[:], w["cn_b2"].rearrange("(c p) -> p c", p=128),
                             allow_slow_non_contiguous=True), "L")
        p.inc(p.sy.dma_start(b3c[:], w["cn_b3"].rearrange("(c p) -> p c", p=128),
                             allow_slow_non_contiguous=True), "L")
        p.inc(p.sy.dma_start(gb1c[:], w["gc_b1"].rearrange("(c p) -> p c", p=128),
                             allow_slow_non_contiguous=True), "L")
        p.inc(p.sy.dma_start(gb2c[:], w["gc_b2"].rearrange("(c p) -> p c", p=128),
                             allow_slow_non_contiguous=True), "L")
        p.inc(p.sy.dma_start(smalls[:, 0:1], w["cn_b4"][0:1]), "L")
        p.inc(p.sy.dma_start(smalls[:, 1:2], w["gc_b3"][0:1]), "L")
        lv = p.inc(p.sy.dma_start(smalls[:, 2:3], w["adaptive_range"][0:1]), "L")

        lnf = p.sb("l0f_lnf", [128, KC * TT], F32)
        out1 = p.sb("l0f_out1", [128, KC * TT], F32)
        outb = p.sb("l0f_outb", [128, KC * TT], BF16)
        h1 = p.sb("l0f_h1", [128, KC * TT], BF16)
        h2 = p.sb("l0f_h2", [128, 4 * TT], BF16)
        h3 = p.sb("l0f_h3", [128, 2 * TT], BF16)
        rows = p.sb("l0f_rows", [1, 4 * TT], F32)
        gacc = p.sb("l0f_gacc", [128, 2 * KC], F32)
        gred = p.sb("l0f_gred", [128, KC], F32)
        gcb = p.sb("l0f_gcb", [128, 2 * KC], BF16)
        gh1 = p.sb("l0f_gh1", [128, 4 * 2], BF16)
        gh2 = p.sb("l0f_gh2", [128, 2 * 2], BF16)
        res = ResidualState(p, "l0f")
        twtmp = p.sb("l0f_twtmp", [128, TT], F32)
        lnfv = chunks_view(lnf, KC, TT)
        out1v = chunks_view(out1, KC, TT)
        outbv = chunks_view(outb, KC, TT)
        h1v = chunks_view(h1, KC, TT)
        h2v = chunks_view(h2, 4, TT)
        h3v = chunks_view(h3, 2, TT)
        w1v, w2v = w_view(w1, 1024), w_view(w2, 512)
        w3v, w4v = w_view(w3, 256), w_view(w4, 1)
        g1v, g2v, g3v = w_view(g1, 512), w_view(g2, 256), w_view(g3, 1)
        p.wait(p.t, "G", gv)
        p.wait(p.t, "L", lv)

        def row(i, w_=TT):
            return rows[:, i * TT:i * TT + w_]

        for tt in range(NTT):
            b = tt // (NTT // 2)
            lnt = emit_ln_tile(p, c, ln, xin_ap, tt, lambda kc: lnfv(kc))
            for kc in range(KC):
                p.v.reduce_sum(gred[:, kc:kc + 1], lnfv(kc), axis=AX.X)
            if tt % 4 == 0:
                vv = p.inc(p.v.tensor_copy(gacc[:, b * KC:(b + 1) * KC],
                                           gred[:]), "V")
            else:
                vv = p.inc(p.v.tensor_tensor(gacc[:, b * KC:(b + 1) * KC],
                                             gacc[:, b * KC:(b + 1) * KC],
                                             gred[:], ALU.add), "V")
            ln.last_v = vv
        src_r = bass.AP(gacc, 0, [[2 * KC, 128], [1, KC], [KC, 2]])
        dst_r = gcb[:].rearrange("p (k b) -> p k b", b=2)
        vv = p.inc(p.v.tensor_scalar_mul(dst_r, src_r, 1.0 / 2048.0), "V")
        p.wait(p.t, "V", vv)

        def gc_rhs(kc):
            return gcb[:, kc * 2:kc * 2 + 2]
        for fc in range(4):
            ps_ap, slot = pool.acquire()
            rec = None
            for kc in range(KC):
                rec = p.t.matmul(ps_ap[:, 0:2], g1v(kc, fc), gc_rhs(kc),
                                 start=(kc == 0), stop=(kc == KC - 1))
            tvv = p.inc(rec, "T")
            p.wait(p.sc, "T", tvv)
            sv = p.inc(p.sc.activation(gh1[:, fc * 2:fc * 2 + 2], ps_ap[:, 0:2],
                                       AF.Gelu, bias=gb1c[:, fc:fc + 1]), "S")
            pool.release(slot, "S", sv)

        def gh1_rhs(kc):
            return bass.AP(gh1, kc * 2, [[8, 128], [1, 2]])
        p.wait(p.t, "S", p.val("S"))
        for fc in range(2):
            ps_ap, slot = pool.acquire()
            rec = None
            for kc in range(4):
                rec = p.t.matmul(ps_ap[:, 0:2], g2v(kc, fc), gh1_rhs(kc),
                                 start=(kc == 0), stop=(kc == 3))
            tvv = p.inc(rec, "T")
            p.wait(p.sc, "T", tvv)
            sv = p.inc(p.sc.activation(gh2[:, fc * 2:fc * 2 + 2], ps_ap[:, 0:2],
                                       AF.Gelu, bias=gb2c[:, fc:fc + 1]), "S")
            pool.release(slot, "S", sv)

        def gh2_rhs(kc):
            return bass.AP(gh2, kc * 2, [[4, 128], [1, 2]])
        p.wait(p.t, "S", p.val("S"))
        rec = None
        for kc in range(2):
            rec = p.t.matmul(prow[:, 0:2], g3v(kc, 0, 1), gh2_rhs(kc),
                             start=(kc == 0), stop=(kc == 1))
        tvv = p.inc(rec, "T")
        p.wait(p.sc, "T", tvv)
        sv = p.inc(p.sc.activation(smalls[:, 4:6], prow[:, 0:2], AF.Sigmoid,
                                   bias=smalls[:, 1:2]), "S")
        p.wait(p.v, "S", sv)
        p.v.tensor_scalar_mul(smalls[:, 4:6], smalls[:, 4:6], 0.7)
        vv = p.inc(p.v.tensor_scalar_mul(smalls[:, 3:4], smalls[:, 2:3], 2.0),
                   "V")

        TWIST = [(4, 1.0), (5, 1.0), (6, -1.0), (7, -1.0),
                 (0, -1.0), (1, -1.0), (2, 1.0), (3, 1.0)]
        p.wait(p.sc, "V", vv)
        for tt in range(NTT):
            b = tt // (NTT // 2)
            lnt = emit_ln_tile(p, c, ln, xin_ap, tt, lambda kc: lnfv(kc))
            cur, nxt = lnf, out1
            for cyc in range(3):
                curv = chunks_view(cur, KC, TT)
                nxtv = chunks_view(nxt, KC, TT)
                for kc in range(KC):
                    vv = p.inc(p.v.tensor_copy(outbv(kc), curv(kc)), "V")
                p.wait(p.t, "V", vv)

                def epi1(fc, _tt, ps_ap, slot, tval):
                    p.wait(p.sc, "T", tval)
                    sv = p.inc(p.sc.activation(h1v(fc), ps_ap[:], AF.Gelu,
                                               bias=b1c[:, fc:fc + 1]), "S")
                    pool.release(slot, "S", sv)
                emit_proj(p, w1v, KC, lambda kc, _: outbv(kc), 1024, epi1,
                          pool, tts=[tt])
                p.wait(p.t, "S", p.val("S"))

                def epi2(fc, _tt, ps_ap, slot, tval):
                    p.wait(p.sc, "T", tval)
                    sv = p.inc(p.sc.activation(h2v(fc), ps_ap[:], AF.Gelu,
                                               bias=b2c[:, fc:fc + 1]), "S")
                    pool.release(slot, "S", sv)
                emit_proj(p, w2v, KC, lambda kc, _: h1v(kc), 512, epi2,
                          pool, tts=[tt])
                p.wait(p.t, "S", p.val("S"))

                def epi3(fc, _tt, ps_ap, slot, tval):
                    p.wait(p.sc, "T", tval)
                    sv = p.inc(p.sc.activation(h3v(fc), ps_ap[:], AF.Gelu,
                                               bias=b3c[:, fc:fc + 1]), "S")
                    pool.release(slot, "S", sv)
                emit_proj(p, w3v, 4, lambda kc, _: h2v(kc), 256, epi3,
                          pool, tts=[tt])
                p.wait(p.t, "S", p.val("S"))

                rec = None
                for kc in range(2):
                    rec = p.t.matmul(prow[:], w4v(kc, 0, 1), h3v(kc),
                                     start=(kc == 0), stop=(kc == 1))
                tvv = p.inc(rec, "T")
                p.wait(p.sc, "T", tvv)
                sv = p.inc(p.sc.activation(row(0), prow[:], AF.Sigmoid,
                                           bias=smalls[:, 0:1]), "S")
                p.wait(p.v, "S", sv)
                p.v.tensor_scalar(row(2), row(0), 0.3, smalls[:, 4 + b:5 + b],
                                  ALU.mult, ALU.add)
                p.v.tensor_scalar(row(2), row(2), -0.5, None, ALU.add)
                vv = p.inc(p.v.tensor_scalar(row(1), row(2),
                                             smalls[:, 3:4], 0.1,
                                             ALU.mult, ALU.add), "V")
                p.wait(p.t, "V", vv)
                bcp, bslot = pool.acquire()
                tvv = p.inc(p.t.matmul(bcp[:], c.ones_row_f[:], row(1),
                                       start=True, stop=True), "T")
                p.wait(p.v, "T", tvv)
                for kc in range(KC):
                    srckc, sgn = TWIST[kc]
                    op = ALU.add if sgn > 0 else ALU.subtract
                    p.v.tensor_tensor(twtmp[:], bcp[:], curv(srckc), ALU.mult)
                    vv = p.inc(p.v.tensor_tensor(nxtv(kc), curv(kc), twtmp[:],
                                                 op), "V")
                pool.release(bslot, "V", vv)
                cur, nxt = nxt, (lnf if cur is out1 else out1)
            curv = chunks_view(cur, KC, TT)
            for kc in range(KC):
                slot = res.i % res.n
                res.i += 1
                buf = res.buf[slot]
                if res.guard[slot] is not None:
                    p.wait(p.sy, "O", res.guard[slot])
                lv = p.inc(p.sy.dma_start(buf[:, 0:TT],
                                          feat_dram_view(xin_ap, kc, tt * TT,
                                                         TT)), "L")
                p.wait(p.v, "L", lv)
                vv = p.inc(p.v.scalar_tensor_tensor(buf[:, TT:2 * TT],
                                                    curv(kc), 0.5,
                                                    buf[:, 0:TT], ALU.mult,
                                                    ALU.add), "V")
                p.wait(p.sy, "V", vv)
                ov = p.inc(p.sy.dma_start(feat_dram_view(xout_ap, kc, tt * TT,
                                                         TT),
                                          buf[:, TT:2 * TT]), "O")
                res.guard[slot] = ov
            ln.last_v = p.val("V")
            p.wait(p.v, "T", p.val("T"))



def emit_l2_attn(p, c, xin_ap, xout_ap, w, cw_gate):
    N = 16
    SEQ = 2048
    with p.scope():
        pool = PsumPool(p, "l2a_ps", 3, [128, TT], F32)
        prow = p.ps("l2a_prow", [1, TT], F32)
        tp = TransPool(p, "l2a_tp")
        ln = LNState(p, "l2a_ln", pool)
        wgate = p.sb("l2a_wg", [128, KC * KC * 128], BF16)
        emit_load_wconv(p, cw_gate, wgate, KC, range(KC), "Q")
        wgv = wconv_view(wgate, KC)
        wl_f = p.sb("l2a_wlf", [16, 1024], F32)
        wl_b = p.sb("l2a_wlb", [16, 1024], BF16)
        wlamT = p.sb("l2a_wlT", [128, KC * N], BF16)
        wbT = p.sb("l2a_wbT", [128, KC * N], BF16)
        wc_f = p.sb("l2a_wcf", [128, KC * N], F32)
        wc_b = p.sb("l2a_wcb", [128, KC * N], BF16)
        wcT = p.sb("l2a_wcT", [16, KC * 128], BF16)
        bcol = p.sb("l2a_bcol", [16, 1], F32)
        rmsw = p.sb("l2a_rmsw", [128, KC], F32)
        lv = p.inc(p.sy.dma_start(bcol[:],
                                  w["b_lambda"].rearrange("(p o) -> p o", o=1)),
                   "L")
        lv = p.inc(p.sy.dma_start(rmsw[:],
                                  w["rms_weight"].rearrange("(c p) -> p c",
                                                            p=128),
                                  allow_slow_non_contiguous=True), "L")
        for (wf, wname) in ((wl_f, "W_lambda"),):
            pass
        lv = p.inc(p.sy.dma_start(wl_f[:], w["W_lambda"]), "L")
        lv = p.inc(p.sy.dma_start(
            wc_f[:].rearrange("p (c n) -> p c n", n=N),
            w["W_C"].rearrange("(c p) n -> p c n", p=128)), "L")
        p.wait(p.v, "L", lv)
        vv = p.inc(p.v.tensor_copy(wl_b[:], wl_f[:]), "V")
        vv = p.inc(p.v.tensor_copy(wc_b[:], wc_f[:]), "V")
        p.wait(p.t, "V", vv)
        tprev = []
        for kc in range(KC):
            tp.step(wl_b[:, kc * 128:(kc + 1) * 128], c.ident_b[0:16, 0:16],
                    wlamT[:, kc * N:(kc + 1) * N], out_rows=128, out_cols=16)
        p.wait(p.sy, "V", p.val("V"))
        lv = p.inc(p.sy.dma_start(wl_f[:], w["W_B"]), "L")
        p.wait(p.v, "L", lv)
        p.wait(p.v, "T", p.val("T"))
        vv = p.inc(p.v.tensor_copy(wl_b[:], wl_f[:]), "V")
        p.wait(p.t, "V", vv)
        for kc in range(KC):
            tp.step(wl_b[:, kc * 128:(kc + 1) * 128], c.ident_b[0:16, 0:16],
                    wbT[:, kc * N:(kc + 1) * N], out_rows=128, out_cols=16)
        for kc in range(KC):
            tp.step(wc_b[:, kc * N:(kc + 1) * N], c.ident_b[:],
                    wcT[:, kc * 128:(kc + 1) * 128], out_rows=16,
                    out_cols=128)
        p.wait(p.t, "V", p.val("V"))

        lnfull = p.sb("l2a_lnfull", [128, KC * TOK], BF16)
        lam = p.sb("l2a_lam", [16, TOK], F32)
        bx = p.sb("l2a_bx", [16, TOK], F32)
        hsb = p.sb("l2a_hsb", [16, TOK], BF16)
        gateb = p.sb("l2a_gate", [128, KC * TT], BF16)
        ygb = p.sb("l2a_yg", [128, KC * TT], F32)
        res = ResidualState(p, "l2a")
        lnv = chunks_view(lnfull, KC, TOK)
        gv_ = chunks_view(gateb, KC, TT)
        ygv = chunks_view(ygb, KC, TT)

        for tt in range(NTT):
            lnt = emit_ln_tile(p, c, ln, xin_ap, tt,
                               lambda kc, _t=tt: lnv(kc, _t * TT, TT))
            p.wait(p.t, "V", lnt)
            for (wT, dst, act) in ((wlamT, lam, "sig"), (wbT, bx, "copy")):
                ps_ap, slot = pool.acquire()
                rec = None
                for kc in range(KC):
                    rec = p.t.matmul(ps_ap[0:16, :], wT[:, kc * N:(kc + 1) * N],
                                     lnv(kc, tt * TT, TT), start=(kc == 0),
                                     stop=(kc == KC - 1))
                tvv = p.inc(rec, "T")
                p.wait(p.sc, "T", tvv)
                if act == "sig":
                    sv = p.inc(p.sc.activation(dst[:, tt * TT:(tt + 1) * TT],
                                               ps_ap[0:16, :], AF.Sigmoid,
                                               bias=bcol[:]), "S")
                else:
                    sv = p.inc(p.sc.activation(dst[:, tt * TT:(tt + 1) * TT],
                                               ps_ap[0:16, :], AF.Copy), "S")
                pool.release(slot, "S", sv)
        p.wait(p.v, "S", p.val("S"))
        for b in range(2):
            sl = slice(b * SEQ, (b + 1) * SEQ)
            vv = p.inc(p.v.tensor_tensor_scan(hsb[:, sl], lam[:, sl],
                                              bx[:, sl], 0.0, ALU.mult,
                                              ALU.add), "V")
        hs_done = vv
        p.wait(p.t, "V", hs_done)
        prow_guard = [0]
        for tt in range(NTT):
            def epi_g(fc, _tt, ps_ap, slot, tval):
                p.wait(p.sc, "T", tval)
                sv = p.inc(p.sc.activation(gv_(fc), ps_ap[:], AF.Sigmoid), "S")
                pool.release(slot, "S", sv)
            emit_proj(p, wgv, KC, lambda kc, _t=tt: lnv(kc, _t * TT, TT), D,
                      epi_g, pool, tts=[tt])
            p.wait(p.t, "S", p.val("S"))
            for fc in range(KC):
                ps_ap, slot = pool.acquire()
                tvv = p.inc(p.t.matmul(ps_ap[:],
                                       wcT[:, fc * 128:(fc + 1) * 128],
                                       hsb[:, tt * TT:(tt + 1) * TT],
                                       start=True, stop=True), "T")
                p.wait(p.v, "T", tvv)
                vv = p.inc(p.v.tensor_tensor(ygv(fc), ps_ap[:], gv_(fc),
                                             ALU.mult), "V")
                pool.release(slot, "V", vv)
            p.wait(p.sc, "V", p.val("V"))
            tq = {}
            for kc in range(KC):
                sqv = ln.sq[:, (kc % 2) * TT:(kc % 2) * TT + TT]
                if kc >= 2:
                    p.wait(p.sc, "T", tq[kc - 2])
                sk = p.inc(p.sc.activation(sqv, ygv(kc), AF.Square), "S")
                p.wait(p.t, "S", sk)
                if kc == 0 and prow_guard[0]:
                    p.wait(p.t, "V", prow_guard[0])
                tq[kc] = p.inc(p.t.matmul(prow[:], c.ones_col_f[:], sqv,
                                          start=(kc == 0), stop=(kc == KC - 1)),
                               "T")
            tsum = p.val("T")
            p.wait(p.v, "T", tsum)
            p.v.tensor_scalar_mul(ln.row(0), prow[:], 1.0 / D)
            vv = p.inc(p.v.tensor_copy(ln.row(1), ln.row(0)), "V")
            prow_guard[0] = vv
            p.wait(p.sc, "V", vv)
            sv = p.inc(p.sc.activation(ln.row(2), ln.row(1), AF.Sqrt,
                                       bias=c.one(1e-6)), "S")
            p.wait(p.v, "S", sv)
            vv = p.inc(p.v.reciprocal(ln.row(3), ln.row(2)), "V")
            p.wait(p.t, "V", vv)
            bcp, bslot = pool.acquire()
            tvv = p.inc(p.t.matmul(bcp[:], c.ones_row_f[:], ln.row(3),
                                   start=True, stop=True), "T")
            p.wait(p.v, "T", tvv)
            for kc in range(KC):
                slot = res.i % res.n
                res.i += 1
                buf = res.buf[slot]
                if res.guard[slot] is not None:
                    p.wait(p.sy, "O", res.guard[slot])
                lv = p.inc(p.sy.dma_start(buf[:, 0:TT],
                                          feat_dram_view(xin_ap, kc, tt * TT,
                                                         TT)), "L")
                p.wait(p.v, "L", lv)
                p.v.tensor_tensor(buf[:, 2 * TT:3 * TT], ygv(kc), bcp[:],
                                  ALU.mult)
                vv = p.inc(p.v.scalar_tensor_tensor(buf[:, TT:2 * TT],
                                                    buf[:, 2 * TT:3 * TT],
                                                    rmsw[:, kc:kc + 1],
                                                    buf[:, 0:TT], ALU.mult,
                                                    ALU.add), "V")
                p.wait(p.sy, "V", vv)
                ov = p.inc(p.sy.dma_start(feat_dram_view(xout_ap, kc, tt * TT,
                                                         TT),
                                          buf[:, TT:2 * TT]), "O")
                res.guard[slot] = ov
            pool.release(bslot, "V", p.val("V"))
            p.wait(p.sc, "T", p.val("T"))
            p.wait(p.v, "T", p.val("T"))


def emit_l2_ffn(p, c, xin_ap, xout_ap, w, cw1, cw2):
    NE, FF = 8, 4096
    NFF = FF // 128
    nc = p.nc
    lnhbm = nc.dram_tensor(p.name_prefix + "lnhbm", [D, TOK], BF16,
                           kind="Internal").ap()
    acc = nc.dram_tensor(p.name_prefix + "acc", [D, TOK], F32,
                         kind="Internal").ap()
    _dbg.update({"acc": acc, "lnhbm": lnhbm})
    with p.scope():
        wrow = p.sb("wrow", [NE, TOK], F32)
        _dbg["wrow"] = wrow
        b2sb = p.sb("b2sb", [NE, D], F32)
        b1sb = p.sb("b1sb", [128, NE * NFF], F32)
        lv = p.inc(p.sy.dma_start(b2sb[:], w["b2"]), "L")
        lv = p.inc(p.sy.dma_start(
            b1sb[:].rearrange("p (n c) -> p n c", n=NE),
            w["b1"].rearrange("n (c p) -> p n c", p=128),
            allow_slow_non_contiguous=True), "L")

        with p.scope():
            pool = PsumPool(p, "ps1", 4, [128, TT], F32)
            ln = LNState(p, "ln", pool)
            lnb = p.sb("lnb", [128, KC * TT], BF16)
            gw_f = p.sb("gwf", [NE, D], F32)
            gw_b = p.sb("gwb", [NE, D], BF16)
            gwT = p.sb("gwT", [128, KC * NE], BF16)
            tpg = TransPool(p, "tpg")
            grow = p.sb("grow", [128, 4 * NE], F32)
            gcol = p.sb("gcol", [128, 4], F32)
            lv2 = p.inc(p.sy.dma_start(gw_f[:], w["gate_w"]), "L")
            p.wait(p.v, "L", lv2)
            vv = p.inc(p.v.tensor_copy(gw_b[:], gw_f[:]), "V")
            p.wait(p.t, "V", vv)
            for kc in range(KC):
                tpg.step(gw_b[:, kc * 128:(kc + 1) * 128],
                         c.ident_b[0:NE, 0:NE],
                         gwT[:, kc * NE:(kc + 1) * NE], out_rows=128,
                         out_cols=NE)
            p.wait(p.t, "V", p.val("V"))
            lnv = chunks_view(lnb, KC, TT)
            P_, GE = grow[:, 0:NE], grow[:, NE:2 * NE]
            MK, WW = grow[:, 2 * NE:3 * NE], grow[:, 3 * NE:4 * NE]
            for tt in range(NTT):
                lnt = emit_ln_tile(p, c, ln, xin_ap, tt, lambda kc: lnv(kc))
                p.wait(p.sy, "V", lnt)
                dst = lnhbm.rearrange("(c p) t -> p c t",
                                      p=128)[:, :, tt * TT:(tt + 1) * TT]
                p.inc(p.sy.dma_start(dst,
                                     lnb[:].rearrange("p (c t) -> p c t",
                                                      c=KC)), "O")
                p.wait(p.t, "V", lnt)
                for tc in range(TT // 128):
                    gtc = tt * 4 + tc
                    ps_ap, slot = pool.acquire()
                    rec = None
                    for kc in range(KC):
                        rec = p.t.matmul(ps_ap[:, 0:NE], lnv(kc, tc * 128, 128),
                                         gwT[:, kc * NE:(kc + 1) * NE],
                                         start=(kc == 0), stop=(kc == KC - 1))
                    tvv = p.inc(rec, "T")
                    p.wait(p.v, "T", tvv)
                    vv = p.inc(p.v.reduce_max(gcol[:, 0:1], ps_ap[:, 0:NE],
                                              axis=AX.X, negate=True), "V")
                    p.wait(p.sc, "V", vv)
                    sv = p.inc(p.sc.activation(P_, ps_ap[:, 0:NE], AF.Exp,
                                               bias=gcol[:, 0:1]), "S")
                    pool.release(slot, "S", sv)
                    p.wait(p.v, "S", sv)
                    p.v.reduce_sum(gcol[:, 1:2], P_, axis=AX.X)
                    p.v.reciprocal(gcol[:, 2:3], gcol[:, 1:2])
                    p.v.tensor_scalar_mul(P_, P_, gcol[:, 2:3])
                    p.v.reduce_max(gcol[:, 0:1], P_, axis=AX.X)
                    p.v.tensor_scalar(GE, P_, gcol[:, 0:1], None, ALU.is_equal)
                    p.v.scalar_tensor_tensor(MK, GE, -2.0, P_, ALU.mult,
                                             ALU.add)
                    p.v.reduce_max(gcol[:, 0:1], MK, axis=AX.X)
                    p.v.tensor_scalar(GE, P_, gcol[:, 0:1], None, ALU.is_ge)
                    p.v.tensor_tensor(WW, P_, GE, ALU.mult)
                    p.v.reduce_sum(gcol[:, 1:2], WW, axis=AX.X)
                    p.v.tensor_scalar(gcol[:, 1:2], gcol[:, 1:2], 1e-9, None,
                                      ALU.add)
                    p.v.reciprocal(gcol[:, 2:3], gcol[:, 1:2])
                    vv = p.inc(p.v.tensor_scalar_mul(WW, WW, gcol[:, 2:3]),
                               "V")
                    p.wait(p.t, "V", vv)
                    ps2, slot2 = pool.acquire()
                    tvv = p.inc(p.t.transpose(ps2[0:NE, 0:128], WW,
                                              c.ident_f[:]), "T")
                    p.wait(p.v, "T", tvv)
                    vv = p.inc(p.v.tensor_copy(
                        wrow[:, gtc * 128:(gtc + 1) * 128],
                        ps2[0:NE, 0:128]), "V")
                    pool.release(slot2, "V", vv)
                    p.wait(p.t, "V", vv)
                p.wait(p.v, "T", p.val("T"))

        with p.scope():
            pool = PsumPool(p, "ps2", 7, [128, TT], F32)
            ebc = p.sb("ebc", [NE, NE * 128], F32)
            ebc_np = np.zeros((NE, NE * 128), np.float32)
            for n_ in range(NE):
                ebc_np[n_, n_ * 128:(n_ + 1) * 128] = 1.0
            h_ = p.nc.inline_tensor(ebc_np, p.name_prefix + "ebc_c")
            p.wait(p.sy, "V", p.val("V"))
            ebv = p.inc(p.sy.dma_start(ebc[:], h_.ap()), "L")
            p.wait(p.t, "L", ebv)
            w1sb = p.sb("w1sb", [128, KC * FF], BF16)
            w2sb = p.sb("w2sb", [128, NFF * D], BF16)
            lnt2 = [p.sb("lnt0", [128, KC * TT], BF16)]
            hbuf = p.sb("hbuf", [128, NFF * TT], BF16)
            wnb = p.sb("wnb", [128, TT], F32)
            outsb = [p.sb(f"outsb{i}", [128, TT], F32) for i in range(4)]
            w1v, w2v = w_view(w1sb, FF), w_view(w2sb, D)
            hv = chunks_view(hbuf, NFF, TT)
            ln_guard = [0, 0]
            out_guard = [0, 0, 0, 0]
            acc_guard = {}
            w1_guard = 0
            w2_guard = 0
            for n in range(NE):
                if w1_guard:
                    p.wait(p.sc, "T", w1_guard)
                qv1 = 0
                for kc in range(KC):
                    qv1 = p.inc(p.sc.dma_start(
                        w1sb[:, kc * FF:(kc + 1) * FF],
                        cw1[n * 1024 + kc * 128:n * 1024 + (kc + 1) * 128, :]),
                        "Q")
                if w2_guard:
                    p.wait(p.sc, "T", w2_guard)
                qv = 0
                for kc in range(NFF):
                    qv = p.inc(p.sc.dma_start(
                        w2sb[:, kc * D:(kc + 1) * D],
                        cw2[n * FF + kc * 128:n * FF + (kc + 1) * 128, :]),
                        "Q")
                p.wait(p.t, "Q", qv1)
                first_tt = [True]
                for tt in range(NTT):
                    par = 0
                    if ln_guard[par]:
                        p.wait(p.sy, "T", ln_guard[par])
                    src = lnhbm.rearrange("(c p) t -> p c t",
                                          p=128)[:, :, tt * TT:(tt + 1) * TT]
                    lv = p.inc(p.sy.dma_start(
                        lnt2[par][:].rearrange("p (c t) -> p c t", c=KC), src),
                        "L")
                    lv2 = chunks_view(lnt2[par], KC, TT)
                    ps_b, sl_b = pool.acquire()
                    tvv = p.inc(p.t.matmul(ps_b[:],
                                           ebc[:, n * 128:(n + 1) * 128],
                                           wrow[:, tt * TT:(tt + 1) * TT],
                                           start=True, stop=True), "T")
                    p.wait(p.v, "T", tvv)
                    vv = p.inc(p.v.tensor_copy(wnb[:], ps_b[:]), "V")
                    pool.release(sl_b, "V", vv)
                    p.wait(p.t, "L", lv)
                    for fc in range(NFF):
                        psh, slh = pool.acquire()
                        rec = None
                        for kc in range(KC):
                            rec = p.t.matmul(psh[:], w1v(kc, fc), lv2(kc),
                                             start=(kc == 0),
                                             stop=(kc == KC - 1))
                        tvv = p.inc(rec, "T")
                        p.wait(p.sc, "T", tvv)
                        sv = p.inc(p.sc.activation(
                            hv(fc), psh[:], AF.Gelu,
                            bias=b1sb[:, n * NFF + fc:n * NFF + fc + 1]), "S")
                        pool.release(slh, "S", sv)
                    ln_guard[par] = p.val("T")
                    w1_guard = p.val("T")
                    p.wait(p.t, "S", p.val("S"))
                    if first_tt[0]:
                        p.wait(p.t, "Q", qv)
                        first_tt[0] = False
                    for fc in range(KC):
                        pso, slo = pool.acquire()
                        rec = None
                        for kc in range(NFF):
                            rec = p.t.matmul(pso[:], w2v(kc, fc), hv(kc),
                                             start=(kc == 0),
                                             stop=(kc == NFF - 1))
                        tvv = p.inc(rec, "T")
                        osl = fc % 4
                        if out_guard[osl]:
                            p.wait(p.v, "G", out_guard[osl])
                        p.wait(p.v, "T", tvv)
                        vv = p.inc(p.v.tensor_tensor(outsb[osl][:], pso[:],
                                                     wnb[:], ALU.mult), "V")
                        pool.release(slo, "V", vv)
                        p.wait(p.g, "V", vv)
                        key = (fc, tt)
                        if key in acc_guard:
                            p.wait(p.g, "G", acc_guard[key])
                        gv = p.inc(p.g.dma_start(
                            feat_dram_view(acc, fc, tt * TT, TT),
                            outsb[osl][:],
                            accum_op=(ALU.add if n > 0 else ALU.bypass)), "G")
                        out_guard[osl] = gv
                        acc_guard[key] = gv
                    w2_guard = p.val("T")
            final_g = p.val("G")

        with p.scope():
            pool3 = PsumPool(p, "ps3", 2, [128, TT], F32)
            res = ResidualState(p, "res")
            accb = [p.sb(f"accb{i}", [128, TT], F32) for i in range(2)]
            acc_g = [0, 0]
            p.wait(p.sy, "G", final_g)
            p.wait(p.t, "L", p.val("L"))
            for tt in range(NTT):
                for fc in range(KC):
                    par = (tt * KC + fc) % 2
                    if acc_g[par]:
                        p.wait(p.sy, "V", acc_g[par])
                    lv = p.inc(p.sy.dma_start(accb[par][:],
                                              feat_dram_view(acc, fc, tt * TT,
                                                             TT)), "L")
                    ps_ap, slot = pool3.acquire()
                    tvv = p.inc(p.t.matmul(
                        ps_ap[:], b2sb[:, fc * 128:(fc + 1) * 128],
                        wrow[:, tt * TT:(tt + 1) * TT], start=True, stop=True),
                        "T")
                    slot_r = res.i % res.n
                    res.i += 1
                    buf = res.buf[slot_r]
                    if res.guard[slot_r] is not None:
                        p.wait(p.sy, "O", res.guard[slot_r])
                    lv = p.inc(p.sy.dma_start(
                        buf[:, 0:TT], feat_dram_view(xin_ap, fc, tt * TT, TT)),
                        "L")
                    p.wait(p.v, "L", lv)
                    p.wait(p.v, "T", tvv)
                    p.v.tensor_tensor(buf[:, 2 * TT:3 * TT], accb[par][:],
                                      ps_ap[:], ALU.add)
                    vv = p.inc(p.v.scalar_tensor_tensor(
                        buf[:, TT:2 * TT], buf[:, 2 * TT:3 * TT], 0.5,
                        buf[:, 0:TT], ALU.mult, ALU.add), "V")
                    pool3.release(slot, "V", vv)
                    acc_g[par] = vv
                    p.wait(p.sy, "V", vv)
                    ov = p.inc(p.sy.dma_start(
                        feat_dram_view(xout_ap, fc, tt * TT, TT),
                        buf[:, TT:2 * TT]), "O")
                    res.guard[slot_r] = ov


def emit_l0_attn(p, c, xin_ap, xout_ap, w, prep_only=False):
    CH = 256
    NC2 = 2
    SEQ = 2048
    NKT = SEQ // 128
    NQT = SEQ // TT
    with p.scope():
        pool = PsumPool(p, "l0a_ps", 3, [128, TT], F32)
        ln = LNState(p, "l0a_ln", pool)
        qw = p.sb("l0a_qw", [128, KC * CH], BF16)
        kw = p.sb("l0a_kw", [128, KC * CH], BF16)
        vw = p.sb("l0a_vw", [128, KC * CH], BF16)
        owT = p.sb("l0a_owT", [128, NC2 * D], BF16)
        smalls = p.sb("l0a_small", [1, 4], F32)
        lcol = p.sb("l0a_lcol", [128, 1], F32)
        oscol = p.sb("l0a_oscol", [128, 1], F32)
        prep = p.scope() if not prep_only else None
        if prep is not None:
            prep.__enter__()
        tp = TransPool(p, "l0a_tp")
        bsb = p.sb("l0a_bsb", [128, KC * CH], F32)
        bsb_b = p.sb("l0a_bsbb", [128, KC * CH], BF16)
        basisT = p.sb("l0a_bT", [128, NC2 * D], BF16)
        coefT = p.sb("l0a_cT", [128, 4 * NC2 * CH], BF16)
        cf = p.sb("l0a_cf", [128, NC2 * CH], F32)
        cfb = p.sb("l0a_cfb", [128, NC2 * CH], BF16)
        lv = p.inc(p.sy.dma_start(
            bsb[:].rearrange("p (c n) -> p c n", n=CH),
            w["basis"].rearrange("(c p) n -> p c n", p=128)), "L")
        p.wait(p.v, "L", lv)
        vv = p.inc(p.v.tensor_copy(bsb_b[:], bsb[:]), "V")
        p.wait(p.t, "V", vv)
        for kc in range(KC):
            for cc in range(NC2):
                tp.step(bsb_b[:, kc * CH + cc * 128:kc * CH + (cc + 1) * 128],
                        c.ident_b[:],
                        basisT[:, cc * D + kc * 128:cc * D + (kc + 1) * 128])
        for i, nm in enumerate(("q_coeffs", "k_coeffs", "v_coeffs", "o_coeffs")):
            p.wait(p.sy, "V", p.val("V"))
            lv = p.inc(p.sy.dma_start(
                cf[:].rearrange("p (c n) -> p c n", n=CH),
                w[nm].rearrange("(c p) n -> p c n", p=128)), "L")
            p.wait(p.v, "L", lv)
            vv = p.inc(p.v.tensor_copy(cfb[:], cf[:]), "V")
            p.wait(p.t, "V", vv)
            for kc in range(NC2):
                for cc in range(NC2):
                    tp.step(cfb[:, kc * CH + cc * 128:kc * CH + (cc + 1) * 128],
                            c.ident_b[:],
                            coefT[:, i * NC2 * CH + cc * CH + kc * 128:
                                  i * NC2 * CH + cc * CH + (kc + 1) * 128])
        p.wait(p.t, "V", p.val("V"))
        for i, dstw in enumerate((qw, kw, vw)):
            for vc in range(KC):
                ps_ap, slot = pool.acquire()
                rec = None
                for bc in range(NC2):
                    rec = p.t.matmul(
                        ps_ap[:, 0:CH],
                        basisT[:, bc * D + vc * 128:bc * D + (vc + 1) * 128],
                        coefT[:, i * NC2 * CH + bc * CH:(i * NC2 + bc + 1) * CH],
                        start=(bc == 0), stop=(bc == NC2 - 1))
                tvv = p.inc(rec, "T")
                p.wait(p.v, "T", tvv)
                vv = p.inc(p.v.tensor_copy(dstw[:, vc * CH:(vc + 1) * CH],
                                           ps_ap[:, 0:CH]), "V")
                pool.release(slot, "V", vv)
        for ccO in range(NC2):
            for vs in range(2):
                ps_ap, slot = pool.acquire()
                rec = None
                for bc in range(NC2):
                    rec = p.t.matmul(
                        ps_ap[:],
                        coefT[:, 3 * NC2 * CH + bc * CH + ccO * 128:
                              3 * NC2 * CH + bc * CH + (ccO + 1) * 128],
                        basisT[:, bc * D + vs * TT:bc * D + (vs + 1) * TT],
                        start=(bc == 0), stop=(bc == NC2 - 1))
                tvv = p.inc(rec, "T")
                p.wait(p.v, "T", tvv)
                vv = p.inc(p.v.tensor_copy(
                    owT[:, ccO * D + vs * TT:ccO * D + (vs + 1) * TT],
                    ps_ap[:]), "V")
                pool.release(slot, "V", vv)
        p.wait(p.sy, "V", p.val("V"))
        p.wait(p.sy, "S", p.val("S"))
        lv = p.inc(p.sy.dma_start(smalls[:, 0:1], w["decay_logit"][0:1]), "L")
        lv = p.inc(p.sy.dma_start(smalls[:, 1:2], w["out_scale"][0:1]), "L")
        p.wait(p.sc, "L", lv)
        sv = p.inc(p.sc.activation(smalls[:, 2:3], smalls[:, 0:1], AF.Sigmoid),
                   "S")
        p.wait(p.v, "S", sv)
        vv = p.inc(p.v.tensor_scalar_max(smalls[:, 2:3], smalls[:, 2:3],
                                         1e-38), "V")
        p.wait(p.sc, "V", vv)
        sv = p.inc(p.sc.activation(smalls[:, 2:3], smalls[:, 2:3], AF.Ln), "S")
        p.wait(p.t, "S", sv)
        ps_ap, slot = pool.acquire()
        p.t.matmul(ps_ap[:, 0:1], c.ones_row_f[:], smalls[:, 2:3], start=True,
                   stop=True)
        tvv = p.inc(p.t.matmul(ps_ap[:, 1:2], c.ones_row_f[:], smalls[:, 1:2],
                               start=False, stop=True,
                               skip_group_check=True), "T")
        p.wait(p.v, "T", tvv)
        p.v.tensor_copy(lcol[:], ps_ap[:, 0:1])
        vv = p.inc(p.v.tensor_copy(oscol[:], ps_ap[:, 1:2]), "V")
        pool.release(slot, "V", vv)
        if prep_only:
            _dbg.update({"coefT": coefT, "basisT": basisT, "owT": owT})
            return
        prep.__exit__(None, None, None)

        lnfull = p.sb("l0a_lnfull", [128, KC * TOK], BF16)
        lnv = chunks_view(lnfull, KC, TOK)
        for tt in range(NTT):
            emit_ln_tile(p, c, ln, xin_ap, tt,
                         lambda kc, _t=tt: lnv(kc, _t * TT, TT))
        qb = p.sb("l0a_qb", [128, NC2 * TOK], BF16)
        kb = p.sb("l0a_kb", [128, NC2 * TOK], BF16)
        vb = p.sb("l0a_vb", [128, NC2 * TOK], BF16)
        vtok = p.sb("l0a_vtok", [128, (TOK // 128) * CH], BF16)
        _dbg.update({"qw": qw, "kw": kw, "vw": vw, "owT": owT, "qb": qb,
                     "kb": kb, "vtok": vtok, "lcol": lcol, "oscol": oscol})
        p.wait(p.t, "V", p.val("V"))
        for (wsb, dst) in ((qw, qb), (kw, kb), (vw, vb)):
            wv_ = w_view(wsb, CH)

            def epi(fc, tt_, ps_ap2, slot2, tval, _dst=dst):
                p.wait(p.sc, "T", tval)
                sv2 = p.inc(p.sc.activation(
                    _dst[:, fc * TOK + tt_ * TT:fc * TOK + (tt_ + 1) * TT],
                    ps_ap2[:], AF.Copy), "S")
                pool.release(slot2, "S", sv2)
            emit_proj(p, wv_, KC, lambda kc, _t: lnv(kc, _t * TT, TT), CH,
                      epi, pool)
        p.wait(p.t, "S", p.val("S"))
        with p.scope():
            tp2 = TransPool(p, "l0a_tp2")
            for t128 in range(TOK // 128):
                for cc in range(NC2):
                    tp2.step(
                        vb[:, cc * TOK + t128 * 128:cc * TOK + (t128 + 1) * 128],
                        c.ident_b[:],
                        vtok[:, t128 * CH + cc * 128:t128 * CH + (cc + 1) * 128])
            p.wait(p.t, "V", p.val("V"))
        spool = PsumPool(p, "l0a_ss", 2, [128, TT], F32)

        wdt = p.sb("l0a_wdt", [128, TT], F32)
        msk = p.sb("l0a_msk", [128, TT], F32)
        wex = p.sb("l0a_wex", [128, TT], F32)
        pbuf = [p.sb(f"l0a_pb{i}", [128, TT], BF16) for i in range(2)]
        rb = p.sb("l0a_rb", [128, NC2 * TT], BF16)
        res = ResidualState(p, "l0a")
        owv = w_view(owT, D)
        wexm = p.sb("l0a_wexm", [128, TT], F32)
        pb_guard = [0, 0]
        for b in range(2):
            for qt in range(NQT):
                qt0 = qt * TT
                glob_q = b * SEQ + qt0
                pso = [None, None]
                slo = [None, None]
                for cc in range(NC2):
                    pso[cc], slo[cc] = pool.acquire()
                kt_start = qt0 // 128
                for kt in range(kt_start, NKT):
                    kt0 = kt * 128
                    pss, sls = spool.acquire()
                    rec = None
                    for ccq in range(NC2):
                        rec = p.t.matmul(
                            pss[:], kb[:, ccq * TOK + b * SEQ + kt0:
                                        ccq * TOK + b * SEQ + kt0 + 128],
                            qb[:, ccq * TOK + glob_q:ccq * TOK + glob_q + TT],
                            start=(ccq == 0), stop=(ccq == NC2 - 1))
                    tvv = p.inc(rec, "T")
                    c0 = float(kt0 - qt0 - 1)
                    p.v.tensor_scalar(wdt[:], c.iota_pf[:], c0, 0.0, ALU.add,
                                      ALU.max)
                    vv = p.inc(p.v.tensor_single_scalar(msk[:], c.iota_pf[:],
                                                        -c0, ALU.is_ge), "V")
                    p.wait(p.sc, "V", vv)
                    sv = p.inc(p.sc.activation(wex[:], wdt[:], AF.Exp,
                                               scale=lcol[:]), "S")
                    par = kt % 2
                    if pb_guard[par]:
                        p.wait(p.v, "T", pb_guard[par])
                    p.wait(p.v, "S", sv)
                    p.wait(p.v, "T", tvv)
                    p.v.tensor_tensor(wexm[:], wex[:], msk[:], ALU.mult)
                    vv = p.inc(p.v.tensor_tensor(pbuf[par][:], pss[:], wexm[:],
                                                 ALU.mult), "V")
                    spool.release(sls, "V", vv)
                    p.wait(p.t, "V", vv)
                    for cc in range(NC2):
                        rec = p.t.matmul(
                            pso[cc][:],
                            vtok[:, (b * NKT + kt) * CH + cc * 128:
                                 (b * NKT + kt) * CH + (cc + 1) * 128],
                            pbuf[par][:], start=(kt == kt_start),
                            stop=(kt == NKT - 1))
                    pb_guard[par] = p.inc(rec, "T")
                for cc in range(NC2):
                    p.wait(p.sc, "T", p.val("T"))
                    sv = p.inc(p.sc.activation(rb[:, cc * TT:(cc + 1) * TT],
                                               pso[cc][:], AF.Copy), "S")
                    pool.release(slo[cc], "S", sv)
                p.wait(p.t, "S", p.val("S"))

                def epi_o(fc, _tt, ps_ap2, slot2, tval):
                    vv2 = emit_residual_store(p, c, res, xin_ap, xout_ap, fc,
                                              _tt, ps_ap2[:], 1.0, tval,
                                              scale_col=oscol[:])
                    pool.release(slot2, "V", vv2)
                emit_proj(p, owv, NC2, lambda kc, _t: rb[:, kc * TT:(kc + 1) * TT],
                          D, epi_o, pool, tts=[b * NQT + qt])
                p.wait(p.v, "T", p.val("T"))


def emit_l1_attn(p, c, xin_ap, xout_ap, w, cw_qkv, cw_proj):
    H, DH = 16, 64
    SEQ = 2048
    NKT = SEQ // 128
    NQT4 = SEQ // TT
    nc = p.nc
    mrow_hbm = nc.dram_tensor(p.name_prefix + "mrow", [16, 16 * 128], BF16,
                              kind="Internal").ap()
    drow_hbm = nc.dram_tensor(p.name_prefix + "drow", [16, 16 * 128], F32,
                              kind="Internal").ap()
    with p.scope():
        pool = PsumPool(p, "l1a_ps", 3, [128, TT], F32)
        wpb = [p.sb(f"l1a_wpb{i}", [128, KC * 128], BF16) for i in range(2)]
        wp_guard = [0, 0]
        tcol = p.sb("l1a_tcol", [16, 1], F32)
        hfcol = p.sb("l1a_hf", [128, KC], F32)
        projb = p.sb("l1a_projb", [128, KC], F32)
        with p.scope():
            hsel_np = np.zeros((16, 1024), np.float32)
            for h_ in range(16):
                hsel_np[h_, h_ * 64:(h_ + 1) * 64] = 1.0
            hsel_t = nc.inline_tensor(hsel_np, p.name_prefix + "hsel")
            hsel = p.sb("l1a_hsel", [16, 1024], F32)
            lv = p.inc(p.sy.dma_start(hsel[:], hsel_t.ap()), "L")
            lv = p.inc(p.sy.dma_start(tcol[:],
                                      w["temperature"].rearrange(
                                          "(p o) -> p o", o=1)), "L")
            lv = p.inc(p.sy.dma_start(projb[:],
                                      w["proj_b"].rearrange("(c p) -> p c",
                                                            p=128),
                                      allow_slow_non_contiguous=True), "L")
            p.wait(p.v, "L", lv)
            p.v.tensor_scalar_mul(tcol[:], tcol[:], 64.0)
            vv = p.inc(p.v.reciprocal(tcol[:], tcol[:]), "V")
            p.wait(p.t, "V", vv)
            for kc in range(KC):
                ps_ap, slot = pool.acquire()
                tvv = p.inc(p.t.matmul(ps_ap[:, 0:1],
                                       hsel[:, kc * 128:(kc + 1) * 128],
                                       tcol[:], start=True, stop=True), "T")
                p.wait(p.v, "T", tvv)
                vv = p.inc(p.v.tensor_copy(hfcol[:, kc:kc + 1],
                                           ps_ap[:, 0:1]), "V")
                pool.release(slot, "V", vv)
            p.wait(p.sy, "V", p.val("V"))

        qh = p.sb("l1a_qh", [128, KC * SEQ], BF16)
        kh = p.sb("l1a_kh", [128, KC * SEQ], BF16)
        vtok = p.sb("l1a_vtok", [128, NKT * D], BF16)
        dualrow = p.sb("l1a_dual", [1, 2 * TT], F32)
        rn2 = p.sb("l1a_rn2", [1, 2 * TT], F32)
        res = ResidualState(p, "l1a")

        for b in range(2):
            with p.scope():
                lnA = LNState(p, "l1a_ln%d" % b, pool)
                lnb = p.sb("l1a_lnb%d" % b, [128, KC * TT], BF16)
                qraw = p.sb("l1a_qraw%d" % b, [128, KC * TT], BF16)
                kraw = p.sb("l1a_kraw%d" % b, [128, KC * TT], BF16)
                vtmp = p.sb("l1a_vtmp%d" % b, [128, TT], BF16)
                wqb = [p.sb(f"l1a_wq{b}{i}", [128, KC * 128], BF16)
                       for i in range(2)]
                stq, stk = lnA.st1, lnA.st2
                tpv = TransPool(p, "l1a_tpv%d" % b)
                lnv = chunks_view(lnb, KC, TT)
                qrv = chunks_view(qraw, KC, TT)
                krv = chunks_view(kraw, KC, TT)
                stq_guard = [0]
                wq_guard = [0, 0]
                qfc = {}

                def load_wq(fc):
                    sl = fc % 2
                    if wq_guard[sl]:
                        p.wait(p.sc, "T", wq_guard[sl])
                    return emit_load_wconv(p, cw_qkv, wqb[sl], KC, [fc], "Q")
                for ttl in range(NQT4):
                    gt = b * NQT4 + ttl
                    lnt = emit_ln_tile(p, c, lnA, xin_ap, gt,
                                       lambda kc: lnv(kc))
                    p.wait(p.t, "V", lnt)
                    if stq_guard[0]:
                        p.wait(p.t, "V", stq_guard[0])
                    sq_t = {}
                    pend = None
                    qfc[0] = load_wq(0)

                    def emit_sq(pf, psv, psl):
                        p.wait(p.t, "S", psv)
                        st_ = stq if pf < 8 else stk
                        sq_t[pf] = p.inc(p.t.matmul(
                            st_[:], c.ones_col_f[:],
                            lnA.sq[:, psl * TT:(psl + 1) * TT],
                            start=(pf in (0, 8)), stop=(pf in (7, 15))), "T")

                    for fc in range(24):
                        if fc + 1 < 24:
                            qfc[fc + 1] = load_wq(fc + 1)
                        wqv_ = wconv_view(wqb[fc % 2], KC)
                        ps_ap, slot = pool.acquire()
                        p.wait(p.t, "Q", qfc[fc])
                        rec = None
                        for kc in range(KC):
                            rec = p.t.matmul(ps_ap[:], wqv_(kc, 0),
                                             lnv(kc), start=(kc == 0),
                                             stop=(kc == KC - 1))
                        tvv = p.inc(rec, "T")
                        wq_guard[fc % 2] = tvv
                        if pend is not None:
                            emit_sq(*pend)
                            pend = None
                        p.wait(p.sc, "T", tvv)
                        if fc < 16:
                            sv = p.inc(p.sc.activation(
                                qrv(fc) if fc < 8 else krv(fc - 8),
                                ps_ap[:], AF.Copy), "S")
                            sl2 = fc % 2
                            if fc >= 2:
                                p.wait(p.sc, "T", sq_t[fc - 2])
                            sv = p.inc(p.sc.activation(
                                lnA.sq[:, sl2 * TT:(sl2 + 1) * TT], ps_ap[:],
                                AF.Square), "S")
                            pend = (fc, sv, sl2)
                        else:
                            sv = p.inc(p.sc.activation(vtmp[:], ps_ap[:],
                                                       AF.Copy), "S")
                            p.wait(p.t, "S", sv)
                            for j4 in range(4):
                                kt128 = ttl * 4 + j4
                                tpv.step(vtmp[:, j4 * 128:(j4 + 1) * 128],
                                         c.ident_b[:],
                                         vtok[:, kt128 * D + (fc - 16) * 128:
                                              kt128 * D + (fc - 15) * 128])
                            p.wait(p.t, "V", p.val("V"))
                        pool.release(slot, "S" if fc < 16 else "V",
                                     p.val("S") if fc < 16 else p.val("V"))
                    p.wait(p.v, "T", p.val("T"))
                    for (st, off) in ((stq, 0), (stk, 1)):
                        dr = dualrow[:, off * TT:(off + 1) * TT]
                        p.v.tensor_copy(dr, st[:])
                        p.wait(p.sc, "V", p.val("V"))
                        p.inc(p.sc.activation(dr, dr, AF.Sqrt), "S")
                        p.wait(p.v, "S", p.val("S"))
                        p.v.tensor_scalar_max(dr, dr, 1e-12)
                        vv = p.inc(p.v.reciprocal(
                            rn2[:, off * TT:(off + 1) * TT], dr), "V")
                    stq_guard[0] = p.val("V")
                    lnA.last_v = max(lnA.last_v, stq_guard[0])
                    p.wait(p.t, "V", p.val("V"))
                    bcq, sq_ = pool.acquire()
                    p.t.matmul(bcq[:], c.ones_row_f[:],
                               rn2[:, 0:TT], start=True, stop=True)
                    bck, sk_ = pool.acquire()
                    tvv = p.inc(p.t.matmul(bck[:], c.ones_row_f[:],
                                           rn2[:, TT:2 * TT],
                                           start=True, stop=True), "T")
                    p.wait(p.v, "T", tvv)
                    for kc in range(KC):
                        p.v.scalar_tensor_tensor(
                            qh[:, kc * SEQ + ttl * TT:kc * SEQ + (ttl + 1) * TT],
                            qrv(kc), hfcol[:, kc:kc + 1], bcq[:], ALU.mult,
                            ALU.mult)
                        vv = p.inc(p.v.tensor_tensor(
                            kh[:, kc * SEQ + ttl * TT:kc * SEQ + (ttl + 1) * TT],
                            krv(kc), bck[:], ALU.mult), "V")
                    pool.release(sq_, "V", vv)
                    pool.release(sk_, "V", vv)
                    p.wait(p.sc, "T", p.val("T"))
                    p.wait(p.v, "T", p.val("T"))

            with p.scope():
                attn_o = p.sb("l1a_ao%d" % b, [128, KC * SEQ], BF16)
                spool = PsumPool(p, "l1a_sp%d" % b, 4, [128, TT], F32)
                mcol = p.sb("l1a_mcol%d" % b, [128, 2 * H * NKT], F32)
                mtmpM = p.sb("l1a_mtmpM%d" % b, [16, H * 128], BF16)
                mtmpD = p.sb("l1a_mtmpD%d" % b, [16, H * 128], F32)
                mred = p.sb("l1a_mred%d" % b, [128, 8], F32)
                mrowb2 = [p.sb(f"l1a_mrb{b}{i}", [1, SEQ], BF16)
                          for i in range(2)]
                drow2 = [p.sb(f"l1a_dr2{b}{i}", [1, SEQ], F32)
                         for i in range(2)]
                pbuf = [p.sb(f"l1a_pb{b}{i}", [128, TT], BF16)
                        for i in range(3)]
                dival = p.sb("l1a_div%d" % b, [128, TT], F32)

                def _slab(src, h_, t0, wd):
                    kc0 = ((h_ * DH) // 128) * SEQ
                    off = (h_ * DH) % 128
                    return src[off:off + DH, kc0 + t0:kc0 + t0 + wd]

                bigp = PsumPool.__new__(PsumPool)
                bigp.p = p
                bigp.tiles = spool.tiles + pool.tiles
                bigp.n = len(bigp.tiles)
                bigp.i = 0
                bigp.guard = spool.guard[:] + pool.guard[:]
                for h_ in range(H):
                    for qc in range(NKT):
                        tiles = []
                        for ktt in range(NQT4):
                            ps_ap, slot = bigp.acquire()
                            tvv = p.inc(p.t.matmul(
                                ps_ap[:], _slab(qh, h_, qc * 128, 128),
                                _slab(kh, h_, ktt * TT, TT), start=True,
                                stop=True), "T")
                            tiles.append((ps_ap, slot))
                            p.wait(p.v, "T", tvv)
                            vv = p.inc(p.v.reduce_max(
                                mred[:, ktt:ktt + 1], ps_ap[:],
                                axis=AX.X), "V")
                        mc = mcol[:, h_ * NKT + qc:h_ * NKT + qc + 1]
                        vv = p.inc(p.v.reduce_max(
                            mc, mred[:, 0:4], axis=AX.X, negate=True), "V")
                        p.wait(p.sc, "V", vv)
                        for ktt in range(NQT4):
                            ps_ap, slot = tiles[ktt]
                            sv = p.inc(p.sc.activation(
                                dival[:], ps_ap[:], AF.Exp, bias=mc,
                                accum_out=mred[:, 4 + ktt:5 + ktt]), "S")
                            bigp.release(slot, "S", sv)
                        p.wait(p.v, "S", p.val("S"))
                        vv = p.inc(p.v.reduce_sum(
                            mcol[:, H * NKT + h_ * NKT + qc:
                                 H * NKT + h_ * NKT + qc + 1],
                            mred[:, 4:8], axis=AX.X), "V")
                spool.guard[:] = bigp.guard[0:spool.n]
                pool.guard[:] = bigp.guard[spool.n:]
                spool.i = 0
                pool.i = 0
                p.wait(p.t, "V", p.val("V"))
                for g_ in range(2 * H):
                    ps_ap, slot = spool.acquire()
                    tvv = p.inc(p.t.transpose(
                        ps_ap[0:16, 0:128],
                        mcol[:, g_ * NKT:(g_ + 1) * NKT], c.ident_f[:]), "T")
                    p.wait(p.v, "T", tvv)
                    if g_ < H:
                        dst_ = mtmpM[:, g_ * 128:(g_ + 1) * 128]
                    else:
                        dst_ = mtmpD[:, (g_ - H) * 128:(g_ - H + 1) * 128]
                    vv = p.inc(p.v.tensor_copy(dst_, ps_ap[0:16, 0:128]), "V")
                    spool.release(slot, "V", vv)
                    p.wait(p.t, "V", vv)
                p.wait(p.sy, "V", p.val("V"))
                p.inc(p.sy.dma_start(mrow_hbm, mtmpM[:]), "O")
                ov = p.inc(p.sy.dma_start(drow_hbm, mtmpD[:]), "O")
                p.wait(p.sy, "O", ov)
                mr_guard = [0, 0]
                mr_ready = {}

                def fetch_mrow(h2):
                    sl = h2 % 2
                    if mr_guard[sl]:
                        p.wait(p.sy, "T", mr_guard[sl])
                    srcm = bass.AP(mrow_hbm.tensor, h2 * 128,
                                   [[1, 1], [H * 128, NKT], [1, 128]])
                    p.inc(p.sy.dma_start(
                        mrowb2[sl][:].rearrange("o (a b) -> o a b", b=128),
                        srcm), "L")
                    srcd = bass.AP(drow_hbm.tensor, h2 * 128,
                                   [[1, 1], [H * 128, NKT], [1, 128]])
                    p.inc(p.sy.dma_start(
                        drow2[sl][:].rearrange("o (a b) -> o a b", b=128),
                        srcd), "L")
                    lv2 = p.val("L")
                    p.wait(p.v, "L", lv2)
                    return p.inc(p.v.reciprocal(drow2[sl][:], drow2[sl][:]),
                                 "V")

                mr_ready[0] = fetch_mrow(0)
                for h_ in range(H):
                    kc0 = (h_ * DH) // 128
                    off = (h_ * DH) % 128
                    qslab = lambda t0, wd: qh[off:off + DH,
                                              kc0 * SEQ + t0:kc0 * SEQ + t0 + wd]
                    kslab = lambda t0, wd: kh[off:off + DH,
                                              kc0 * SEQ + t0:kc0 * SEQ + t0 + wd]
                    if h_ + 1 < H:
                        mr_ready[h_ + 1] = fetch_mrow(h_ + 1)
                    p.wait(p.t, "V", mr_ready[h_])
                    mrow_b = mrowb2[h_ % 2][:]
                    pb_free = [0, 0, 0]
                    for qt in range(NQT4):
                        qt0 = qt * TT
                        pat, sat = pool.acquire()
                        for kt in range(NKT):
                            pss, sls = spool.acquire()
                            p.t.matmul(pss[:], kslab(kt * 128, 128),
                                       qslab(qt0, TT), start=True, stop=False)
                            rec = p.t.matmul(pss[:], c.ones_row_b[:],
                                             mrow_b[0:1, qt0:qt0 + TT],
                                             start=False, stop=True)
                            tvv = p.inc(rec, "T")
                            par = kt % 3
                            p.wait(p.sc, "T", tvv)
                            if pb_free[par]:
                                p.wait(p.sc, "T", pb_free[par])
                            sv = p.inc(p.sc.activation(pbuf[par][:], pss[:],
                                                       AF.Exp), "S")
                            spool.release(sls, "S", sv)
                            p.wait(p.t, "S", sv)
                            rec = p.t.matmul(pat[0:DH, :],
                                             vtok[:, kt * D + h_ * DH:
                                                  kt * D + (h_ + 1) * DH],
                                             pbuf[par][:], start=(kt == 0),
                                             stop=(kt == NKT - 1))
                            tvv = p.inc(rec, "T")
                            pb_free[par] = tvv
                        mr_guard[h_ % 2] = tvv
                        bcd, sbd = spool.acquire()
                        tvv = p.inc(p.t.matmul(bcd[0:DH, :],
                                               c.ones_row_f[0:1, 0:DH],
                                               drow2[h_ % 2][:, qt0:qt0 + TT],
                                               start=True, stop=True), "T")
                        p.wait(p.sc, "T", tvv)
                        sv = p.inc(p.sc.activation(dival[0:DH, :],
                                                   pat[0:DH, :], AF.Copy),
                                   "S")
                        pool.release(sat, "S", sv)
                        p.wait(p.v, "S", sv)
                        p.wait(p.v, "T", tvv)
                        vv = p.inc(p.v.tensor_tensor(
                            attn_o[off:off + DH,
                                   kc0 * SEQ + qt0:kc0 * SEQ + qt0 + TT],
                            dival[0:DH, :], bcd[0:DH, :], ALU.mult), "V")
                        spool.release(sbd, "V", vv)

                p.wait(p.t, "V", p.val("V"))
                wpq = {}

                def load_wp(fc2):
                    sl = fc2 % 2
                    if wp_guard[sl]:
                        p.wait(p.sc, "T", wp_guard[sl])
                    return emit_load_wconv(p, cw_proj, wpb[sl], KC, [fc2],
                                           "Q")
                wpq[0] = load_wp(0)
                for fc in range(KC):
                    if fc + 1 < KC:
                        wpq[fc + 1] = load_wp(fc + 1)
                    wvv = wconv_view(wpb[fc % 2], KC)
                    p.wait(p.t, "Q", wpq[fc])
                    for ttl in range(NQT4):
                        ps_ap, slot = pool.acquire()
                        rec = None
                        for kc in range(KC):
                            rec = p.t.matmul(
                                ps_ap[:], wvv(kc, 0),
                                attn_o[:, kc * SEQ + ttl * TT:
                                       kc * SEQ + (ttl + 1) * TT],
                                start=(kc == 0), stop=(kc == KC - 1))
                        tval = p.inc(rec, "T")
                        wp_guard[fc % 2] = tval
                        vv2 = emit_residual_store(
                            p, c, res, xin_ap, xout_ap, fc, b * NQT4 + ttl,
                            ps_ap[:], 1.0, tval,
                            bias_col=projb[:, fc:fc + 1])
                        pool.release(slot, "V", vv2)
                p.wait(p.v, "T", p.val("T"))
                p.wait(p.sc, "T", p.val("T"))


def emit_l3_attn(p, c, xin_ap, xout_ap, w, cwq, cwk, cwv, cwo):
    NT3 = 16
    SB_ = 256
    H, DH = 16, 64
    nc = p.nc
    qhbm = nc.dram_tensor(p.name_prefix + "l3aq", [D, TOK], BF16,
                          kind="Internal").ap()
    vhbm = nc.dram_tensor(p.name_prefix + "l3av", [D, TOK], BF16,
                          kind="Internal").ap()
    with p.scope():
        pool = PsumPool(p, "l3a_ps", 4, [128, TT], F32)
        spool = PsumPool(p, "l3a_sp", 2, [128, TT], F32)
        bcols = p.sb("l3a_bcols", [128, 4 * KC], F32)
        for i, nm in enumerate(("bq", "bk", "bv", "bo")):
            lv = p.inc(p.sy.dma_start(
                bcols[:, i * KC:(i + 1) * KC],
                w[nm].rearrange("(c p) -> p c", p=128),
                allow_slow_non_contiguous=True), "L")
        p.wait(p.v, "L", lv)
        vv = p.inc(p.v.tensor_scalar_mul(bcols[:, 0:KC], bcols[:, 0:KC],
                                         0.125), "V")
        rkcols = p.sb("l3a_rkc", [128, NT3 * NT3], F32)
        rvcols = p.sb("l3a_rvc", [128, NT3 * NT3], F32)
        with p.scope():
            rtab = p.sb("l3a_rtab", [33, 64], F32)
            sel_np = np.zeros((33, 256), np.float32)
            for t_ in range(16):
                for s_ in range(16):
                    r_ = min(max(s_ - t_, -16), 16) + 16
                    sel_np[r_, t_ * 16 + s_] = 1.0
            sel_t = nc.inline_tensor(sel_np, p.name_prefix + "l3a_sel")
            sel = p.sb("l3a_selb", [33, 256], F32)
            lv = p.inc(p.sy.dma_start(sel[:], sel_t.ap()), "L")
            for (nm, dst) in (("rk_table", rkcols), ("rv_table", rvcols)):
                lv = p.inc(p.sy.dma_start(rtab[:], w[nm]), "L")
                p.wait(p.t, "L", lv)
                ps_ap, slot = pool.acquire()
                tvv = p.inc(p.t.matmul(ps_ap[0:64, 0:256], rtab[:], sel[:],
                                       start=True, stop=True), "T")
                p.wait(p.v, "T", tvv)
                p.v.tensor_copy(dst[0:64, :], ps_ap[0:64, 0:256])
                vv = p.inc(p.v.tensor_copy(dst[64:128, :],
                                           ps_ap[0:64, 0:256]), "V")
                pool.release(slot, "V", vv)
                p.wait(p.sy, "V", vv)
        hsT_np = np.zeros((128, KC * 16), ml_dtypes.bfloat16)
        for kc in range(KC):
            for pp in range(128):
                hsT_np[pp, kc * 16 + (kc * 128 + pp) // 64] = 1.0
        hsT_t = nc.inline_tensor(np.asarray(hsT_np), p.name_prefix + "l3a_hsTc")
        hselT = p.sb("l3a_hsT", [128, KC * 16], BF16)
        lv = p.inc(p.sy.dma_start(hselT[:], hsT_t.ap()), "L")
        hsB_np = np.zeros((16, 1024), ml_dtypes.bfloat16)
        for h_ in range(16):
            hsB_np[h_, h_ * 64:(h_ + 1) * 64] = 1.0
        hsB_t = nc.inline_tensor(np.asarray(hsB_np), p.name_prefix + "l3a_hsBc")
        hselB = p.sb("l3a_hsB", [16, 1024], BF16)
        lv = p.inc(p.sy.dma_start(hselB[:], hsB_t.ap()), "L")

        kbuf = p.sb("l3a_k", [128, KC * TOK], BF16)
        kv_ = chunks_view(kbuf, KC, TOK)

        with p.scope():
            lnA = LNState(p, "l3a_ln", pool)
            lnb = p.sb("l3a_lnb", [128, KC * TT], BF16)
            qtile = p.sb("l3a_qt", [128, KC * TT], BF16)
            wb = [p.sb(f"l3a_wb{i}", [128, KC * 128], BF16) for i in range(2)]
            lnv = chunks_view(lnb, KC, TT)
            qtv = chunks_view(qtile, KC, TT)
            wguard = [0, 0]
            q_o_guard = [0]

            def load_w(cw, fc):
                sl = fc % 2
                if wguard[sl]:
                    p.wait(p.sc, "T", wguard[sl])
                return emit_load_wconv(p, cw, wb[sl], KC, [fc], "Q")
            for tt in range(NTT):
                lnt = emit_ln_tile(p, c, lnA, xin_ap, tt, lambda kc: lnv(kc))
                p.wait(p.t, "V", lnt)
                for (wi, cw) in enumerate((cwq, cwk, cwv)):
                    qfc = {0: load_w(cw, 0)}
                    for fc in range(KC):
                        if fc + 1 < KC:
                            qfc[fc + 1] = load_w(cw, fc + 1)
                        ps_ap, slot = pool.acquire()
                        p.wait(p.t, "Q", qfc[fc])
                        rec = None
                        wvv = wconv_view(wb[fc % 2], KC)
                        for kc in range(KC):
                            rec = p.t.matmul(ps_ap[:], wvv(kc, 0), lnv(kc),
                                             start=(kc == 0),
                                             stop=(kc == KC - 1))
                        tvv = p.inc(rec, "T")
                        wguard[fc % 2] = tvv
                        p.wait(p.v, "T", tvv)
                        if wi == 1:
                            sv = p.inc(p.v.tensor_scalar_add(
                                kv_(fc, tt * TT, TT), ps_ap[:],
                                bcols[:, KC + fc:KC + fc + 1]), "V")
                        else:
                            if q_o_guard[0]:
                                p.wait(p.v, "O", q_o_guard[0])
                            boff = 0 if wi == 0 else 2 * KC
                            if wi == 0:
                                sv = p.inc(p.v.tensor_scalar(
                                    qtv(fc), ps_ap[:], 0.125,
                                    bcols[:, boff + fc:boff + fc + 1],
                                    ALU.mult, ALU.add), "V")
                            else:
                                sv = p.inc(p.v.tensor_scalar_add(
                                    qtv(fc), ps_ap[:],
                                    bcols[:, boff + fc:boff + fc + 1]), "V")
                            p.wait(p.sy, "V", sv)
                            hb = qhbm if wi == 0 else vhbm
                            q_o_guard[0] = p.inc(p.sy.dma_start(
                                feat_dram_view(hb, fc, tt * TT, TT),
                                qtv(fc)), "O")
                        pool.release(slot, "V", sv)
                p.wait(p.v, "T", p.val("T"))

        with p.scope():
            wo = p.sb("l3a_wo", [128, KC * KC * 128], BF16)
            qv_ = emit_load_wconv(p, cwo, wo, KC, range(KC), "Q")
            wov = wconv_view(wo, KC)
            qt2 = [p.sb(f"l3a_q2{i}", [128, KC * SB_], BF16) for i in range(2)]
            vsb = [p.sb(f"l3a_vs{i}", [128, KC * SB_], BF16) for i in range(2)]
            sc2 = p.sb("l3a_sc2", [16, NT3 * SB_], F32)
            sc3 = p.sb("l3a_sc3", [16, NT3 * SB_], F32)
            pnb = p.sb("l3a_pnb", [16, NT3 * SB_], BF16)
            srows = p.sb("l3a_srows", [16, 3 * SB_], F32)
            recipb = p.sb("l3a_recipb", [16, SB_], BF16)
            zb2 = [p.sb(f"l3a_z{i}", [128, KC * SB_], BF16) for i in range(2)]
            av2 = [p.sb(f"l3a_av{i}", [128, KC * SB_], F32) for i in range(2)]
            accf = p.sb("l3a_acc", [128, KC * SB_], F32)
            attb = p.sb("l3a_attb", [128, KC * SB_], BF16)
            res = ResidualState(p, "l3a")
            qld = [0, 0]
            vld_guard = [0, 0]
            vready = {}
            p.wait(p.t, "Q", qv_)

            def load_vs(s_, vi):
                sl = vi % 2
                if vld_guard[sl]:
                    p.wait(p.sy, "V", vld_guard[sl])
                src = vhbm.rearrange("(c p) t -> p c t",
                                     p=128)[:, :, s_ * SB_:(s_ + 1) * SB_]
                return p.inc(p.sy.dma_start(
                    vsb[sl][:].rearrange("p (c t) -> p c t", c=KC), src), "L")
            for t_ in range(NT3):
                par = t_ % 2
                if qld[par]:
                    p.wait(p.sy, "V", qld[par])
                src = qhbm.rearrange("(c p) t -> p c t",
                                     p=128)[:, :, t_ * SB_:(t_ + 1) * SB_]
                lv = p.inc(p.sy.dma_start(
                    qt2[par][:].rearrange("p (c t) -> p c t", c=KC), src), "L")
                p.wait(p.v, "L", lv)
                z_guard = [0, 0]
                for s_ in range(NT3):
                    zs = zb2[s_ % 2]
                    zv = zs[:].rearrange("p (c t) -> p c t", c=KC)
                    kslc = bass.AP(kbuf, s_ * SB_, [[KC * TOK, 128],
                                                    [TOK, KC], [1, SB_]])
                    qslc = qt2[par][:].rearrange("p (c t) -> p c t", c=KC)
                    if z_guard[s_ % 2]:
                        p.wait(p.v, "T", z_guard[s_ % 2])
                    vv = p.inc(p.v.scalar_tensor_tensor(
                        zv, kslc, rkcols[:, t_ * 16 + s_:t_ * 16 + s_ + 1],
                        qslc, ALU.add, ALU.mult), "V")
                    p.wait(p.t, "V", vv)
                    ps_ap, slot = spool.acquire()
                    rec = None
                    for kc in range(KC):
                        rec = p.t.matmul(ps_ap[0:16, 0:SB_],
                                         hselT[:, kc * 16:(kc + 1) * 16],
                                         zs[:, kc * SB_:(kc + 1) * SB_],
                                         start=(kc == 0), stop=(kc == KC - 1))
                    tvv = p.inc(rec, "T")
                    z_guard[s_ % 2] = tvv
                    p.wait(p.sc, "T", tvv)
                    sv = p.inc(p.sc.activation(sc2[:, s_ * SB_:(s_ + 1) * SB_],
                                               ps_ap[0:16, 0:SB_], AF.Copy),
                               "S")
                    spool.release(slot, "S", sv)
                p.wait(p.v, "S", p.val("S"))
                v2 = bass.AP(sc2, 0, [[NT3 * SB_, 16], [1, SB_], [SB_, NT3]])
                v3 = bass.AP(sc3, 0, [[NT3 * SB_, 16], [1, SB_], [SB_, NT3]])
                p.v.reduce_max(bass.AP(srows, 0, [[3 * SB_, 16], [1, SB_],
                                                  [0, 1]]), v2, axis=AX.X,
                               negate=True)
                vv = p.inc(p.v.tensor_tensor(
                    v3, v2, bass.AP(srows, 0, [[3 * SB_, 16], [1, SB_],
                                               [0, NT3]]), ALU.add), "V")
                p.wait(p.sc, "V", vv)
                sv = p.inc(p.sc.activation(pnb[:], sc3[:], AF.Exp), "S")
                p.wait(p.v, "S", sv)
                pv3 = bass.AP(pnb, 0, [[NT3 * SB_, 16], [1, SB_], [SB_, NT3]])
                p.v.reduce_sum(bass.AP(srows, SB_, [[3 * SB_, 16], [1, SB_],
                                                    [0, 1]]), pv3, axis=AX.X)
                vv = p.inc(p.v.reciprocal(srows[:, 2 * SB_:3 * SB_],
                                          srows[:, SB_:2 * SB_]), "V")
                vv = p.inc(p.v.tensor_copy(recipb[:],
                                           srows[:, 2 * SB_:3 * SB_]), "V")
                p.wait(p.t, "V", vv)
                vready[0] = load_vs(0, 0)
                av_guard = [0, 0]
                for s_ in range(NT3):
                    if s_ + 1 < NT3:
                        vready[s_ + 1] = load_vs(s_ + 1, s_ + 1)
                    rvc = rvcols[:, t_ * 16 + s_:t_ * 16 + s_ + 1]
                    vsl = vsb[s_ % 2]
                    avt = av2[s_ % 2]
                    p.wait(p.v, "L", vready[s_])
                    if av_guard[s_ % 2]:
                        p.wait(p.v, "P", av_guard[s_ % 2])
                    for kg in range(KC // 2):
                        bcp, bslot = pool.acquire()
                        last_vv = None
                        for j in range(2):
                            kc = kg * 2 + j
                            tvv = p.inc(p.t.matmul(
                                bcp[:, j * SB_:(j + 1) * SB_],
                                hselB[:, kc * 128:(kc + 1) * 128],
                                pnb[:, s_ * SB_:(s_ + 1) * SB_],
                                start=True, stop=True), "T")
                            p.wait(p.v, "T", tvv)
                            last_vv = p.inc(p.v.scalar_tensor_tensor(
                                avt[:, kc * SB_:(kc + 1) * SB_],
                                vsl[:, kc * SB_:(kc + 1) * SB_], rvc,
                                bcp[:, j * SB_:(j + 1) * SB_], ALU.add,
                                ALU.mult), "V")
                        pool.release(bslot, "V", last_vv)
                    p.wait(p.g, "V", p.val("V"))
                    if s_ == 0:
                        pv = p.inc(p.g.tensor_copy(accf[:], avt[:]), "P")
                    else:
                        pv = p.inc(p.g.tensor_tensor(accf[:], accf[:],
                                                     avt[:], ALU.add), "P")
                    av_guard[s_ % 2] = pv
                    vld_guard[s_ % 2] = p.val("V")
                p.wait(p.t, "V", p.val("V"))
                p.wait(p.v, "P", p.val("P"))
                for kc in range(KC):
                    bcp, bslot = pool.acquire()
                    tvv = p.inc(p.t.matmul(bcp[:, 0:SB_],
                                           hselB[:, kc * 128:(kc + 1) * 128],
                                           recipb[:],
                                           start=True, stop=True), "T")
                    p.wait(p.v, "T", tvv)
                    vv = p.inc(p.v.tensor_tensor(
                        attb[:, kc * SB_:(kc + 1) * SB_],
                        accf[:, kc * SB_:(kc + 1) * SB_], bcp[:, 0:SB_],
                        ALU.mult), "V")
                    pool.release(bslot, "V", vv)
                qld[par] = vv
                p.wait(p.t, "V", vv)
                for fc in range(KC):
                    ps_ap, slot = pool.acquire()
                    rec = None
                    for kc in range(KC):
                        rec = p.t.matmul(ps_ap[:, 0:SB_], wov(kc, fc),
                                         attb[:, kc * SB_:(kc + 1) * SB_],
                                         start=(kc == 0), stop=(kc == KC - 1))
                    tvv = p.inc(rec, "T")
                    slot_r = res.i % res.n
                    res.i += 1
                    buf = res.buf[slot_r]
                    if res.guard[slot_r] is not None:
                        p.wait(p.sy, "O", res.guard[slot_r])
                    lv = p.inc(p.sy.dma_start(
                        buf[:, 0:SB_],
                        feat_dram_view(xin_ap, fc, t_ * SB_, SB_)), "L")
                    p.wait(p.v, "L", lv)
                    p.wait(p.v, "T", tvv)
                    p.v.tensor_scalar(buf[:, TT:TT + SB_], ps_ap[:, 0:SB_],
                                      bcols[:, 3 * KC + fc:3 * KC + fc + 1],
                                      None, ALU.add)
                    vvr = p.inc(p.v.tensor_tensor(
                        buf[:, TT + SB_:TT + 2 * SB_], buf[:, TT:TT + SB_],
                        buf[:, 0:SB_], ALU.add), "V")
                    pool.release(slot, "V", vvr)
                    p.wait(p.sy, "V", vvr)
                    ov = p.inc(p.sy.dma_start(
                        feat_dram_view(xout_ap, fc, t_ * SB_, SB_),
                        buf[:, TT + SB_:TT + 2 * SB_]), "O")
                    res.guard[slot_r] = ov
                p.wait(p.sc, "T", p.val("T"))
                p.wait(p.v, "T", p.val("T"))


LAYER_SPECS_012 = [
    ("l0_ffn", "L0_ffn_"), ("l0_attn", "L0_attn_"), ("l0_ffn", "L0_ffn_"),
    ("l1_ffn", "L1_ffn_"), ("l1_attn", "L1_attn_"), ("l1_ffn", "L1_ffn_"),
    ("l2_ffn", "L2_ffn_"), ("l2_attn", "L2_attn_"), ("l2_ffn", "L2_ffn_"),
]
LAYER_SPECS_3 = [
    ("l3_ffn", "L3_ffn_"), ("l3_attn", "L3_attn_"), ("l3_ffn", "L3_ffn_"),
]

_prog_cache = {}


def _chain_prog(specs, a2a_before=None):
    key = (tuple(specs), a2a_before)
    if key not in _prog_cache:
        _prog_cache[key] = build_chain(list(specs), a2a_before=a2a_before)
    return _prog_cache[key]


def kernel(**inputs):
    x = np.asarray(inputs["L0_attn_x"], dtype=np.float32)
    weights = {}
    for k, v in inputs.items():
        if k == "L0_attn_x":
            continue
        a = np.asarray(v)
        shape, dt = INPUT_SPECS[k]
        if dt == I32:
            a = np.ascontiguousarray(a.reshape(shape), dtype=np.int32)
        else:
            a = np.ascontiguousarray(a.reshape(shape), dtype=np.float32)
        weights[k] = a

    shards = [np.ascontiguousarray(x[2 * cc:2 * cc + 2].reshape(4096, 1024).T)
              for cc in range(NCORES)]

    def run_specs(specs, cur, a2a_before=None):
        nc_prog, used = _chain_prog(tuple(specs), a2a_before)
        in_maps = []
        for cc in range(NCORES):
            m = {"xin": cur[cc]}
            for nm in used:
                m[nm] = weights[nm]
            in_maps.append(m)
        res = run_bass_kernel_spmd(nc_prog, in_maps,
                                   core_ids=list(range(NCORES)))
        return [np.asarray(r["xout"]) for r in res.results]

    cur3 = run_specs(LAYER_SPECS_012 + LAYER_SPECS_3, shards,
                     a2a_before=len(LAYER_SPECS_012))

    out = np.empty((16, 2048, 1024), np.float32)
    for cc in range(NCORES):
        o = cur3[cc].T.reshape(16, 256, 1024)
        out[:, cc * 256:(cc + 1) * 256, :] = o
    return out
